# version 07
# baseline (speedup 1.0000x reference)
import math
import numpy as np
import ml_dtypes
import concourse.bass as bass
import concourse.mybir as mybir
from concourse.bass_utils import run_bass_kernel_spmd

F32 = mybir.dt.float32
BF16 = mybir.dt.bfloat16
AF = mybir.ActivationFunctionType
ALU = mybir.AluOpType
AX = mybir.AxisListType

NCORES = 8
B, S, H = 2, 4096, 2048
TOWN = 1024
KV, QC = 512, 1024
NH, DH, DR = 16, 128, 64
II = 8192
EPS = 1e-5
ALPHA = 1.0 / math.sqrt(DH + DR)
LN2_INV = float(1.0 / np.log(2.0))

ENGS = ("sync", "vector", "scalar", "tensor", "gpsimd")
_DEBUG = False
PARANOID_SCALE = True
PARANOID_ACT = 0


class Plan:

    def __init__(self):
        self.streams = {e: [] for e in ENGS}
        self.cnt = {}
        self.bank_guard = {}

    AUTO = {}

    def op(self, eng, fn, waits=()):
        waits = list(waits)
        auto = self.AUTO.get(eng)
        if auto is not None:
            waits.append((auto, self.cnt.get(auto, 0)))
        for sem, val in waits:
            assert val <= self.cnt.get(sem, 0), (
                f"wait {sem}>={val} exceeds planned count {self.cnt.get(sem, 0)} "
                f"(deadlock)")
        if auto is not None:
            self.cnt[auto] = self.cnt.get(auto, 0) + 1
            self.streams[eng].append([fn, waits, auto, 1])
        else:
            self.streams[eng].append([fn, waits, None, 0])

    def mark(self, eng, sem, n=1):
        entry = self.streams[eng][-1]
        auto = self.AUTO.get(eng)
        if auto is not None:
            assert sem == auto and n == 1, f"{eng} marks only {auto}"
            return self.cnt[sem]
        if entry[2] == sem:
            return self.cnt[sem]
        assert entry[2] is None, "op already has an inc"
        entry[2] = sem
        entry[3] = n
        self.cnt[sem] = self.cnt.get(sem, 0) + n
        return self.cnt[sem]

    def now(self, sem):
        return self.cnt.get(sem, 0)

    def guard(self, bank):
        return self.bank_guard.pop(bank, [])

    def set_guard(self, bank, waits):
        self.bank_guard[bank] = list(waits)



def emit_scale_batch(P, jobs, amax, t0, t1, t2, t2h, half):
    def gop(fn, waits=()):
        P.op("vector", fn, list(waits) + [("sDV", P.now("sDV"))])
        P.mark("vector", "sDV")
    m_prev_ln = getattr(P, "scale_ln_mark", 0)
    for src3, sl, wt in jobs:
        gop(lambda v, src3=src3, sl=sl: v.tensor_reduce(
            amax[:, sl], src3, AX.X, ALU.max, apply_absolute_value=True),
            list(wt) + [("sAC", m_prev_ln)])
    gop(lambda v: v.tensor_scalar_max(amax, amax, 1e-30))
    mv = P.now("sDV")
    P.op("scalar", lambda s: s.activation(t2, amax, AF.Ln), [("sDV", mv)])
    ma = P.mark("scalar", "sAC")
    P.scale_ln_mark = ma
    gop(lambda v: v.tensor_scalar(t2, t2, LN2_INV, 104.5, ALU.mult, ALU.add),
        [("sAC", ma)])
    first = True
    for b in (128, 64, 32, 16, 8, 4, 2, 1):
        fb = float(b)
        gop(lambda v, fb=fb: v.tensor_single_scalar(t1, t2, fb, ALU.is_ge))
        gop(lambda v, fb=fb: v.scalar_tensor_tensor(t2, t1, -fb, t2, ALU.mult, ALU.add))
        if b == 128:
            f64 = float(2.0**64 - 1.0)
            gop(lambda v, f64=f64: v.tensor_scalar(t1, t1, f64, 1.0, ALU.mult, ALU.add))
            gop(lambda v: v.tensor_scalar_mul(t0, t1, float(2.0**-104)))
            gop(lambda v: v.tensor_tensor(t0, t0, t1, ALU.mult))
            first = False
            continue
        fw = float(2.0**b - 1.0)
        gop(lambda v, fw=fw: v.tensor_scalar(t1, t1, fw, 1.0, ALU.mult, ALU.add))
        if first:
            gop(lambda v: v.tensor_scalar_mul(t0, t1, float(2.0**-104)))
            first = False
        else:
            gop(lambda v: v.tensor_tensor(t0, t0, t1, ALU.mult))
    gop(lambda v: v.tensor_tensor(t1, t0, amax, ALU.is_gt))
    gop(lambda v: v.tensor_scalar(t1, t1, -0.125, 0.25, ALU.mult, ALU.add))
    gop(lambda v: v.tensor_tensor(t0, t0, t1, ALU.mult))
    gop(lambda v: v.reciprocal(t1, t0))
    if half:
        gop(lambda v: v.tensor_scalar_mul(t2h, t0, 0.5))
    else:
        gop(lambda v: v.tensor_copy(out=t2h, in_=t0))
    return P.now("sDV")


def emit_quant_w(P, src, outq, rs, sch, azf, b1, b2, b3, m_sc, wt, wt_out=(),
                 m_scr_act=0):
    Pp, F = src.shape
    nb = F // 32
    src3 = src.rearrange("p (b c) -> p b c", c=32)
    rs_b = rs[:, :, None].broadcast_to([Pp, nb, 32])
    sch_b = sch[:, :, None].broadcast_to([Pp, nb, 32])
    az3 = azf.rearrange("p (b c) -> p b c", c=32)
    P.op("vector", lambda v: v.tensor_tensor(az3, src3, rs_b, ALU.mult),
         list(wt) + [("sDV", m_sc), ("sAC", m_scr_act)])
    m_z = P.mark("vector", "sDV")
    P.op("scalar", lambda s: s.activation(b3, azf, AF.Sign), [("sDV", m_z)])
    P.op("scalar", lambda s: s.activation(azf, azf, AF.Abs))
    m_as = P.mark("scalar", "sAC")
    V = lambda fn, w=(): P.op("vector", fn, w)
    V(lambda v: v.tensor_single_scalar(b1, azf, 0.25, ALU.is_gt), [("sAC", m_as)])
    V(lambda v: v.scalar_tensor_tensor(b1, azf, 0.75, b1, ALU.is_ge, ALU.add))
    V(lambda v: v.scalar_tensor_tensor(b1, azf, 1.25, b1, ALU.is_gt, ALU.add))
    V(lambda v: v.scalar_tensor_tensor(b1, azf, 1.75, b1, ALU.is_ge, ALU.add))
    V(lambda v: v.tensor_single_scalar(b2, azf, 2.5, ALU.is_gt))
    V(lambda v: v.scalar_tensor_tensor(b2, azf, 3.5, b2, ALU.is_ge, ALU.add))
    V(lambda v: v.tensor_single_scalar(azf, azf, 5.0, ALU.is_gt))
    V(lambda v: v.scalar_tensor_tensor(b2, azf, 2.0, b2, ALU.mult, ALU.add))
    V(lambda v: v.scalar_tensor_tensor(b1, b2, 2.0, b1, ALU.mult, ALU.add))
    V(lambda v: v.tensor_tensor(b1, b1, b3, ALU.mult))
    b13 = b1.rearrange("p (b c) -> p b c", c=32)
    oq3 = outq.rearrange("p (b c) -> p b c", c=32)
    V(lambda v: v.tensor_tensor(oq3, b13, sch_b, ALU.mult), list(wt_out))
    return m_z, m_as, P.mark("vector", "sDV")


def emit_quant_act_multi(P, cascades, m_sc, m_scr_act=0):
    seq = []

    for c in cascades:
        Pp, F = c["src"].shape
        nb = F // 32
        src3 = c["src"].rearrange("p (b c) -> p b c", c=32)
        rs_b = c["rs"][:, :, None].broadcast_to([Pp, nb, 32])
        sc_b = c["scb"][:, :, None].broadcast_to([Pp, nb, 32])
        zf, azf = c["zf"], c["azf"]
        g1, g2, g3, g4 = c["g1"], c["g2"], c["g3"], c["g4"]
        z3 = zf.rearrange("p (b c) -> p b c", c=32)
        oq3 = c["outq"].rearrange("p (b c) -> p b c", c=32)
        ops = []
        ops.append(("vz", lambda v, z3=z3, src3=src3, rs_b=rs_b:
                    v.tensor_tensor(z3, src3, rs_b, ALU.mult), c["wt"]))
        ops.append(("s", lambda s, azf=azf, zf=zf: s.activation(azf, zf, AF.Abs), ()))
        ops.append(("s2", lambda s, g1=g1, zf=zf: s.activation(g1, zf, AF.Sign), ()))
        ops.append(("v1", lambda v, g2=g2, azf=azf:
                    v.tensor_single_scalar(g2, azf, 2.0, ALU.is_ge), ()))
        ops.append(("v", lambda v, g3=g3, azf=azf:
                    v.tensor_single_scalar(g3, azf, 4.0, ALU.is_ge), ()))
        ops.append(("v", lambda v, g4=g4, g2=g2:
                    v.tensor_scalar(g4, g2, -4.0, 8.0, ALU.mult, ALU.add), ()))
        ops.append(("v", lambda v, g4=g4, g3=g3:
                    v.scalar_tensor_tensor(g4, g3, -2.0, g4, ALU.mult, ALU.add), ()))
        ops.append(("v", lambda v, g2=g2:
                    v.tensor_scalar(g2, g2, 0.125, 0.125, ALU.mult, ALU.add), ()))
        ops.append(("v", lambda v, g2=g2, g3=g3:
                    v.scalar_tensor_tensor(g2, g3, 0.25, g2, ALU.mult, ALU.add), ()))
        ops.append(("v", lambda v, zf=zf, azf=azf, g4=g4:
                    v.tensor_tensor(zf, azf, g4, ALU.mult), ()))
        ops.append(("v", lambda v, g3=g3, zf=zf:
                    v.tensor_single_scalar(g3, zf, 15.5, ALU.is_ge), ()))
        ops.append(("v", lambda v, azf=azf, g3=g3, zf=zf:
                    v.scalar_tensor_tensor(azf, g3, -16.0, zf, ALU.mult, ALU.add), ()))
        for b in (8, 4, 2, 1):
            fb = float(b)
            ops.append(("v", lambda v, g3=g3, azf=azf, fb=fb:
                        v.tensor_single_scalar(g3, azf, fb - 0.5, ALU.is_ge), ()))
            ops.append(("v", lambda v, azf=azf, g3=g3, fb=fb:
                        v.scalar_tensor_tensor(azf, g3, -fb, azf, ALU.mult, ALU.add), ()))
        ops.append(("v", lambda v, g3=g3, azf=azf:
                    v.scalar_tensor_tensor(g3, azf, -0.5, g3, ALU.is_equal, ALU.mult), ()))
        ops.append(("v", lambda v, zf=zf, azf=azf:
                    v.tensor_tensor(zf, zf, azf, ALU.subtract), ()))
        ops.append(("v", lambda v, zf=zf, g3=g3:
                    v.tensor_tensor(zf, zf, g3, ALU.subtract), ()))
        ops.append(("v", lambda v, zf=zf, g2=g2:
                    v.tensor_tensor(zf, zf, g2, ALU.mult), ()))
        ops.append(("v", lambda v, zf=zf:
                    v.tensor_scalar_min(zf, zf, 7.5), ()))
        ops.append(("v", lambda v, zf=zf, g1=g1:
                    v.tensor_tensor(zf, zf, g1, ALU.mult), ()))
        ops.append(("vo", lambda v, oq3=oq3, z3=z3, sc_b=sc_b:
                    v.tensor_tensor(oq3, z3, sc_b, ALU.mult), c.get("wt_out", ())))
        seq.append(ops)

    merged = []
    n = max(len(s) for s in seq)
    for i in range(n):
        for s in seq:
            if i < len(s):
                merged.append(s[i])
    pend_z = 0
    m_as = 0
    for tag, fn, wt in merged:
        if tag == "vz":
            P.op("vector", fn, list(wt) + [("sDV", m_sc), ("sAC", m_scr_act)])
            P.mark("vector", "sDV")
            pend_z = P.now("sDV")
        elif tag == "s":
            P.op("scalar", fn, [("sDV", pend_z)])
        elif tag == "s2":
            P.op("scalar", fn, ())
            m_as = P.mark("scalar", "sAC")
        elif tag == "v1":
            P.op("vector", fn, [("sAC", m_as)])
        else:
            P.op("vector", fn, list(wt))
    return m_as, P.mark("vector", "sDV")



def build_nc():
    nc = bass.Bass("TRN2", target_bir_lowering=False,
                   detect_race_conditions=False)

    def dram(name, shape, dtype=F32, kind="ExternalInput"):
        return nc.dram_tensor(name, list(shape), dtype, kind=kind).ap()

    x_fm = dram("x_fm", [H, S])
    x_tm = dram("x_tm", [TOWN, H])
    wkvT = dram("wkvT", [H, KV])
    wkuT = dram("wkuT", [KV, NH * DH])
    wvuT = dram("wvuT", [KV, NH * DH])
    wkrT = dram("wkrT", [KV, NH * DR])
    wqdT = dram("wqdT", [H, QC])
    wquT = dram("wquT", [QC, NH * DH])
    wqrT = dram("wqrT", [QC, NH * DR])
    woutT = dram("woutT", [NH * DH, H])
    b_kv = dram("b_kv", [128, KV // 128])
    b_qd = dram("b_qd", [128, QC // 128])
    b_qu = dram("b_qu", [128, NH * DH // 128])
    b_qr = dram("b_qr", [128, NH // 2])
    b_vu_row = dram("b_vu_row", [1, NH * DH])
    b_out = dram("b_out", [128, H // 128])
    rmat = dram("rmat", [128, (NH // 2) * DR])
    ones_col = dram("ones_col", [128, 1])
    ones_row = dram("ones_row", [1, 128])
    wg = dram("wg", [II, H])
    wu = dram("wu", [II, H])
    wd = dram("wd", [H, II])

    y_out = dram("y_out", [H, TOWN], F32, kind="ExternalOutput")

    scratch_kind = "ExternalOutput" if _DEBUG else "Internal"
    acc_d = dram("acc_d", [H, TOWN], F32, kind=scratch_kind)
    ctx_d = dram("ctx_d", [128, NH * TOWN], BF16, kind="Internal")
    xq_d = dram("xq_d", [TOWN, H], BF16, kind=scratch_kind)
    wgq_d = dram("wgq_d", [II, H], BF16, kind=scratch_kind)
    wuq_d = dram("wuq_d", [II, H], BF16, kind=scratch_kind)
    wdq_d = dram("wdq_d", [H, II], BF16, kind=scratch_kind)
    itq_d = dram("itq_d", [TOWN, II], BF16, kind=scratch_kind)
    if _DEBUG:
        dbg_kvc = dram("dbg_kvc", [128, 4 * S], BF16, kind="ExternalOutput")
        dbg_qc = dram("dbg_qc", [128, 8 * TOWN], BF16, kind="ExternalOutput")
        dbg_ctx = dram("dbg_ctx", [128, NH * TOWN], BF16, kind="ExternalOutput")
        dbg_kc = dram("dbg_kc", [128, S], BF16, kind="ExternalOutput")
        dbg_kr = dram("dbg_kr", [128, S], BF16, kind="ExternalOutput")
        dbg_v = dram("dbg_v", [128, 32 * 256], BF16, kind="ExternalOutput")
        dbg_qcs = dram("dbg_qcs", [128, TOWN], BF16, kind="ExternalOutput")
        dbg_qrot = dram("dbg_qrot", [128, TOWN], BF16, kind="ExternalOutput")
        dbg_bvu = dram("dbg_bvu", [1, NH * DH], BF16, kind="ExternalOutput")
        dbg_oner = dram("dbg_oner", [1, 128], BF16, kind="ExternalOutput")
        dbg_wvu = dram("dbg_wvu", [128, 4 * 256], BF16, kind="ExternalOutput")
        dbg_wqu = dram("dbg_wqu", [128, 8 * 256], BF16, kind="ExternalOutput")
        dbg_wgqT = dram("dbg_wgqT", [128, 16 * 512], BF16, kind="ExternalOutput")
        dbg_sg = dram("dbg_sg", [128, 512], F32, kind="ExternalOutput")
        dbg_it = dram("dbg_it", [128, 512], F32, kind="ExternalOutput")
        dbg_xqfm = dram("dbg_xqfm", [128, 16 * TOWN], BF16, kind="ExternalOutput")
        dbg_itq0 = dram("dbg_itq0", [128, 512], BF16, kind="ExternalOutput")
        dbg_ism = dram("dbg_ism", [128, 32], F32, kind="ExternalOutput")

    P = Plan()

    def sb(name, shape, dtype):
        return nc.alloc_sbuf_tensor(name, list(shape), dtype).ap()

    onec_bf = sb("onec_bf", [128, 1], BF16)
    oner_bf = sb("oner_bf", [1, 128], BF16)
    bkv_sb = sb("bkv_sb", [128, KV // 128], F32)
    bqd_sb = sb("bqd_sb", [128, QC // 128], F32)
    bqu_sb = sb("bqu_sb", [128, NH * DH // 128], F32)
    bqr_sb = sb("bqr_sb", [128, NH // 2], F32)
    bvu_bf = sb("bvu_bf", [1, NH * DH], BF16)
    bout_sb = sb("bout_sb", [128, H // 128], F32)
    rmat_sb = sb("rmat_sb", [128, (NH // 2) * DR], F32)
    colkv_bf = sb("colkv_bf", [1, KV], BF16)
    colqd_bf = sb("colqd_bf", [1, QC], BF16)
    srow = sb("srow", [1, 4 * 512], F32)
    eps_col = sb("eps_col", [128, 1], F32)
    onec_f = sb("onec_f", [128, 1], F32)
    oner_f = sb("oner_f", [1, 128], F32)
    rowbf = sb("rowbf", [1, 2 * 512], BF16)
    base_consts = nc.sbuf_base

    kv_c = sb("kv_c", [128, 4 * S], BF16)
    q_c = sb("q_c", [128, 8 * TOWN], BF16)
    base_persist = nc.sbuf_base

    ps = [nc.alloc_psum_tensor(f"ps{i}", [128, 512], F32).ap() for i in range(8)]

    wstage = sb("wstage", [128, 4096], F32)
    wkv_bf = sb("wkv_bf", [128, 16 * 512], BF16)
    wqd_bf = sb("wqd_bf", [128, 16 * 1024], BF16)
    xstage = sb("xstage", [128, 16 * 512], F32)
    xbf = sb("xbf", [128, 16 * 512], BF16)
    sqbf = sb("sqbf", [128, 16 * 512], BF16)
    bvu_f32 = wstage[0:1, 0:NH * DH]

    for dst, src in ((bkv_sb, b_kv), (bqd_sb, b_qd), (bqu_sb, b_qu),
                     (bqr_sb, b_qr), (bout_sb, b_out), (rmat_sb, rmat),
                     (bvu_f32, b_vu_row)):
        P.op("sync", lambda e, dst=dst, src=src: e.dma_start(out=dst, in_=src))
        P.mark("sync", "sLD", 16)
    P.op("sync", lambda e: e.dma_start(out=onec_f, in_=ones_col))
    P.mark("sync", "sLD", 16)
    P.op("sync", lambda e: e.dma_start(out=oner_f, in_=ones_row))
    m_consts = P.mark("sync", "sLD", 16)
    P.op("vector", lambda v: v.tensor_copy(out=onec_bf, in_=onec_f),
         [("sLD", m_consts)])
    P.op("vector", lambda v: v.tensor_copy(out=oner_bf, in_=oner_f))
    P.op("vector", lambda v: v.tensor_copy(out=bvu_bf, in_=bvu_f32))
    P.op("vector", lambda v: v.memset(eps_col, EPS))
    m_c0 = P.mark("vector", "sDV")

    for hlf in range(2):
        P.op("sync", lambda e, hlf=hlf: e.dma_start(
            out=wstage[:, 0:8 * 512].rearrange("p (c n) -> p c n", c=8),
            in_=wkvT[hlf * 1024:(hlf + 1) * 1024, :].rearrange("(c p) n -> p c n", p=128)),
            [("sDV", m_c0 if hlf == 0 else P.now("sDV"))])
        m_wkv = P.mark("sync", "sLD", 16)
        P.op("vector", lambda v, hlf=hlf: v.tensor_copy(
            out=wkv_bf[:, hlf * 8 * 512:(hlf + 1) * 8 * 512], in_=wstage[:, 0:8 * 512]),
            [("sLD", m_wkv)])
        P.mark("vector", "sDV")
    m_wkvc = P.now("sDV")
    for c in range(16):
        P.op("tensor", lambda e, c=c: e.matmul(
            ps[0][0:1, 0:KV], onec_bf, wkv_bf[:, c * KV:(c + 1) * KV],
            start=(c == 0), stop=(c == 15)),
            [("sDV", m_wkvc)] if c == 0 else [])
    m_pe = P.mark("tensor", "sPE")
    P.op("vector", lambda v: v.tensor_copy(out=colkv_bf, in_=ps[0][0:1, 0:KV]),
         [("sPE", m_pe)])
    m_colkv = P.mark("vector", "sDV")
    for qtr in range(4):
        P.op("sync", lambda e, qtr=qtr: e.dma_start(
            out=wstage[:, 0:4 * 1024].rearrange("p (c n) -> p c n", c=4),
            in_=wqdT[qtr * 512:(qtr + 1) * 512, :].rearrange("(c p) n -> p c n", p=128)),
            [("sDV", m_colkv if qtr == 0 else P.now("sDV"))])
        mq = P.mark("sync", "sLD", 16)
        P.op("vector", lambda v, qtr=qtr: v.tensor_copy(
            out=wqd_bf[:, qtr * 4 * QC:(qtr + 1) * 4 * QC], in_=wstage[:, 0:4 * 1024]),
            [("sLD", mq)])
        P.mark("vector", "sDV")
    m_wqdc = P.now("sDV")
    for half in range(2):
        for c in range(16):
            P.op("tensor", lambda e, c=c, half=half: e.matmul(
                ps[half][0:1, :], onec_bf,
                wqd_bf[:, c * QC + half * 512: c * QC + (half + 1) * 512],
                start=(c == 0), stop=(c == 15)),
                [("sDV", m_wqdc)] if (c == 0 and half == 0) else [])
    m_pe = P.mark("tensor", "sPE")
    P.op("vector", lambda v: v.tensor_copy(out=colqd_bf[0:1, 0:512], in_=ps[0][0:1, :]),
         [("sPE", m_pe)])
    P.op("vector", lambda v: v.tensor_copy(out=colqd_bf[0:1, 512:1024], in_=ps[1][0:1, :]))
    m_colqd = P.mark("vector", "sDV")

    r_m = srow[0:1, 0:512]
    r_ss = srow[0:1, 512:1024]
    r_t = srow[0:1, 1024:1536]
    rstd_bf = rowbf[0:1, 0:512]
    mrs_bf = rowbf[0:1, 512:1024]
    for tt in range(8):
        own = tt < 2
        P.op("sync", lambda e, tt=tt: e.dma_start(
            out=xstage.rearrange("p (c n) -> p c n", c=16),
            in_=x_fm[:, tt * 512:(tt + 1) * 512].rearrange("(c p) n -> p c n", p=128)),
            [("sPE", P.now("sPE")), ("sAC", P.now("sAC")), ("sDV", P.now("sDV"))])
        m_x = P.mark("sync", "sLD", 16)
        P.op("vector", lambda v: v.tensor_copy(out=xbf, in_=xstage), [("sLD", m_x)])
        m_xbf = P.mark("vector", "sDV")
        P.op("scalar", lambda s: s.activation(sqbf, xstage, AF.Square), [("sLD", m_x)])
        m_sq = P.mark("scalar", "sAC")
        for c in range(16):
            P.op("tensor", lambda e, c=c: e.matmul(
                ps[1][0:1, :], onec_bf, xbf[:, c * 512:(c + 1) * 512],
                start=(c == 0), stop=(c == 15)),
                ([("sDV", m_xbf)] + P.guard(1)) if c == 0 else [])
        for c in range(16):
            P.op("tensor", lambda e, c=c: e.matmul(
                ps[2][0:1, :], onec_bf, sqbf[:, c * 512:(c + 1) * 512],
                start=(c == 0), stop=(c == 15)),
                ([("sAC", m_sq)] + P.guard(2)) if c == 0 else [])
        m_stats = P.mark("tensor", "sPE")
        P.op("vector", lambda v: v.tensor_scalar_mul(r_m, ps[1][0:1, :], 1.0 / H),
             [("sPE", m_stats)])
        P.op("vector", lambda v: v.tensor_scalar_mul(r_ss, ps[2][0:1, :], 1.0 / H))
        m_statrd = P.mark("vector", "sDV")
        P.set_guard(1, [("sDV", m_statrd)])
        P.set_guard(2, [("sDV", m_statrd)])
        P.op("vector", lambda v: v.scalar_tensor_tensor(r_t, r_m, 1.0, r_m, ALU.mult, ALU.mult))
        m_h0 = P.mark("vector", "sDV")
        P.op("vector", lambda v: v.tensor_tensor(r_ss, r_ss, r_t, ALU.subtract),
            [("sDV", m_h0)])
        m_var = P.mark("vector", "sDV")
        P.op("scalar", lambda s: s.activation(r_ss, r_ss, AF.Sqrt, bias=eps_col[0:1, :]),
             [("sDV", m_var)])
        m_sd = P.mark("scalar", "sAC")
        P.op("vector", lambda v: v.reciprocal(r_ss, r_ss), [("sAC", m_sd)])
        m_h1 = P.mark("vector", "sDV")
        P.op("vector", lambda v: v.tensor_copy(out=rstd_bf, in_=r_ss), [("sDV", m_h1)])
        P.op("vector", lambda v: v.scalar_tensor_tensor(r_m, r_m, -1.0, r_ss, ALU.mult, ALU.mult))
        m_h2 = P.mark("vector", "sDV")
        P.op("vector", lambda v: v.tensor_copy(out=mrs_bf, in_=r_m), [("sDV", m_h2)])
        m_rows = P.mark("vector", "sDV")
        P.op("tensor", lambda e: e.matmul(ps[3], oner_bf, rstd_bf, start=True, stop=True),
             [("sDV", m_rows)] + P.guard(3))
        m_bc = P.mark("tensor", "sPE")
        dst = xbf
        dst3 = dst.rearrange("p (c n) -> p c n", c=16)
        P.op("vector", lambda v, dst3=dst3: v.tensor_tensor(
            dst3, xstage.rearrange("p (c n) -> p c n", c=16),
            ps[3][:, None, :].broadcast_to([128, 16, 512]), ALU.mult),
            [("sPE", m_bc)])
        m_xnr = P.mark("vector", "sDV")
        P.set_guard(3, [("sDV", m_xnr)])
        for ot in range(4):
            bank = 4 + (ot % 2)
            for c in range(16):
                P.op("tensor", lambda e, ot=ot, c=c, dst=dst, bank=bank: e.matmul(
                    ps[bank], wkv_bf[:, c * KV + ot * 128: c * KV + ot * 128 + 128],
                    dst[:, c * 512:(c + 1) * 512], start=(c == 0), stop=False),
                    ([("sDV", m_xnr)] + P.guard(bank)) if c == 0 else [])
            P.op("tensor", lambda e, ot=ot, bank=bank: e.matmul(
                ps[bank], colkv_bf[0:1, ot * 128:(ot + 1) * 128], mrs_bf,
                start=False, stop=True))
            m_kv = P.mark("tensor", "sPE")
            P.op("scalar", lambda s, ot=ot, tt=tt, bank=bank: s.activation(
                kv_c[:, ot * S + tt * 512: ot * S + (tt + 1) * 512], ps[bank],
                AF.Identity, bias=bkv_sb[:, ot:ot + 1]), [("sPE", m_kv)])
            m_ev = P.mark("scalar", "sAC")
            P.set_guard(bank, [("sAC", m_ev)])
        if own:
            for ot in range(8):
                bank = 6 + (ot % 2)
                for c in range(16):
                    P.op("tensor", lambda e, ot=ot, c=c, dst=dst, bank=bank: e.matmul(
                        ps[bank], wqd_bf[:, c * QC + ot * 128: c * QC + ot * 128 + 128],
                        dst[:, c * 512:(c + 1) * 512], start=(c == 0), stop=False),
                        P.guard(bank) if c == 0 else [])
                P.op("tensor", lambda e, ot=ot, bank=bank: e.matmul(
                    ps[bank], colqd_bf[0:1, ot * 128:(ot + 1) * 128], mrs_bf,
                    start=False, stop=True))
                m_qc = P.mark("tensor", "sPE")
                P.op("scalar", lambda s, ot=ot, tt=tt, bank=bank: s.activation(
                    q_c[:, ot * TOWN + tt * 512: ot * TOWN + (tt + 1) * 512], ps[bank],
                    AF.Identity, bias=bqd_sb[:, ot:ot + 1]), [("sPE", m_qc)])
                m_ev = P.mark("scalar", "sAC")
                P.set_guard(bank, [("sAC", m_ev)])

    if _DEBUG:
        P.op("gpsimd", lambda e: e.dma_start(out=dbg_kvc, in_=kv_c),
             [("sAC", P.now("sAC"))])
        P.mark("gpsimd", "sST", 16)
        P.op("gpsimd", lambda e: e.dma_start(out=dbg_qc, in_=q_c))
        P.mark("gpsimd", "sST", 16)

    nc.sbuf_base = base_persist
    ctx_pair = sb("ctx_pair", [128, 2 * TOWN], BF16)
    mkv_bf = sb("mkv_bf", [128, 4], BF16)
    mkv_f = sb("mkv_f", [128, 4], F32)
    wstp = sb("wstp", [128, 4 * 256], F32)
    wku_bf = sb("wku_bf", [128, 4 * 256], BF16)
    wvu_bf = sb("wvu_bf", [128, 4 * 256], BF16)
    wkr_bf = sb("wkr_bf", [128, 4 * 128], BF16)
    wqu_bf = sb("wqu_bf", [128, 8 * 256], BF16)
    wqr_bf = sb("wqr_bf", [128, 8 * 128], BF16)
    kc0 = sb("kc0", [128, S], BF16)
    kc1 = sb("kc1", [128, S], BF16)
    krf32 = sb("krf32", [128, S], F32)
    krot = sb("krot", [128, S], BF16)
    vpair = sb("vpair", [128, 32 * 256], BF16)
    qcs = [sb(f"qcs{i}", [128, TOWN], BF16) for i in range(2)]
    qrf32 = sb("qrf32", [128, TOWN], F32)
    qrot = sb("qrot", [128, TOWN], BF16)
    pT = [sb(f"pT{i}", [128, 512], BF16) for i in range(2)]
    linv_sb = sb("linv_sb", [128, 512], F32)
    biasK = sb("biasK", [128, 4], F32)

    P.op("vector", lambda v: v.tensor_reduce(
        mkv_f, kv_c.rearrange("p (c n) -> p c n", c=4), AX.X, ALU.add),
        [("sAC", P.now("sAC"))])
    m_red = P.mark("vector", "sDV")
    P.op("vector", lambda v: v.tensor_scalar(mkv_bf, mkv_f, 1.0 / S, None, ALU.mult),
         [("sDV", m_red)])
    m_mkv = P.mark("vector", "sDV")

    kcs = (kc0, kc1)
    for pr in range(NH // 2):
        h0 = 2 * pr
        loads = (
            (wkuT[:, h0 * DH:(h0 + 2) * DH], wku_bf, 4, 256),
            (wvuT[:, h0 * DH:(h0 + 2) * DH], wvu_bf, 4, 256),
            (wkrT[:, h0 * DR:(h0 + 2) * DR], wkr_bf, 4, 128),
            (wquT[:, h0 * DH:(h0 + 2) * DH], wqu_bf, 8, 256),
            (wqrT[:, h0 * DR:(h0 + 2) * DR], wqr_bf, 8, 128),
        )
        for src, dstb, nch, w in loads:
            per = max(1, 1024 // w)
            for r0 in range(0, nch, per):
                n2 = min(per, nch - r0)
                P.op("sync", lambda e, src=src, r0=r0, n2=n2, w=w: e.dma_start(
                    out=wstp[:, 0:n2 * w].rearrange("p (c n) -> p c n", c=n2),
                    in_=src[r0 * 128:(r0 + n2) * 128, :].rearrange(
                        "(c p) n -> p c n", p=128)),
                    [("sDV", P.now("sDV")), ("sPE", P.now("sPE")),
                     ("sST", P.now("sST"))])
                m_l = P.mark("sync", "sLD", 16)
                P.op("vector", lambda v, dstb=dstb, r0=r0, n2=n2, w=w: v.tensor_copy(
                    out=dstb[:, r0 * w:(r0 + n2) * w], in_=wstp[:, 0:n2 * w]),
                    [("sLD", m_l)])
                P.mark("vector", "sDV")
        m_wld = P.now("sDV")

        for hh in range(2):
            for c in range(4):
                P.op("tensor", lambda e, hh=hh, c=c: e.matmul(
                    ps[0][:, hh:hh + 1],
                    wku_bf[:, c * 256 + hh * 128: c * 256 + hh * 128 + 128],
                    mkv_bf[:, c:c + 1], start=(c == 0), stop=(c == 3)),
                    ([("sDV", m_wld), ("sDV", m_mkv)] + P.guard(0)) if (hh == 0 and c == 0) else [])
        for c in range(4):
            P.op("tensor", lambda e, c=c: e.matmul(
                ps[0][:, 2:3], wkr_bf[:, c * 128:(c + 1) * 128],
                mkv_bf[:, c:c + 1], start=(c == 0), stop=(c == 3)))
        m_kbar = P.mark("tensor", "sPE")
        P.op("vector", lambda v: v.tensor_scalar_mul(biasK[:, 0:3], ps[0][:, 0:3], -1.0),
             [("sPE", m_kbar), ("sAC", P.now("sAC"))])
        m_biasK = P.mark("vector", "sDV")
        P.set_guard(0, [("sDV", m_biasK)])

        for hh in range(2):
            for k8 in range(8):
                bank = 1 + (k8 % 2)
                for c in range(4):
                    P.op("tensor", lambda e, hh=hh, k8=k8, c=c, bank=bank: e.matmul(
                        ps[bank],
                        wku_bf[:, c * 256 + hh * 128: c * 256 + hh * 128 + 128],
                        kv_c[:, c * S + k8 * 512: c * S + (k8 + 1) * 512],
                        start=(c == 0), stop=(c == 3)),
                        ([("sDV", m_biasK)] + P.guard(bank)) if c == 0 else [])
                m_k = P.mark("tensor", "sPE")
                P.op("scalar", lambda s, hh=hh, k8=k8, bank=bank: s.activation(
                    kcs[hh][:, k8 * 512:(k8 + 1) * 512], ps[bank], AF.Identity,
                    bias=biasK[:, hh:hh + 1]), [("sPE", m_k)])
                m_e = P.mark("scalar", "sAC")
                P.set_guard(bank, [("sAC", m_e)])
        for k8 in range(8):
            bank = 1 + (k8 % 2)
            for c in range(4):
                P.op("tensor", lambda e, k8=k8, c=c, bank=bank: e.matmul(
                    ps[bank], wkr_bf[:, c * 128:(c + 1) * 128],
                    kv_c[:, c * S + k8 * 512: c * S + (k8 + 1) * 512],
                    start=(c == 0), stop=(c == 3)), P.guard(bank) if c == 0 else [])
            m_k = P.mark("tensor", "sPE")
            P.op("scalar", lambda s, k8=k8, bank=bank: s.activation(
                krf32[:, k8 * 512:(k8 + 1) * 512], ps[bank], AF.Identity,
                bias=biasK[:, 2:3]), [("sPE", m_k)])
            m_e = P.mark("scalar", "sAC")
            P.set_guard(bank, [("sAC", m_e)])
        m_krf = P.now("sAC")
        for hh in range(2):
            for k8 in range(8):
                bank = 3 + (k8 % 2)
                P.op("tensor", lambda e, hh=hh, k8=k8, bank=bank, pr=pr: e.matmul(
                    ps[bank][0:DR, :],
                    rmat_sb[hh * DR:(hh + 1) * DR, pr * DR:(pr + 1) * DR],
                    krf32[hh * DR:(hh + 1) * DR, k8 * 512:(k8 + 1) * 512],
                    start=True, stop=True),
                    ([("sAC", m_krf)] + P.guard(bank)) if k8 in (0, 1) else P.guard(bank))
                m_r = P.mark("tensor", "sPE")
                P.op("scalar", lambda s, hh=hh, k8=k8, bank=bank: s.activation(
                    krot[hh * DR:(hh + 1) * DR, k8 * 512:(k8 + 1) * 512],
                    ps[bank][0:DR, :], AF.Copy), [("sPE", m_r)])
                m_e = P.mark("scalar", "sAC")
                P.set_guard(bank, [("sAC", m_e)])
        for kt in range(32):
            bank = 5 + (kt % 2)
            for c in range(4):
                P.op("tensor", lambda e, kt=kt, c=c, bank=bank: e.matmul(
                    ps[bank][:, 0:256], kv_c[:, c * S + kt * 128: c * S + kt * 128 + 128],
                    wvu_bf[:, c * 256:(c + 1) * 256], start=(c == 0), stop=False),
                    P.guard(bank) if c == 0 else [])
            P.op("tensor", lambda e, kt=kt, bank=bank, h0=h0: e.matmul(
                ps[bank][:, 0:256], oner_bf, bvu_bf[0:1, h0 * DH: h0 * DH + 256],
                start=False, stop=True))
            m_v = P.mark("tensor", "sPE")
            P.op("scalar", lambda s, kt=kt, bank=bank: s.activation(
                vpair[:, kt * 256:(kt + 1) * 256], ps[bank][:, 0:256], AF.Copy),
                [("sPE", m_v)])
            m_e = P.mark("scalar", "sAC")
            P.set_guard(bank, [("sAC", m_e)])
        for hh in range(2):
            for qh in range(2):
                bank = 1 + (qh % 2)
                for c in range(8):
                    P.op("tensor", lambda e, hh=hh, qh=qh, c=c, bank=bank: e.matmul(
                        ps[bank],
                        wqu_bf[:, c * 256 + hh * 128: c * 256 + hh * 128 + 128],
                        q_c[:, c * TOWN + qh * 512: c * TOWN + (qh + 1) * 512],
                        start=(c == 0), stop=(c == 7)), P.guard(bank) if c == 0 else [])
                m_q = P.mark("tensor", "sPE")
                P.op("scalar", lambda s, hh=hh, qh=qh, bank=bank, h0=h0: s.activation(
                    qcs[hh][:, qh * 512:(qh + 1) * 512], ps[bank], AF.Identity,
                    bias=bqu_sb[:, h0 + hh: h0 + hh + 1]), [("sPE", m_q)])
                m_e = P.mark("scalar", "sAC")
                P.set_guard(bank, [("sAC", m_e)])
        for qh in range(2):
            bank = 1 + (qh % 2)
            for c in range(8):
                P.op("tensor", lambda e, qh=qh, c=c, bank=bank: e.matmul(
                    ps[bank], wqr_bf[:, c * 128:(c + 1) * 128],
                    q_c[:, c * TOWN + qh * 512: c * TOWN + (qh + 1) * 512],
                    start=(c == 0), stop=(c == 7)), P.guard(bank) if c == 0 else [])
            m_q = P.mark("tensor", "sPE")
            P.op("scalar", lambda s, qh=qh, bank=bank, pr=pr: s.activation(
                qrf32[:, qh * 512:(qh + 1) * 512], ps[bank], AF.Identity,
                bias=bqr_sb[:, pr:pr + 1]), [("sPE", m_q)])
            m_e = P.mark("scalar", "sAC")
            P.set_guard(bank, [("sAC", m_e)])
        m_qrf = P.now("sAC")
        for hh in range(2):
            for qh in range(2):
                bank = 3 + (qh % 2)
                P.op("tensor", lambda e, hh=hh, qh=qh, bank=bank, pr=pr: e.matmul(
                    ps[bank][0:DR, :],
                    rmat_sb[hh * DR:(hh + 1) * DR, pr * DR:(pr + 1) * DR],
                    qrf32[hh * DR:(hh + 1) * DR, qh * 512:(qh + 1) * 512],
                    start=True, stop=True), [("sAC", m_qrf)] + P.guard(bank))
                m_r = P.mark("tensor", "sPE")
                P.op("scalar", lambda s, hh=hh, qh=qh, bank=bank: s.activation(
                    qrot[hh * DR:(hh + 1) * DR, qh * 512:(qh + 1) * 512],
                    ps[bank][0:DR, :], AF.Copy), [("sPE", m_r)])
                m_e = P.mark("scalar", "sAC")
                P.set_guard(bank, [("sAC", m_e)])
        m_qrot = P.now("sAC")

        if _DEBUG and pr == 0:
            P.op("gpsimd", lambda e: e.dma_start(out=dbg_kc, in_=kc0),
                 [("sAC", m_qrot)])
            P.mark("gpsimd", "sST", 16)
            P.op("gpsimd", lambda e: e.dma_start(out=dbg_kr, in_=krot))
            P.mark("gpsimd", "sST", 16)
            P.op("gpsimd", lambda e: e.dma_start(out=dbg_v, in_=vpair))
            P.mark("gpsimd", "sST", 16)
            P.op("gpsimd", lambda e: e.dma_start(out=dbg_qcs, in_=qcs[0]))
            P.mark("gpsimd", "sST", 16)
            P.op("gpsimd", lambda e: e.dma_start(out=dbg_qrot, in_=qrot))
            P.mark("gpsimd", "sST", 16)
            P.op("gpsimd", lambda e: e.dma_start(out=dbg_bvu, in_=bvu_bf))
            P.mark("gpsimd", "sST", 16)
            P.op("gpsimd", lambda e: e.dma_start(out=dbg_oner, in_=oner_bf))
            P.mark("gpsimd", "sST", 16)
            P.op("gpsimd", lambda e: e.dma_start(out=dbg_wvu, in_=wvu_bf))
            P.mark("gpsimd", "sST", 16)
            P.op("gpsimd", lambda e: e.dma_start(out=dbg_wqu, in_=wqu_bf))
            m_dbg = P.mark("gpsimd", "sST", 16)

        for hh in range(2):
            for qh in range(2):
                for kt in range(32):
                    bank = 1 + (kt % 2)
                    P.op("tensor", lambda e, hh=hh, qh=qh, kt=kt, bank=bank: e.matmul(
                        ps[bank], kcs[hh][:, kt * 128:(kt + 1) * 128],
                        qcs[hh][:, qh * 512:(qh + 1) * 512], start=True, stop=False),
                        ([("sAC", m_qrot)] if (hh == 0 and qh == 0 and kt == 0) else []) + P.guard(bank))
                    P.op("tensor", lambda e, hh=hh, qh=qh, kt=kt, bank=bank: e.matmul(
                        ps[bank], krot[hh * DR:(hh + 1) * DR, kt * 128:(kt + 1) * 128],
                        qrot[hh * DR:(hh + 1) * DR, qh * 512:(qh + 1) * 512],
                        start=False, stop=True))
                    m_s = P.mark("tensor", "sPE")
                    P.op("scalar", lambda s, kt=kt, bank=bank: s.activation(
                        pT[kt % 2], ps[bank], AF.Exp, scale=ALPHA),
                        [("sPE", m_s)] + P.guard(bank + 10))
                    m_p = P.mark("scalar", "sAC")
                    P.set_guard(bank, [("sAC", m_p)])
                    P.op("tensor", lambda e, hh=hh, kt=kt: e.matmul(
                        ps[5], vpair[:, kt * 256 + hh * 128: kt * 256 + hh * 128 + 128],
                        pT[kt % 2], start=(kt == 0), stop=(kt == 31)),
                        [("sAC", m_p)] + (P.guard(5) if kt == 0 else []))
                    P.op("tensor", lambda e, kt=kt: e.matmul(
                        ps[6][0:1, :], onec_bf, pT[kt % 2],
                        start=(kt == 0), stop=(kt == 31)),
                        P.guard(6) if kt == 0 else [])
                    m_ctxmm = P.mark("tensor", "sPE")
                    P.set_guard(bank + 10, [("sPE", m_ctxmm)])
                m_l = P.now("sPE")
                P.op("vector", lambda v: v.reciprocal(srow[0:1, 1536:2048], ps[6][0:1, :]),
                     [("sPE", m_l)])
                m_h3 = P.mark("vector", "sDV")
                P.op("vector", lambda v: v.tensor_copy(out=rowbf[0:1, 0:512],
                                                       in_=srow[0:1, 1536:2048]),
                     [("sDV", m_h3)])
                m_linv = P.mark("vector", "sDV")
                P.set_guard(6, [("sDV", m_linv)])
                P.op("tensor", lambda e: e.matmul(ps[7], oner_bf, rowbf[0:1, 0:512],
                                                  start=True, stop=True),
                     [("sDV", m_linv)] + P.guard(7))
                m_bc = P.mark("tensor", "sPE")
                P.op("scalar", lambda s: s.activation(linv_sb, ps[7], AF.Copy),
                     [("sPE", m_bc)])
                m_lsb = P.mark("scalar", "sAC")
                P.set_guard(7, [("sAC", m_lsb)])
                P.op("vector", lambda v, hh=hh, qh=qh: v.tensor_tensor(
                    ctx_pair[:, hh * TOWN + qh * 512: hh * TOWN + (qh + 1) * 512],
                    ps[5], linv_sb, ALU.mult),
                    [("sAC", m_lsb)] + P.guard(22))
                m_cv = P.mark("vector", "sDV")
                P.set_guard(5, [("sDV", m_cv)])
                P.set_guard(21, [("sDV", m_cv)])
        P.op("gpsimd", lambda e, h0=h0: e.dma_start(
            out=ctx_d[:, h0 * TOWN:(h0 + 2) * TOWN], in_=ctx_pair),
            [("sDV", m_cv)])
        m_ctx_st = P.mark("gpsimd", "sST", 16)
        P.set_guard(22, [("sST", m_ctx_st)])

    m_attn_done = P.now("sDV")
    nc.sbuf_base = base_persist
    ctx_full = sb("ctx_full", [128, NH * TOWN], BF16)
    wo_st = sb("wo_st", [128, 16 * 128], F32)
    wo_bf = sb("wo_bf", [128, 16 * 128], BF16)
    ev_o = sb("ev_o", [128, 512], F32)
    xr_o = sb("xr_o", [128, 1024], F32)
    P.op("sync", lambda e: e.dma_start(out=ctx_full, in_=ctx_d),
         [("sDV", m_attn_done), ("sAC", P.now("sAC")), ("sPE", P.now("sPE")),
          ("sST", P.now("sST"))])
    m_ctxf = P.mark("sync", "sLD", 16)
    for odt in range(16):
        P.op("sync", lambda e, odt=odt: e.dma_start(
            out=wo_st.rearrange("p (c n) -> p c n", c=16),
            in_=woutT[:, odt * 128:(odt + 1) * 128].rearrange("(c p) n -> p c n", p=128)),
            [("sDV", P.now("sDV")), ("sPE", P.now("sPE"))])
        m_l = P.mark("sync", "sLD", 16)
        P.op("vector", lambda v: v.tensor_copy(out=wo_bf, in_=wo_st), [("sLD", m_l)])
        m_w = P.mark("vector", "sDV")
        for qh in range(2):
            bank = 1 + qh
            for c in range(16):
                P.op("tensor", lambda e, odt=odt, qh=qh, c=c, bank=bank: e.matmul(
                    ps[bank], wo_bf[:, c * 128:(c + 1) * 128],
                    ctx_full[:, c * TOWN + qh * 512: c * TOWN + (qh + 1) * 512],
                    start=(c == 0), stop=(c == 15)),
                    ([("sDV", m_w), ("sLD", m_ctxf)] + P.guard(bank)) if c == 0 else [])
            m_o = P.mark("tensor", "sPE")
            P.op("scalar", lambda s, odt=odt, bank=bank: s.activation(
                ev_o, ps[bank], AF.Identity, bias=bout_sb[:, odt:odt + 1]),
                [("sPE", m_o)] + P.guard(20) + P.guard(21))
            m_oe = P.mark("scalar", "sAC")
            P.set_guard(bank, [("sAC", m_oe)])
            P.op("sync", lambda e, odt=odt, qh=qh: e.dma_start(
                out=xr_o[:, 0:512],
                in_=x_fm[odt * 128:(odt + 1) * 128, qh * 512:(qh + 1) * 512]),
                [("sDV", P.now("sDV"))])
            m_xr = P.mark("sync", "sLD", 16)
            P.op("vector", lambda v: v.tensor_tensor(
                xr_o[:, 512:1024], ev_o, xr_o[:, 0:512], ALU.add),
                [("sAC", m_oe), ("sLD", m_xr), ("sST", P.now("sST"))])
            m_acc = P.mark("vector", "sDV")
            P.set_guard(20, [("sDV", m_acc)])
            P.op("gpsimd", lambda e, odt=odt, qh=qh: e.dma_start(
                out=acc_d[odt * 128:(odt + 1) * 128, qh * 512:(qh + 1) * 512],
                in_=xr_o[:, 512:1024]), [("sDV", m_acc)])
            P.mark("gpsimd", "sST", 16)
    m_acc_all = P.now("sST")

    nc.sbuf_base = base_consts
    xq_fm = sb("xq_fm", [128, 16 * TOWN], BF16)
    scr = [sb(f"scr{i}", [128, 2048], F32) for i in range(2)]
    scrb = [sb(f"scrb{i}", [128, 2048], BF16) for i in range(4)]
    smq = [sb(f"smq{i}", [128, 256], F32) for i in range(4)]
    smb = sb("smb", [128, 256], BF16)
    cols = sb("cols", [128, 4], F32)
    base_fbranch = nc.sbuf_base
    xt = sb("xt", [128, H], F32)
    sqf = sb("sqf", [128, H], F32)
    xqt = sb("xqt", [128, H], BF16)
    nc.sbuf_base = base_fbranch
    wst = [sb(f"wst{i}", [128, 2048], F32) for i in range(4)]
    wq_nat = sb("wq_nat", [128, 4 * 2048], BF16)
    wgqT = sb("wgqT", [128, 16 * 512], BF16)
    wuqT = sb("wuqT", [128, 16 * 512], BF16)
    sg_f = sb("sg_f", [128, 512], F32)
    _ism_pack = sb("_ism_pack", [128, 32], F32)
    it_f = [sb(f"it_f{i}", [128, 512], F32) for i in range(2)]
    itq = [sb(f"itq{i}", [128, 512], BF16) for i in range(2)]

    m_scr_act = 0
    for tt in range(8):
        P.op("sync", lambda e, tt=tt: e.dma_start(
            out=xt, in_=x_tm[tt * 128:(tt + 1) * 128, :]),
            [("sDV", P.now("sDV")), ("sAC", P.now("sAC")), ("sST", P.now("sST"))])
        m_x = P.mark("sync", "sLD", 16)
        P.op("vector", lambda v: v.tensor_reduce(cols[:, 0:1], xt, AX.X, ALU.add),
             [("sLD", m_x)])
        m_red = P.mark("vector", "sDV")
        P.op("vector", lambda v: v.tensor_scalar_mul(cols[:, 0:1], cols[:, 0:1], 1.0 / H),
             [("sDV", m_red)])
        m_m = P.mark("vector", "sDV")
        P.op("scalar", lambda s: s.activation(sqf, xt, AF.Square), [("sLD", m_x)])
        m_sq = P.mark("scalar", "sAC")
        P.op("vector", lambda v: v.tensor_reduce(cols[:, 1:2], sqf, AX.X, ALU.add),
             [("sAC", m_sq)])
        m_red2 = P.mark("vector", "sDV")
        P.op("vector", lambda v: v.tensor_scalar_mul(cols[:, 1:2], cols[:, 1:2], 1.0 / H),
             [("sDV", m_red2)])
        m_g1 = P.mark("vector", "sDV")
        P.op("vector", lambda v: v.scalar_tensor_tensor(
            cols[:, 2:3], cols[:, 0:1], 1.0, cols[:, 0:1], ALU.mult, ALU.mult),
            [("sDV", m_g1)])
        m_g2 = P.mark("vector", "sDV")
        P.op("vector", lambda v: v.tensor_tensor(cols[:, 1:2], cols[:, 1:2],
                                                 cols[:, 2:3], ALU.subtract),
            [("sDV", m_g2)])
        m_var = P.mark("vector", "sDV")
        P.op("scalar", lambda s: s.activation(cols[:, 1:2], cols[:, 1:2], AF.Sqrt,
                                              bias=eps_col), [("sDV", m_var)])
        m_sd = P.mark("scalar", "sAC")
        P.op("vector", lambda v: v.reciprocal(cols[:, 1:2], cols[:, 1:2]),
             [("sAC", m_sd)])
        m_g3 = P.mark("vector", "sDV")
        P.op("vector", lambda v: v.scalar_tensor_tensor(
            cols[:, 3:4], cols[:, 0:1], -1.0, cols[:, 1:2], ALU.mult, ALU.mult),
            [("sDV", m_g3)])
        m_xn0 = P.mark("vector", "sDV")
        P.op("scalar", lambda s: s.activation(sqf, xt, AF.Identity,
                                              scale=cols[:, 1:2], bias=cols[:, 3:4]),
             [("sDV", m_xn0)])
        m_xn = P.mark("scalar", "sAC")
        sqf3 = sqf.rearrange("p (b c) -> p b c", c=32)
        m_sc = emit_scale_batch(
            P, [(sqf3, slice(0, 64), [("sAC", m_xn)])],
            smq[0][:, 0:64], smq[1][:, 0:64], smq[2][:, 0:64], smq[3][:, 0:64],
            smb[:, 0:64], half=False)
        m_as_q, m_q = emit_quant_act_multi(P, [dict(
            src=sqf, outq=xqt, rs=smq[2][:, 0:64], scb=smb[:, 0:64],
            zf=scr[0], azf=scr[1], g1=scrb[0], g2=scrb[1], g3=scrb[2],
            g4=scrb[3], wt=[("sAC", m_xn)],
            wt_out=[("sST", P.now("sST"))])], m_sc, m_scr_act)
        m_scr_act = m_as_q
        P.op("gpsimd", lambda e, tt=tt: e.dma_start(
            out=xq_d[tt * 128:(tt + 1) * 128, :], in_=xqt), [("sDV", m_q)])
        P.mark("gpsimd", "sST", 16)
    m_xq_st = P.now("sST")
    P.op("sync", lambda e: e.dma_start(
        out=xq_fm.rearrange("p (c n) -> p c n", c=16),
        in_=xq_d.rearrange("n (c p) -> n c p", p=128), transpose=True),
        [("sST", m_xq_st)])
    m_xqfm = P.mark("sync", "sLD", 16)

    f0_tail = [("sDV", P.now("sDV")), ("sAC", P.now("sAC"))]
    m_wst_free = [list(f0_tail) for _ in range(4)]
    m_st_wqnat = 0
    m_itq_store = [0, 0]

    def quant_w_group(subs_loader, store_fn):
        nonlocal m_scr_act
        jobs = []
        m_loads = []
        for sub in range(4):
            P.op("sync", subs_loader(sub), m_wst_free[sub])
            m_l = P.mark("sync", "sLD", 16)
            m_loads.append(m_l)
            w3 = wst[sub][:].rearrange("p (b c) -> p b c", c=32)
            jobs.append((w3, slice(sub * 64, (sub + 1) * 64), [("sLD", m_l)]))
        m_sc = emit_scale_batch(P, jobs, smq[0][:, 0:256], smq[1][:, 0:256],
                                smq[2][:, 0:256], smq[3][:, 0:256],
                                smb[:, 0:256], half=True)
        for sub in range(4):
            m_z, m_as, m_fin = emit_quant_w(
                P, wst[sub][:], wq_nat[:, sub * 2048:(sub + 1) * 2048],
                smq[2][:, sub * 64:(sub + 1) * 64],
                smb[:, sub * 64:(sub + 1) * 64],
                scr[0], scrb[0], scrb[1], scrb[2], m_sc,
                [("sLD", m_loads[sub])],
                wt_out=([("sST", m_st_wqnat)] if sub == 0 else ()),
                m_scr_act=m_scr_act)
            m_scr_act = m_as
            m_wst_free[sub] = [("sDV", m_z)]
        store_fn(m_fin)

    for seg in range(16):
        for mat, (wsrc, wqd_t, wT_dst) in enumerate(((wg, wgq_d, wgqT), (wu, wuq_d, wuqT))):
            def loader(sub, wsrc=wsrc, seg=seg):
                return lambda e, wsrc=wsrc, seg=seg, sub=sub: e.dma_start(
                    out=wst[sub],
                    in_=wsrc[seg * 512 + sub * 128: seg * 512 + (sub + 1) * 128, :])

            def store(m_fin, wqd_t=wqd_t, wT_dst=wT_dst, seg=seg):
                nonlocal m_st_wqnat
                P.op("gpsimd", lambda e, wqd_t=wqd_t, seg=seg: e.dma_start(
                    out=wqd_t[seg * 512:(seg + 1) * 512, :].rearrange(
                        "(s p) h -> p s h", p=128),
                    in_=wq_nat.rearrange("p (s h) -> p s h", s=4)),
                    [("sDV", m_fin)])
                m_st = P.mark("gpsimd", "sST", 16)
                m_st_wqnat = m_st
                P.op("sync", lambda e, wqd_t=wqd_t, wT_dst=wT_dst, seg=seg: e.dma_start(
                    out=wT_dst.rearrange("p (c n) -> p c n", c=16),
                    in_=wqd_t[seg * 512:(seg + 1) * 512, :].rearrange(
                        "n (c p) -> n c p", p=128),
                    transpose=True), [("sST", m_st), ("sPE", P.now("sPE"))])
                P.mark("sync", "sLD", 16)

            quant_w_group(loader, store)
        m_wT = P.now("sLD")
        for tp in range(4):
            pair = []
            for half in (0, 1):
                tt = tp * 2 + half
                bg = 1 + 2 * half
                bu = 2 + 2 * half
                for c in range(16):
                    P.op("tensor", lambda e, tt=tt, c=c, bg=bg: e.matmul(
                        ps[bg], xq_fm[:, c * TOWN + tt * 128: c * TOWN + tt * 128 + 128],
                        wgqT[:, c * 512:(c + 1) * 512], start=(c == 0), stop=(c == 15)),
                        ([("sLD", m_wT), ("sLD", m_xqfm)] + P.guard(bg)) if c == 0 else [])
                for c in range(16):
                    P.op("tensor", lambda e, tt=tt, c=c, bu=bu: e.matmul(
                        ps[bu], xq_fm[:, c * TOWN + tt * 128: c * TOWN + tt * 128 + 128],
                        wuqT[:, c * 512:(c + 1) * 512], start=(c == 0), stop=(c == 15)),
                        P.guard(bu) if c == 0 else [])
                m_gu = P.mark("tensor", "sPE")
                P.op("scalar", lambda s, bg=bg: s.activation(sg_f, ps[bg], AF.Silu),
                     [("sPE", m_gu)] + P.guard(30))
                m_sg = P.mark("scalar", "sAC")
                P.set_guard(bg, [("sAC", m_sg)])
                P.op("vector", lambda v, bu=bu, half=half: v.tensor_tensor(
                    it_f[half], sg_f, ps[bu], ALU.mult),
                    [("sAC", m_sg)] + P.guard(31 + half))
                m_it = P.mark("vector", "sDV")
                P.set_guard(bu, [("sDV", m_it)])
                P.set_guard(30, [("sDV", m_it)])
                pair.append(m_it)
            jobs2 = []
            for half in (0, 1):
                i3 = it_f[half][:].rearrange("p (b c) -> p b c", c=32)
                jobs2.append((i3, slice(half * 16, (half + 1) * 16),
                              [("sDV", pair[half])]))
            m_sc2 = emit_scale_batch(P, jobs2, smq[0][:, 0:32], smq[1][:, 0:32],
                                     smq[2][:, 0:32], smq[3][:, 0:32],
                                     smb[:, 0:32], half=False)
            cas = []
            for half in (0, 1):
                sl = slice(half * 512, (half + 1) * 512)
                cas.append(dict(
                    src=it_f[half][:], outq=itq[half][:],
                    rs=smq[2][:, half * 16:(half + 1) * 16],
                    scb=smb[:, half * 16:(half + 1) * 16],
                    zf=scr[0][:, sl], azf=scr[1][:, sl],
                    g1=scrb[0][:, sl], g2=scrb[1][:, sl],
                    g3=scrb[2][:, sl], g4=scrb[3][:, sl],
                    wt=[("sDV", pair[half])],
                    wt_out=[("sST", m_itq_store[half])]))
            m_as2, m_fin2 = emit_quant_act_multi(P, cas, m_sc2, m_scr_act)
            m_scr_act = m_as2
            P.set_guard(31, [("sDV", m_fin2)])
            P.set_guard(32, [("sDV", m_fin2)])
            for half in (0, 1):
                tt = tp * 2 + half
                P.op("gpsimd", lambda e, seg=seg, tt=tt, half=half: e.dma_start(
                    out=itq_d[tt * 128:(tt + 1) * 128, seg * 512:(seg + 1) * 512],
                    in_=itq[half]), [("sDV", m_fin2)])
                m_itq_store[half] = P.mark("gpsimd", "sST", 16)

    for ch in range(16):
        def loader(sub, ch=ch):
            return lambda e, ch=ch, sub=sub: e.dma_start(
                out=wst[sub],
                in_=wd[ch * 128:(ch + 1) * 128, sub * 2048:(sub + 1) * 2048])

        def store(m_fin, ch=ch):
            nonlocal m_st_wqnat
            P.op("gpsimd", lambda e, ch=ch: e.dma_start(
                out=wdq_d[ch * 128:(ch + 1) * 128, :], in_=wq_nat),
                [("sDV", m_fin)])
            m_st_wqnat = P.mark("gpsimd", "sST", 16)

        quant_w_group(loader, store)
    m_wdq = P.now("sST")

    nc.sbuf_base = base_consts
    itq_fm = sb("itq_fm", [128, 64 * TOWN], BF16)
    wdT = [sb(f"wdT{i}", [128, 512], BF16) for i in range(2)]
    ev_f = sb("ev_f", [128, 512], F32)
    xr_f = sb("xr_f", [128, 512], F32)
    P.op("sync", lambda e: e.dma_start(
        out=itq_fm.rearrange("p (c n) -> p c n", c=64),
        in_=itq_d.rearrange("n (c p) -> n c p", p=128), transpose=True),
        [("sST", P.now("sST"))])
    m_itfm = P.mark("sync", "sLD", 16)
    for odg in range(4):
        for oc in range(64):
            bufi = oc % 2
            P.op("sync", lambda e, odg=odg, oc=oc, bufi=bufi: e.dma_start(
                out=wdT[bufi],
                in_=wdq_d[odg * 512:(odg + 1) * 512, oc * 128:(oc + 1) * 128],
                transpose=True),
                [("sST", m_wdq)] + P.guard(40 + bufi))
            m_w = P.mark("sync", "sLD", 16)
            for odt4 in range(4):
                for th in range(2):
                    bank = odt4 * 2 + th
                    P.op("tensor", lambda e, oc=oc, odt4=odt4, th=th, bufi=bufi, bank=bank: e.matmul(
                        ps[bank], wdT[bufi][:, odt4 * 128:(odt4 + 1) * 128],
                        itq_fm[:, oc * TOWN + th * 512: oc * TOWN + (th + 1) * 512],
                        start=(oc == 0), stop=(oc == 63)),
                        ([("sLD", m_w), ("sLD", m_itfm)] + (P.guard(bank) if oc == 0 else []))
                        if (odt4 == 0 and th == 0) else
                        (P.guard(bank) if oc == 0 else []))
            m_mm = P.mark("tensor", "sPE")
            P.set_guard(40 + bufi, [("sPE", m_mm)])
        m_dn = P.now("sPE")
        for odt4 in range(4):
            for th in range(2):
                bank = odt4 * 2 + th
                od = odg * 4 + odt4
                P.op("scalar", lambda s, bank=bank: s.activation(ev_f, ps[bank], AF.Copy),
                     [("sPE", m_dn)] + P.guard(50))
                m_ev = P.mark("scalar", "sAC")
                P.set_guard(bank, [("sAC", m_ev)])
                P.op("sync", lambda e, od=od, th=th: e.dma_start(
                    out=xr_f, in_=acc_d[od * 128:(od + 1) * 128, th * 512:(th + 1) * 512]),
                    [("sST", m_acc_all)] + P.guard(51))
                m_xr = P.mark("sync", "sLD", 16)
                P.op("vector", lambda v: v.tensor_tensor(ev_f, ev_f, xr_f, ALU.add),
                     [("sAC", m_ev), ("sLD", m_xr)])
                m_y = P.mark("vector", "sDV")
                P.set_guard(50, [("sDV", m_y)])
                P.set_guard(51, [("sDV", m_y)])
                P.op("gpsimd", lambda e, od=od, th=th: e.dma_start(
                    out=y_out[od * 128:(od + 1) * 128, th * 512:(th + 1) * 512],
                    in_=ev_f), [("sDV", m_y)])
                m_yst = P.mark("gpsimd", "sST", 16)
                P.set_guard(50, [("sDV", m_y), ("sST", m_yst)])
    m_final = P.now("sST")
    P.op("gpsimd", lambda e: e.engine_nop(), [("sST", m_final)])

    import contextlib
    with contextlib.ExitStack() as es:
        sems = {n: es.enter_context(nc.semaphore(n))
                for n in ("sLD", "sST", "sPE", "sDV", "sAC")}
        block = es.enter_context(nc.Block())

        def replay(eng_obj, stream):
            for fn, waits, inc_sem, inc_n in stream:
                for s, v in waits:
                    eng_obj.wait_ge(sems[s], v)
                ins = fn(eng_obj)
                if inc_sem is not None:
                    ins.then_inc(sems[inc_sem], inc_n)

        @block.sync
        def _(e):
            replay(e, P.streams["sync"])

        @block.vector
        def _(e):
            replay(e, P.streams["vector"])

        @block.scalar
        def _(e):
            replay(e, P.streams["scalar"])

        @block.tensor
        def _(e):
            replay(e, P.streams["tensor"])

        @block.gpsimd
        def _(e):
            replay(e, P.streams["gpsimd"])

    return nc


_NC_CACHE = None


def _get_nc():
    global _NC_CACHE
    if _NC_CACHE is None:
        _NC_CACHE = build_nc()
    return _NC_CACHE


def _rope_mats():
    inv = 10000.0 ** (-(np.arange(0, DR, 2, dtype=np.float64)) / DR)
    rm = np.zeros((128, (NH // 2) * DR), dtype=np.float64)
    for h in range(NH):
        pr, hh = h // 2, h % 2
        ang = h * inv
        c, s = np.cos(ang), np.sin(ang)
        for i in range(DR // 2):
            rm[hh * DR + 2 * i, pr * DR + 2 * i] = c[i]
            rm[hh * DR + 2 * i + 1, pr * DR + 2 * i] = -s[i]
            rm[hh * DR + 2 * i, pr * DR + 2 * i + 1] = s[i]
            rm[hh * DR + 2 * i + 1, pr * DR + 2 * i + 1] = c[i]
    return rm.astype(np.float32)


def kernel(L0_attn_x, L0_attn_kv_down_w, L0_attn_kv_down_b, L0_attn_key_up_w,
           L0_attn_key_up_b, L0_attn_value_up_w, L0_attn_value_up_b,
           L0_attn_key_rope_w, L0_attn_key_rope_b, L0_attn_query_down_w,
           L0_attn_query_down_b, L0_attn_query_up_w, L0_attn_query_up_b,
           L0_attn_query_rope_w, L0_attn_query_rope_b, L0_attn_out_w,
           L0_attn_out_b, L0_ffn_w_gate, L0_ffn_w_up, L0_ffn_w_down):
    f32 = np.float32
    x = np.asarray(L0_attn_x, f32)

    def T(a):
        return np.ascontiguousarray(np.asarray(a, f32).T)

    wkvT = T(L0_attn_kv_down_w)
    wkuT = T(L0_attn_key_up_w)
    wvuT = T(L0_attn_value_up_w)
    wkrT = T(L0_attn_key_rope_w)
    wqdT = T(L0_attn_query_down_w)
    wquT = T(L0_attn_query_up_w)
    wqrT = T(L0_attn_query_rope_w)
    woutT = T(L0_attn_out_w)

    def colpack(b, n):
        return np.ascontiguousarray(np.asarray(b, f32).reshape(n, 128).T)

    b_kv = colpack(L0_attn_kv_down_b, KV // 128)
    b_qd = colpack(L0_attn_query_down_b, QC // 128)
    b_qu = colpack(L0_attn_query_up_b, NH * DH // 128)
    b_out = colpack(L0_attn_out_b, H // 128)
    bqr = np.asarray(L0_attn_query_rope_b, f32).reshape(NH, DR)
    b_qr = np.zeros((128, NH // 2), f32)
    for pr in range(NH // 2):
        b_qr[0:DR, pr] = bqr[2 * pr]
        b_qr[DR:128, pr] = bqr[2 * pr + 1]
    b_vu_row = np.asarray(L0_attn_value_up_b, f32).reshape(1, NH * DH)
    rmat = _rope_mats()
    ones_col = np.ones((128, 1), f32)
    wg = np.ascontiguousarray(np.asarray(L0_ffn_w_gate, f32))
    wu = np.ascontiguousarray(np.asarray(L0_ffn_w_up, f32))
    wd = np.ascontiguousarray(np.asarray(L0_ffn_w_down, f32))

    shared = dict(wkvT=wkvT, wkuT=wkuT, wvuT=wvuT, wkrT=wkrT, wqdT=wqdT,
                  wquT=wquT, wqrT=wqrT, woutT=woutT, b_kv=b_kv, b_qd=b_qd,
                  b_qu=b_qu, b_qr=b_qr, b_vu_row=b_vu_row, b_out=b_out,
                  rmat=rmat, ones_col=ones_col,
                  ones_row=np.ones((1, 128), f32), wg=wg, wu=wu, wd=wd)

    in_maps = []
    for cid in range(NCORES):
        b = cid // 4
        toff = (cid % 4) * TOWN
        xb = x[b]
        x_roll = np.roll(xb, -toff, axis=0)
        m = dict(shared)
        m["x_fm"] = np.ascontiguousarray(x_roll.T)
        m["x_tm"] = np.ascontiguousarray(x_roll[:TOWN])
        in_maps.append(m)

    nc = _get_nc()
    res = run_bass_kernel_spmd(nc, in_maps, core_ids=list(range(NCORES)))

    out = np.empty((B, S, H), dtype=f32)
    for cid in range(NCORES):
        b = cid // 4
        toff = (cid % 4) * TOWN
        y = np.asarray(res.results[cid]["y_out"], f32)
        out[b, toff:toff + TOWN, :] = y.T
    return out


# version 08
# speedup vs baseline: 1.1985x; 1.1985x over previous
import math
import numpy as np
import ml_dtypes
import concourse.bass as bass
import concourse.mybir as mybir
from concourse.bass_utils import run_bass_kernel_spmd

F32 = mybir.dt.float32
BF16 = mybir.dt.bfloat16
AF = mybir.ActivationFunctionType
ALU = mybir.AluOpType
AX = mybir.AxisListType

NCORES = 8
B, S, H = 2, 4096, 2048
TOWN = 1024
KV, QC = 512, 1024
NH, DH, DR = 16, 128, 64
II = 8192
EPS = 1e-5
ALPHA = 1.0 / math.sqrt(DH + DR)
LN2_INV = float(1.0 / np.log(2.0))

ENGS = ("sync", "vector", "scalar", "tensor", "gpsimd")
_DEBUG = False
PARANOID_SCALE = True
PARANOID_ACT = 0


class Plan:

    def __init__(self):
        self.streams = {e: [] for e in ENGS}
        self.cnt = {}
        self.bank_guard = {}

    AUTO = {}

    def op(self, eng, fn, waits=()):
        waits = list(waits)
        auto = self.AUTO.get(eng)
        if auto is not None:
            waits.append((auto, self.cnt.get(auto, 0)))
        for sem, val in waits:
            assert val <= self.cnt.get(sem, 0), (
                f"wait {sem}>={val} exceeds planned count {self.cnt.get(sem, 0)} "
                f"(deadlock)")
        if auto is not None:
            self.cnt[auto] = self.cnt.get(auto, 0) + 1
            self.streams[eng].append([fn, waits, auto, 1])
        else:
            self.streams[eng].append([fn, waits, None, 0])

    def mark(self, eng, sem, n=1):
        entry = self.streams[eng][-1]
        auto = self.AUTO.get(eng)
        if auto is not None:
            assert sem == auto and n == 1, f"{eng} marks only {auto}"
            return self.cnt[sem]
        if entry[2] == sem:
            return self.cnt[sem]
        assert entry[2] is None, "op already has an inc"
        entry[2] = sem
        entry[3] = n
        self.cnt[sem] = self.cnt.get(sem, 0) + n
        return self.cnt[sem]

    def now(self, sem):
        return self.cnt.get(sem, 0)

    def guard(self, bank):
        return self.bank_guard.pop(bank, [])

    def set_guard(self, bank, waits):
        self.bank_guard[bank] = list(waits)



def emit_scale_batch(P, jobs, amax, t0, t1, t2, t2h, half):
    def gop(fn, waits=()):
        P.op("vector", fn, list(waits) + [("sDV", P.now("sDV"))])
        P.mark("vector", "sDV")
    m_prev_ln = getattr(P, "scale_ln_mark", 0)
    for src3, sl, wt in jobs:
        gop(lambda v, src3=src3, sl=sl: v.tensor_reduce(
            amax[:, sl], src3, AX.X, ALU.max, apply_absolute_value=True),
            list(wt) + [("sAC", m_prev_ln)])
    gop(lambda v: v.tensor_scalar_max(amax, amax, 1e-30))
    mv = P.now("sDV")
    P.op("scalar", lambda s: s.activation(t2, amax, AF.Ln), [("sDV", mv)])
    ma = P.mark("scalar", "sAC")
    P.scale_ln_mark = ma
    gop(lambda v: v.tensor_scalar(t2, t2, LN2_INV, 104.5, ALU.mult, ALU.add),
        [("sAC", ma)])
    first = True
    for b in (128, 64, 32, 16, 8, 4, 2, 1):
        fb = float(b)
        gop(lambda v, fb=fb: v.tensor_single_scalar(t1, t2, fb, ALU.is_ge))
        gop(lambda v, fb=fb: v.scalar_tensor_tensor(t2, t1, -fb, t2, ALU.mult, ALU.add))
        if b == 128:
            f64 = float(2.0**64 - 1.0)
            gop(lambda v, f64=f64: v.tensor_scalar(t1, t1, f64, 1.0, ALU.mult, ALU.add))
            gop(lambda v: v.tensor_scalar_mul(t0, t1, float(2.0**-104)))
            gop(lambda v: v.tensor_tensor(t0, t0, t1, ALU.mult))
            first = False
            continue
        fw = float(2.0**b - 1.0)
        gop(lambda v, fw=fw: v.tensor_scalar(t1, t1, fw, 1.0, ALU.mult, ALU.add))
        if first:
            gop(lambda v: v.tensor_scalar_mul(t0, t1, float(2.0**-104)))
            first = False
        else:
            gop(lambda v: v.tensor_tensor(t0, t0, t1, ALU.mult))
    gop(lambda v: v.tensor_tensor(t1, t0, amax, ALU.is_gt))
    gop(lambda v: v.tensor_scalar(t1, t1, -0.125, 0.25, ALU.mult, ALU.add))
    gop(lambda v: v.tensor_tensor(t0, t0, t1, ALU.mult))
    gop(lambda v: v.reciprocal(t1, t0))
    if half:
        gop(lambda v: v.tensor_scalar_mul(t2h, t0, 0.5))
    else:
        gop(lambda v: v.tensor_copy(out=t2h, in_=t0))
    return P.now("sDV")


def emit_quant_w(P, src, outq, rs, sch, azf, b1, b2, b3, m_sc, wt, wt_out=(),
                 m_scr_act=0, wt_b2=()):
    Pp, F = src.shape
    nb = F // 32
    src3 = src.rearrange("p (b c) -> p b c", c=32)
    rs_b = rs[:, :, None].broadcast_to([Pp, nb, 32])
    sch_b = sch[:, :, None].broadcast_to([Pp, nb, 32])
    az3 = azf.rearrange("p (b c) -> p b c", c=32)
    P.op("vector", lambda v: v.tensor_tensor(az3, src3, rs_b, ALU.mult),
         list(wt) + [("sDV", m_sc), ("sAC", m_scr_act)])
    m_z = P.mark("vector", "sDV")
    P.op("scalar", lambda s: s.activation(b3, azf, AF.Sign), [("sDV", m_z)])
    P.op("scalar", lambda s: s.activation(azf, azf, AF.Abs))
    m_as = P.mark("scalar", "sAC")
    V = lambda fn, w=(): P.op("vector", fn, w)
    V(lambda v: v.tensor_single_scalar(b1, azf, 0.25, ALU.is_gt), [("sAC", m_as)])
    V(lambda v: v.scalar_tensor_tensor(b1, azf, 0.75, b1, ALU.is_ge, ALU.add))
    V(lambda v: v.scalar_tensor_tensor(b1, azf, 1.25, b1, ALU.is_gt, ALU.add))
    V(lambda v: v.scalar_tensor_tensor(b1, azf, 1.75, b1, ALU.is_ge, ALU.add))
    V(lambda v: v.tensor_single_scalar(b2, azf, 2.5, ALU.is_gt), list(wt_b2))
    V(lambda v: v.scalar_tensor_tensor(b2, azf, 3.5, b2, ALU.is_ge, ALU.add))
    V(lambda v: v.tensor_single_scalar(azf, azf, 5.0, ALU.is_gt))
    V(lambda v: v.scalar_tensor_tensor(b2, azf, 2.0, b2, ALU.mult, ALU.add))
    V(lambda v: v.scalar_tensor_tensor(b1, b2, 2.0, b1, ALU.mult, ALU.add))
    V(lambda v: v.tensor_tensor(b1, b1, b3, ALU.mult))
    b13 = b1.rearrange("p (b c) -> p b c", c=32)
    oq3 = outq.rearrange("p (b c) -> p b c", c=32)
    V(lambda v: v.tensor_tensor(oq3, b13, sch_b, ALU.mult), list(wt_out))
    return m_z, m_as, P.mark("vector", "sDV")


def emit_quant_act_multi(P, cascades, m_sc, m_scr_act=0):
    seq = []

    for c in cascades:
        Pp, F = c["src"].shape
        nb = F // 32
        src3 = c["src"].rearrange("p (b c) -> p b c", c=32)
        rs_b = c["rs"][:, :, None].broadcast_to([Pp, nb, 32])
        sc_b = c["scb"][:, :, None].broadcast_to([Pp, nb, 32])
        zf, azf = c["zf"], c["azf"]
        g1, g2, g3, g4 = c["g1"], c["g2"], c["g3"], c["g4"]
        z3 = zf.rearrange("p (b c) -> p b c", c=32)
        oq3 = c["outq"].rearrange("p (b c) -> p b c", c=32)
        ops = []
        ops.append(("vz", lambda v, z3=z3, src3=src3, rs_b=rs_b:
                    v.tensor_tensor(z3, src3, rs_b, ALU.mult), c["wt"]))
        ops.append(("s", lambda s, azf=azf, zf=zf: s.activation(azf, zf, AF.Abs), ()))
        ops.append(("s2", lambda s, g1=g1, zf=zf: s.activation(g1, zf, AF.Sign), ()))
        ops.append(("v1", lambda v, g2=g2, azf=azf:
                    v.tensor_single_scalar(g2, azf, 2.0, ALU.is_ge), ()))
        ops.append(("v", lambda v, g3=g3, azf=azf:
                    v.tensor_single_scalar(g3, azf, 4.0, ALU.is_ge), ()))
        ops.append(("v", lambda v, g4=g4, g2=g2:
                    v.tensor_scalar(g4, g2, -4.0, 8.0, ALU.mult, ALU.add), ()))
        ops.append(("v", lambda v, g4=g4, g3=g3:
                    v.scalar_tensor_tensor(g4, g3, -2.0, g4, ALU.mult, ALU.add), ()))
        ops.append(("v", lambda v, g2=g2:
                    v.tensor_scalar(g2, g2, 0.125, 0.125, ALU.mult, ALU.add), ()))
        ops.append(("v", lambda v, g2=g2, g3=g3:
                    v.scalar_tensor_tensor(g2, g3, 0.25, g2, ALU.mult, ALU.add), ()))
        ops.append(("v", lambda v, zf=zf, azf=azf, g4=g4:
                    v.tensor_tensor(zf, azf, g4, ALU.mult), ()))
        ops.append(("v", lambda v, g3=g3, zf=zf:
                    v.tensor_single_scalar(g3, zf, 15.5, ALU.is_ge), ()))
        ops.append(("v", lambda v, azf=azf, g3=g3, zf=zf:
                    v.scalar_tensor_tensor(azf, g3, -16.0, zf, ALU.mult, ALU.add), ()))
        for b in (8, 4, 2, 1):
            fb = float(b)
            ops.append(("v", lambda v, g3=g3, azf=azf, fb=fb:
                        v.tensor_single_scalar(g3, azf, fb - 0.5, ALU.is_ge), ()))
            ops.append(("v", lambda v, azf=azf, g3=g3, fb=fb:
                        v.scalar_tensor_tensor(azf, g3, -fb, azf, ALU.mult, ALU.add), ()))
        ops.append(("v", lambda v, g3=g3, azf=azf:
                    v.scalar_tensor_tensor(g3, azf, -0.5, g3, ALU.is_equal, ALU.mult), ()))
        ops.append(("v", lambda v, zf=zf, azf=azf:
                    v.tensor_tensor(zf, zf, azf, ALU.subtract), ()))
        ops.append(("v", lambda v, zf=zf, g3=g3:
                    v.tensor_tensor(zf, zf, g3, ALU.subtract), ()))
        ops.append(("v", lambda v, zf=zf, g2=g2:
                    v.tensor_tensor(zf, zf, g2, ALU.mult), ()))
        ops.append(("v", lambda v, zf=zf:
                    v.tensor_scalar_min(zf, zf, 7.5), ()))
        ops.append(("v", lambda v, zf=zf, g1=g1:
                    v.tensor_tensor(zf, zf, g1, ALU.mult), ()))
        ops.append(("vo", lambda v, oq3=oq3, z3=z3, sc_b=sc_b:
                    v.tensor_tensor(oq3, z3, sc_b, ALU.mult), c.get("wt_out", ())))
        seq.append(ops)

    merged = []
    n = max(len(s) for s in seq)
    for i in range(n):
        for s in seq:
            if i < len(s):
                merged.append(s[i])
    pend_z = 0
    m_as = 0
    for tag, fn, wt in merged:
        if tag == "vz":
            P.op("vector", fn, list(wt) + [("sDV", m_sc), ("sAC", m_scr_act)])
            P.mark("vector", "sDV")
            pend_z = P.now("sDV")
        elif tag == "s":
            P.op("scalar", fn, [("sDV", pend_z)])
        elif tag == "s2":
            P.op("scalar", fn, ())
            m_as = P.mark("scalar", "sAC")
        elif tag == "v1":
            P.op("vector", fn, [("sAC", m_as)])
        else:
            P.op("vector", fn, list(wt))
    return m_as, P.mark("vector", "sDV")



def build_nc():
    nc = bass.Bass("TRN2", target_bir_lowering=False,
                   detect_race_conditions=False)

    def dram(name, shape, dtype=F32, kind="ExternalInput"):
        return nc.dram_tensor(name, list(shape), dtype, kind=kind).ap()

    x_fm = dram("x_fm", [H, S])
    x_tm = dram("x_tm", [TOWN, H])
    wkvT = dram("wkvT", [H, KV])
    wkuT = dram("wkuT", [KV, NH * DH])
    wvuT = dram("wvuT", [KV, NH * DH])
    wkrT = dram("wkrT", [KV, NH * DR])
    wqdT = dram("wqdT", [H, QC])
    wquT = dram("wquT", [QC, NH * DH])
    wqrT = dram("wqrT", [QC, NH * DR])
    woutT = dram("woutT", [NH * DH, H])
    b_kv = dram("b_kv", [128, KV // 128])
    b_qd = dram("b_qd", [128, QC // 128])
    b_qu = dram("b_qu", [128, NH * DH // 128])
    b_qr = dram("b_qr", [128, NH // 2])
    b_vu_row = dram("b_vu_row", [1, NH * DH])
    b_out = dram("b_out", [128, H // 128])
    rmat = dram("rmat", [128, (NH // 2) * DR])
    ones_col = dram("ones_col", [128, 1])
    ones_row = dram("ones_row", [1, 128])
    wg = dram("wg", [II, H])
    wu = dram("wu", [II, H])
    wd = dram("wd", [H, II])

    y_out = dram("y_out", [H, TOWN], F32, kind="ExternalOutput")

    scratch_kind = "ExternalOutput" if _DEBUG else "Internal"
    acc_d = dram("acc_d", [H, TOWN], F32, kind=scratch_kind)
    ctx_d = dram("ctx_d", [128, NH * TOWN], BF16, kind="Internal")
    xq_d = dram("xq_d", [TOWN, H], BF16, kind=scratch_kind)
    wgq_d = dram("wgq_d", [II, H], BF16, kind=scratch_kind)
    wuq_d = dram("wuq_d", [II, H], BF16, kind=scratch_kind)
    wdq_d = dram("wdq_d", [H, II], BF16, kind=scratch_kind)
    itq_d = dram("itq_d", [TOWN, II], BF16, kind=scratch_kind)
    if _DEBUG:
        dbg_kvc = dram("dbg_kvc", [128, 4 * S], BF16, kind="ExternalOutput")
        dbg_qc = dram("dbg_qc", [128, 8 * TOWN], BF16, kind="ExternalOutput")
        dbg_ctx = dram("dbg_ctx", [128, NH * TOWN], BF16, kind="ExternalOutput")
        dbg_kc = dram("dbg_kc", [128, S], BF16, kind="ExternalOutput")
        dbg_kr = dram("dbg_kr", [128, S], BF16, kind="ExternalOutput")
        dbg_v = dram("dbg_v", [128, 32 * 256], BF16, kind="ExternalOutput")
        dbg_qcs = dram("dbg_qcs", [128, TOWN], BF16, kind="ExternalOutput")
        dbg_qrot = dram("dbg_qrot", [128, TOWN], BF16, kind="ExternalOutput")
        dbg_bvu = dram("dbg_bvu", [1, NH * DH], BF16, kind="ExternalOutput")
        dbg_oner = dram("dbg_oner", [1, 128], BF16, kind="ExternalOutput")
        dbg_wvu = dram("dbg_wvu", [128, 4 * 256], BF16, kind="ExternalOutput")
        dbg_wqu = dram("dbg_wqu", [128, 8 * 256], BF16, kind="ExternalOutput")
        dbg_wgqT = dram("dbg_wgqT", [128, 16 * 512], BF16, kind="ExternalOutput")
        dbg_sg = dram("dbg_sg", [128, 512], F32, kind="ExternalOutput")
        dbg_it = dram("dbg_it", [128, 512], F32, kind="ExternalOutput")
        dbg_xqfm = dram("dbg_xqfm", [128, 16 * TOWN], BF16, kind="ExternalOutput")
        dbg_itq0 = dram("dbg_itq0", [128, 512], BF16, kind="ExternalOutput")
        dbg_ism = dram("dbg_ism", [128, 32], F32, kind="ExternalOutput")

    P = Plan()

    def sb(name, shape, dtype):
        return nc.alloc_sbuf_tensor(name, list(shape), dtype).ap()

    onec_bf = sb("onec_bf", [128, 1], BF16)
    oner_bf = sb("oner_bf", [1, 128], BF16)
    bkv_sb = sb("bkv_sb", [128, KV // 128], F32)
    bqd_sb = sb("bqd_sb", [128, QC // 128], F32)
    bqu_sb = sb("bqu_sb", [128, NH * DH // 128], F32)
    bqr_sb = sb("bqr_sb", [128, NH // 2], F32)
    bvu_bf = sb("bvu_bf", [1, NH * DH], BF16)
    bout_sb = sb("bout_sb", [128, H // 128], F32)
    rmat_sb = sb("rmat_sb", [128, (NH // 2) * DR], F32)
    colkv_bf = sb("colkv_bf", [1, KV], BF16)
    colqd_bf = sb("colqd_bf", [1, QC], BF16)
    srow = sb("srow", [1, 4 * 512], F32)
    eps_col = sb("eps_col", [128, 1], F32)
    onec_f = sb("onec_f", [128, 1], F32)
    oner_f = sb("oner_f", [1, 128], F32)
    rowbf = sb("rowbf", [1, 2 * 512], BF16)
    base_consts = nc.sbuf_base

    kv_c = sb("kv_c", [128, 4 * S], BF16)
    q_c = sb("q_c", [128, 8 * TOWN], BF16)
    base_persist = nc.sbuf_base

    ps = [nc.alloc_psum_tensor(f"ps{i}", [128, 512], F32).ap() for i in range(8)]

    wstage = sb("wstage", [128, 4096], F32)
    wkv_bf = sb("wkv_bf", [128, 16 * 512], BF16)
    wqd_bf = sb("wqd_bf", [128, 16 * 1024], BF16)
    xstage = sb("xstage", [128, 16 * 512], F32)
    xbf = sb("xbf", [128, 16 * 512], BF16)
    sqbf = sb("sqbf", [128, 16 * 512], BF16)
    bvu_f32 = wstage[0:1, 0:NH * DH]

    for dst, src in ((bkv_sb, b_kv), (bqd_sb, b_qd), (bqu_sb, b_qu),
                     (bqr_sb, b_qr), (bout_sb, b_out), (rmat_sb, rmat),
                     (bvu_f32, b_vu_row)):
        P.op("sync", lambda e, dst=dst, src=src: e.dma_start(out=dst, in_=src))
        P.mark("sync", "sLD", 16)
    P.op("sync", lambda e: e.dma_start(out=onec_f, in_=ones_col))
    P.mark("sync", "sLD", 16)
    P.op("sync", lambda e: e.dma_start(out=oner_f, in_=ones_row))
    m_consts = P.mark("sync", "sLD", 16)
    P.op("vector", lambda v: v.tensor_copy(out=onec_bf, in_=onec_f),
         [("sLD", m_consts)])
    P.op("vector", lambda v: v.tensor_copy(out=oner_bf, in_=oner_f))
    P.op("vector", lambda v: v.tensor_copy(out=bvu_bf, in_=bvu_f32))
    P.op("vector", lambda v: v.memset(eps_col, EPS))
    m_c0 = P.mark("vector", "sDV")

    for hlf in range(2):
        P.op("sync", lambda e, hlf=hlf: e.dma_start(
            out=wstage[:, 0:8 * 512].rearrange("p (c n) -> p c n", c=8),
            in_=wkvT[hlf * 1024:(hlf + 1) * 1024, :].rearrange("(c p) n -> p c n", p=128)),
            [("sDV", m_c0 if hlf == 0 else P.now("sDV"))])
        m_wkv = P.mark("sync", "sLD", 16)
        P.op("vector", lambda v, hlf=hlf: v.tensor_copy(
            out=wkv_bf[:, hlf * 8 * 512:(hlf + 1) * 8 * 512], in_=wstage[:, 0:8 * 512]),
            [("sLD", m_wkv)])
        P.mark("vector", "sDV")
    m_wkvc = P.now("sDV")
    for c in range(16):
        P.op("tensor", lambda e, c=c: e.matmul(
            ps[0][0:1, 0:KV], onec_bf, wkv_bf[:, c * KV:(c + 1) * KV],
            start=(c == 0), stop=(c == 15)),
            [("sDV", m_wkvc)] if c == 0 else [])
    m_pe = P.mark("tensor", "sPE")
    P.op("vector", lambda v: v.tensor_copy(out=colkv_bf, in_=ps[0][0:1, 0:KV]),
         [("sPE", m_pe)])
    m_colkv = P.mark("vector", "sDV")
    for qtr in range(4):
        P.op("sync", lambda e, qtr=qtr: e.dma_start(
            out=wstage[:, 0:4 * 1024].rearrange("p (c n) -> p c n", c=4),
            in_=wqdT[qtr * 512:(qtr + 1) * 512, :].rearrange("(c p) n -> p c n", p=128)),
            [("sDV", m_colkv if qtr == 0 else P.now("sDV"))])
        mq = P.mark("sync", "sLD", 16)
        P.op("vector", lambda v, qtr=qtr: v.tensor_copy(
            out=wqd_bf[:, qtr * 4 * QC:(qtr + 1) * 4 * QC], in_=wstage[:, 0:4 * 1024]),
            [("sLD", mq)])
        P.mark("vector", "sDV")
    m_wqdc = P.now("sDV")
    for half in range(2):
        for c in range(16):
            P.op("tensor", lambda e, c=c, half=half: e.matmul(
                ps[half][0:1, :], onec_bf,
                wqd_bf[:, c * QC + half * 512: c * QC + (half + 1) * 512],
                start=(c == 0), stop=(c == 15)),
                [("sDV", m_wqdc)] if (c == 0 and half == 0) else [])
    m_pe = P.mark("tensor", "sPE")
    P.op("vector", lambda v: v.tensor_copy(out=colqd_bf[0:1, 0:512], in_=ps[0][0:1, :]),
         [("sPE", m_pe)])
    P.op("vector", lambda v: v.tensor_copy(out=colqd_bf[0:1, 512:1024], in_=ps[1][0:1, :]))
    m_colqd = P.mark("vector", "sDV")

    r_m = srow[0:1, 0:512]
    r_ss = srow[0:1, 512:1024]
    r_t = srow[0:1, 1024:1536]
    rstd_bf = rowbf[0:1, 0:512]
    mrs_bf = rowbf[0:1, 512:1024]
    for tt in range(8):
        own = tt < 2
        P.op("sync", lambda e, tt=tt: e.dma_start(
            out=xstage.rearrange("p (c n) -> p c n", c=16),
            in_=x_fm[:, tt * 512:(tt + 1) * 512].rearrange("(c p) n -> p c n", p=128)),
            [("sPE", P.now("sPE")), ("sAC", P.now("sAC")), ("sDV", P.now("sDV"))])
        m_x = P.mark("sync", "sLD", 16)
        P.op("vector", lambda v: v.tensor_copy(out=xbf, in_=xstage), [("sLD", m_x)])
        m_xbf = P.mark("vector", "sDV")
        P.op("scalar", lambda s: s.activation(sqbf, xstage, AF.Square), [("sLD", m_x)])
        m_sq = P.mark("scalar", "sAC")
        for c in range(16):
            P.op("tensor", lambda e, c=c: e.matmul(
                ps[1][0:1, :], onec_bf, xbf[:, c * 512:(c + 1) * 512],
                start=(c == 0), stop=(c == 15)),
                ([("sDV", m_xbf)] + P.guard(1)) if c == 0 else [])
        for c in range(16):
            P.op("tensor", lambda e, c=c: e.matmul(
                ps[2][0:1, :], onec_bf, sqbf[:, c * 512:(c + 1) * 512],
                start=(c == 0), stop=(c == 15)),
                ([("sAC", m_sq)] + P.guard(2)) if c == 0 else [])
        m_stats = P.mark("tensor", "sPE")
        P.op("vector", lambda v: v.tensor_scalar_mul(r_m, ps[1][0:1, :], 1.0 / H),
             [("sPE", m_stats)])
        P.op("vector", lambda v: v.tensor_scalar_mul(r_ss, ps[2][0:1, :], 1.0 / H))
        m_statrd = P.mark("vector", "sDV")
        P.set_guard(1, [("sDV", m_statrd)])
        P.set_guard(2, [("sDV", m_statrd)])
        P.op("vector", lambda v: v.scalar_tensor_tensor(r_t, r_m, 1.0, r_m, ALU.mult, ALU.mult))
        m_h0 = P.mark("vector", "sDV")
        P.op("vector", lambda v: v.tensor_tensor(r_ss, r_ss, r_t, ALU.subtract),
            [("sDV", m_h0)])
        m_var = P.mark("vector", "sDV")
        P.op("scalar", lambda s: s.activation(r_ss, r_ss, AF.Sqrt, bias=eps_col[0:1, :]),
             [("sDV", m_var)])
        m_sd = P.mark("scalar", "sAC")
        P.op("vector", lambda v: v.reciprocal(r_ss, r_ss), [("sAC", m_sd)])
        m_h1 = P.mark("vector", "sDV")
        P.op("vector", lambda v: v.tensor_copy(out=rstd_bf, in_=r_ss), [("sDV", m_h1)])
        P.op("vector", lambda v: v.scalar_tensor_tensor(r_m, r_m, -1.0, r_ss, ALU.mult, ALU.mult))
        m_h2 = P.mark("vector", "sDV")
        P.op("vector", lambda v: v.tensor_copy(out=mrs_bf, in_=r_m), [("sDV", m_h2)])
        m_rows = P.mark("vector", "sDV")
        P.op("tensor", lambda e: e.matmul(ps[3], oner_bf, rstd_bf, start=True, stop=True),
             [("sDV", m_rows)] + P.guard(3))
        m_bc = P.mark("tensor", "sPE")
        dst = xbf
        dst3 = dst.rearrange("p (c n) -> p c n", c=16)
        P.op("vector", lambda v, dst3=dst3: v.tensor_tensor(
            dst3, xstage.rearrange("p (c n) -> p c n", c=16),
            ps[3][:, None, :].broadcast_to([128, 16, 512]), ALU.mult),
            [("sPE", m_bc)])
        m_xnr = P.mark("vector", "sDV")
        P.set_guard(3, [("sDV", m_xnr)])
        for ot in range(4):
            bank = 4 + (ot % 2)
            for c in range(16):
                P.op("tensor", lambda e, ot=ot, c=c, dst=dst, bank=bank: e.matmul(
                    ps[bank], wkv_bf[:, c * KV + ot * 128: c * KV + ot * 128 + 128],
                    dst[:, c * 512:(c + 1) * 512], start=(c == 0), stop=False),
                    ([("sDV", m_xnr)] + P.guard(bank)) if c == 0 else [])
            P.op("tensor", lambda e, ot=ot, bank=bank: e.matmul(
                ps[bank], colkv_bf[0:1, ot * 128:(ot + 1) * 128], mrs_bf,
                start=False, stop=True))
            m_kv = P.mark("tensor", "sPE")
            P.op("scalar", lambda s, ot=ot, tt=tt, bank=bank: s.activation(
                kv_c[:, ot * S + tt * 512: ot * S + (tt + 1) * 512], ps[bank],
                AF.Identity, bias=bkv_sb[:, ot:ot + 1]), [("sPE", m_kv)])
            m_ev = P.mark("scalar", "sAC")
            P.set_guard(bank, [("sAC", m_ev)])
        if own:
            for ot in range(8):
                bank = 6 + (ot % 2)
                for c in range(16):
                    P.op("tensor", lambda e, ot=ot, c=c, dst=dst, bank=bank: e.matmul(
                        ps[bank], wqd_bf[:, c * QC + ot * 128: c * QC + ot * 128 + 128],
                        dst[:, c * 512:(c + 1) * 512], start=(c == 0), stop=False),
                        P.guard(bank) if c == 0 else [])
                P.op("tensor", lambda e, ot=ot, bank=bank: e.matmul(
                    ps[bank], colqd_bf[0:1, ot * 128:(ot + 1) * 128], mrs_bf,
                    start=False, stop=True))
                m_qc = P.mark("tensor", "sPE")
                P.op("scalar", lambda s, ot=ot, tt=tt, bank=bank: s.activation(
                    q_c[:, ot * TOWN + tt * 512: ot * TOWN + (tt + 1) * 512], ps[bank],
                    AF.Identity, bias=bqd_sb[:, ot:ot + 1]), [("sPE", m_qc)])
                m_ev = P.mark("scalar", "sAC")
                P.set_guard(bank, [("sAC", m_ev)])

    if _DEBUG:
        P.op("gpsimd", lambda e: e.dma_start(out=dbg_kvc, in_=kv_c),
             [("sAC", P.now("sAC"))])
        P.mark("gpsimd", "sST", 16)
        P.op("gpsimd", lambda e: e.dma_start(out=dbg_qc, in_=q_c))
        P.mark("gpsimd", "sST", 16)

    nc.sbuf_base = base_persist
    ctx_pair = sb("ctx_pair", [128, 2 * TOWN], BF16)
    mkv_bf = sb("mkv_bf", [128, 4], BF16)
    mkv_f = sb("mkv_f", [128, 4], F32)
    wstp = sb("wstp", [128, 4 * 256], F32)
    wku_bf = sb("wku_bf", [128, 4 * 256], BF16)
    wvu_bf = sb("wvu_bf", [128, 4 * 256], BF16)
    wkr_bf = sb("wkr_bf", [128, 4 * 128], BF16)
    wqu_bf = sb("wqu_bf", [128, 8 * 256], BF16)
    wqr_bf = sb("wqr_bf", [128, 8 * 128], BF16)
    kc0 = sb("kc0", [128, S], BF16)
    kc1 = sb("kc1", [128, S], BF16)
    krf32 = sb("krf32", [128, S], F32)
    krot = sb("krot", [128, S], BF16)
    vpair = sb("vpair", [128, 32 * 256], BF16)
    qcs = [sb(f"qcs{i}", [128, TOWN], BF16) for i in range(2)]
    qrf32 = sb("qrf32", [128, TOWN], F32)
    qrot = sb("qrot", [128, TOWN], BF16)
    pT = [sb(f"pT{i}", [128, 512], BF16) for i in range(2)]
    linv_sb = sb("linv_sb", [128, 512], F32)
    biasK = sb("biasK", [128, 4], F32)
    qwst = sb("qwst", [128, 2048], F32)
    qaz = sb("qaz", [128, 2048], F32)
    qb = [sb(f"qb{i}", [128, 2048], BF16) for i in range(3)]
    qsm = [sb(f"qsm{i}", [128, 256], F32) for i in range(4)]
    qsmb = sb("qsmb", [128, 256], BF16)

    P.op("vector", lambda v: v.tensor_reduce(
        mkv_f, kv_c.rearrange("p (c n) -> p c n", c=4), AX.X, ALU.add),
        [("sAC", P.now("sAC"))])
    m_red = P.mark("vector", "sDV")
    P.op("vector", lambda v: v.tensor_scalar(mkv_bf, mkv_f, 1.0 / S, None, ALU.mult),
         [("sDV", m_red)])
    m_mkv = P.mark("vector", "sDV")

    qchunks = []
    for seg_ in range(16):
        for wsrc_, wqd_ in ((wg, wgq_d), (wu, wuq_d)):
            for sub_ in range(4):
                r0 = seg_ * 512 + sub_ * 128
                qchunks.append((wsrc_[r0:r0 + 128, :], wqd_[r0:r0 + 128, :]))
    for ch_ in range(16):
        for sub_ in range(4):
            qchunks.append((wd[ch_ * 128:(ch_ + 1) * 128, sub_ * 2048:(sub_ + 1) * 2048],
                            wdq_d[ch_ * 128:(ch_ + 1) * 128, sub_ * 2048:(sub_ + 1) * 2048]))
    pump_store_mark = [0] * len(qchunks)
    pstate = {"step": 0, "m_qwst_free": [], "m_sc": 0, "m_scr": 0,
              "m_b2st": 0, "first": True}

    def pump_step():
        st = pstate["step"]
        g, ph = st // 5, st % 5
        if g * 4 >= len(qchunks):
            return False
        if ph == 0:
            for sub in range(4):
                ci = g * 4 + sub
                w_ld = list(pstate["m_qwst_free"])
                if pstate["first"]:
                    w_ld += [("sDV", P.now("sDV")), ("sAC", P.now("sAC")),
                             ("sPE", P.now("sPE")), ("sST", P.now("sST"))]
                    pstate["first"] = False
                P.op("sync", lambda e, src=qchunks[ci][0]: e.dma_start(
                    out=qwst, in_=src), w_ld)
                m_l = P.mark("sync", "sLD", 16)
                P.op("vector", lambda v, sub=sub: v.tensor_reduce(
                    qsm[0][:, sub * 64:(sub + 1) * 64],
                    qwst[:].rearrange("p (b c) -> p b c", c=32),
                    AX.X, ALU.max, apply_absolute_value=True),
                    [("sLD", m_l), ("sDV", P.now("sDV")),
                     ("sAC", getattr(P, "scale_ln_mark", 0))])
                m_r = P.mark("vector", "sDV")
                pstate["m_qwst_free"] = [("sDV", m_r)]
            pstate["m_sc"] = emit_scale_batch(
                P, [], qsm[0][:, 0:256], qsm[1][:, 0:256], qsm[2][:, 0:256],
                qsm[3][:, 0:256], qsmb[:, 0:256], half=True)
        else:
            sub = ph - 1
            ci = g * 4 + sub
            src, dst = qchunks[ci]
            P.op("sync", lambda e, src=src: e.dma_start(out=qwst, in_=src),
                 list(pstate["m_qwst_free"]))
            m_l = P.mark("sync", "sLD", 16)
            m_z, m_as, m_fin = emit_quant_w(
                P, qwst[:], qb[1][:],
                qsm[2][:, sub * 64:(sub + 1) * 64],
                qsmb[:, sub * 64:(sub + 1) * 64],
                qaz, qb[0], qb[1], qb[2], pstate["m_sc"],
                [("sLD", m_l)], m_scr_act=pstate["m_scr"],
                wt_b2=[("sST", pstate["m_b2st"])])
            pstate["m_scr"] = m_as
            pstate["m_qwst_free"] = [("sDV", m_z)]
            P.op("gpsimd", lambda e, dst=dst: e.dma_start(out=dst, in_=qb[1]),
                 [("sDV", m_fin)])
            m_st = P.mark("gpsimd", "sST", 16)
            pstate["m_b2st"] = m_st
            pump_store_mark[ci] = m_st
        pstate["step"] = st + 1
        return True

    kcs = (kc0, kc1)
    for pr in range(NH // 2):
        h0 = 2 * pr
        loads = (
            (wkuT[:, h0 * DH:(h0 + 2) * DH], wku_bf, 4, 256),
            (wvuT[:, h0 * DH:(h0 + 2) * DH], wvu_bf, 4, 256),
            (wkrT[:, h0 * DR:(h0 + 2) * DR], wkr_bf, 4, 128),
            (wquT[:, h0 * DH:(h0 + 2) * DH], wqu_bf, 8, 256),
            (wqrT[:, h0 * DR:(h0 + 2) * DR], wqr_bf, 8, 128),
        )
        for src, dstb, nch, w in loads:
            per = max(1, 1024 // w)
            for r0 in range(0, nch, per):
                n2 = min(per, nch - r0)
                P.op("sync", lambda e, src=src, r0=r0, n2=n2, w=w: e.dma_start(
                    out=wstp[:, 0:n2 * w].rearrange("p (c n) -> p c n", c=n2),
                    in_=src[r0 * 128:(r0 + n2) * 128, :].rearrange(
                        "(c p) n -> p c n", p=128)),
                    [("sDV", P.now("sDV")), ("sPE", P.now("sPE")),
                     ("sST", P.now("sST"))])
                m_l = P.mark("sync", "sLD", 16)
                P.op("vector", lambda v, dstb=dstb, r0=r0, n2=n2, w=w: v.tensor_copy(
                    out=dstb[:, r0 * w:(r0 + n2) * w], in_=wstp[:, 0:n2 * w]),
                    [("sLD", m_l)])
                P.mark("vector", "sDV")
        m_wld = P.now("sDV")

        for hh in range(2):
            for c in range(4):
                P.op("tensor", lambda e, hh=hh, c=c: e.matmul(
                    ps[0][:, hh:hh + 1],
                    wku_bf[:, c * 256 + hh * 128: c * 256 + hh * 128 + 128],
                    mkv_bf[:, c:c + 1], start=(c == 0), stop=(c == 3)),
                    ([("sDV", m_wld), ("sDV", m_mkv)] + P.guard(0)) if (hh == 0 and c == 0) else [])
        for c in range(4):
            P.op("tensor", lambda e, c=c: e.matmul(
                ps[0][:, 2:3], wkr_bf[:, c * 128:(c + 1) * 128],
                mkv_bf[:, c:c + 1], start=(c == 0), stop=(c == 3)))
        m_kbar = P.mark("tensor", "sPE")
        P.op("vector", lambda v: v.tensor_scalar_mul(biasK[:, 0:3], ps[0][:, 0:3], -1.0),
             [("sPE", m_kbar), ("sAC", P.now("sAC"))])
        m_biasK = P.mark("vector", "sDV")
        P.set_guard(0, [("sDV", m_biasK)])

        for hh in range(2):
            for k8 in range(8):
                bank = 1 + (k8 % 2)
                for c in range(4):
                    P.op("tensor", lambda e, hh=hh, k8=k8, c=c, bank=bank: e.matmul(
                        ps[bank],
                        wku_bf[:, c * 256 + hh * 128: c * 256 + hh * 128 + 128],
                        kv_c[:, c * S + k8 * 512: c * S + (k8 + 1) * 512],
                        start=(c == 0), stop=(c == 3)),
                        ([("sDV", m_biasK)] + P.guard(bank)) if c == 0 else [])
                m_k = P.mark("tensor", "sPE")
                P.op("scalar", lambda s, hh=hh, k8=k8, bank=bank: s.activation(
                    kcs[hh][:, k8 * 512:(k8 + 1) * 512], ps[bank], AF.Identity,
                    bias=biasK[:, hh:hh + 1]), [("sPE", m_k)])
                m_e = P.mark("scalar", "sAC")
                P.set_guard(bank, [("sAC", m_e)])
        for k8 in range(8):
            bank = 1 + (k8 % 2)
            for c in range(4):
                P.op("tensor", lambda e, k8=k8, c=c, bank=bank: e.matmul(
                    ps[bank], wkr_bf[:, c * 128:(c + 1) * 128],
                    kv_c[:, c * S + k8 * 512: c * S + (k8 + 1) * 512],
                    start=(c == 0), stop=(c == 3)), P.guard(bank) if c == 0 else [])
            m_k = P.mark("tensor", "sPE")
            P.op("scalar", lambda s, k8=k8, bank=bank: s.activation(
                krf32[:, k8 * 512:(k8 + 1) * 512], ps[bank], AF.Identity,
                bias=biasK[:, 2:3]), [("sPE", m_k)])
            m_e = P.mark("scalar", "sAC")
            P.set_guard(bank, [("sAC", m_e)])
        m_krf = P.now("sAC")
        for hh in range(2):
            for k8 in range(8):
                bank = 3 + (k8 % 2)
                P.op("tensor", lambda e, hh=hh, k8=k8, bank=bank, pr=pr: e.matmul(
                    ps[bank][0:DR, :],
                    rmat_sb[hh * DR:(hh + 1) * DR, pr * DR:(pr + 1) * DR],
                    krf32[hh * DR:(hh + 1) * DR, k8 * 512:(k8 + 1) * 512],
                    start=True, stop=True),
                    ([("sAC", m_krf)] + P.guard(bank)) if k8 in (0, 1) else P.guard(bank))
                m_r = P.mark("tensor", "sPE")
                P.op("scalar", lambda s, hh=hh, k8=k8, bank=bank: s.activation(
                    krot[hh * DR:(hh + 1) * DR, k8 * 512:(k8 + 1) * 512],
                    ps[bank][0:DR, :], AF.Copy), [("sPE", m_r)])
                m_e = P.mark("scalar", "sAC")
                P.set_guard(bank, [("sAC", m_e)])
        for kt in range(32):
            bank = 5 + (kt % 2)
            for c in range(4):
                P.op("tensor", lambda e, kt=kt, c=c, bank=bank: e.matmul(
                    ps[bank][:, 0:256], kv_c[:, c * S + kt * 128: c * S + kt * 128 + 128],
                    wvu_bf[:, c * 256:(c + 1) * 256], start=(c == 0), stop=False),
                    P.guard(bank) if c == 0 else [])
            P.op("tensor", lambda e, kt=kt, bank=bank, h0=h0: e.matmul(
                ps[bank][:, 0:256], oner_bf, bvu_bf[0:1, h0 * DH: h0 * DH + 256],
                start=False, stop=True))
            m_v = P.mark("tensor", "sPE")
            P.op("scalar", lambda s, kt=kt, bank=bank: s.activation(
                vpair[:, kt * 256:(kt + 1) * 256], ps[bank][:, 0:256], AF.Copy),
                [("sPE", m_v)])
            m_e = P.mark("scalar", "sAC")
            P.set_guard(bank, [("sAC", m_e)])
        for hh in range(2):
            for qh in range(2):
                bank = 1 + (qh % 2)
                for c in range(8):
                    P.op("tensor", lambda e, hh=hh, qh=qh, c=c, bank=bank: e.matmul(
                        ps[bank],
                        wqu_bf[:, c * 256 + hh * 128: c * 256 + hh * 128 + 128],
                        q_c[:, c * TOWN + qh * 512: c * TOWN + (qh + 1) * 512],
                        start=(c == 0), stop=(c == 7)), P.guard(bank) if c == 0 else [])
                m_q = P.mark("tensor", "sPE")
                P.op("scalar", lambda s, hh=hh, qh=qh, bank=bank, h0=h0: s.activation(
                    qcs[hh][:, qh * 512:(qh + 1) * 512], ps[bank], AF.Identity,
                    bias=bqu_sb[:, h0 + hh: h0 + hh + 1]), [("sPE", m_q)])
                m_e = P.mark("scalar", "sAC")
                P.set_guard(bank, [("sAC", m_e)])
        for qh in range(2):
            bank = 1 + (qh % 2)
            for c in range(8):
                P.op("tensor", lambda e, qh=qh, c=c, bank=bank: e.matmul(
                    ps[bank], wqr_bf[:, c * 128:(c + 1) * 128],
                    q_c[:, c * TOWN + qh * 512: c * TOWN + (qh + 1) * 512],
                    start=(c == 0), stop=(c == 7)), P.guard(bank) if c == 0 else [])
            m_q = P.mark("tensor", "sPE")
            P.op("scalar", lambda s, qh=qh, bank=bank, pr=pr: s.activation(
                qrf32[:, qh * 512:(qh + 1) * 512], ps[bank], AF.Identity,
                bias=bqr_sb[:, pr:pr + 1]), [("sPE", m_q)])
            m_e = P.mark("scalar", "sAC")
            P.set_guard(bank, [("sAC", m_e)])
        m_qrf = P.now("sAC")
        for hh in range(2):
            for qh in range(2):
                bank = 3 + (qh % 2)
                P.op("tensor", lambda e, hh=hh, qh=qh, bank=bank, pr=pr: e.matmul(
                    ps[bank][0:DR, :],
                    rmat_sb[hh * DR:(hh + 1) * DR, pr * DR:(pr + 1) * DR],
                    qrf32[hh * DR:(hh + 1) * DR, qh * 512:(qh + 1) * 512],
                    start=True, stop=True), [("sAC", m_qrf)] + P.guard(bank))
                m_r = P.mark("tensor", "sPE")
                P.op("scalar", lambda s, hh=hh, qh=qh, bank=bank: s.activation(
                    qrot[hh * DR:(hh + 1) * DR, qh * 512:(qh + 1) * 512],
                    ps[bank][0:DR, :], AF.Copy), [("sPE", m_r)])
                m_e = P.mark("scalar", "sAC")
                P.set_guard(bank, [("sAC", m_e)])
        m_qrot = P.now("sAC")

        if _DEBUG and pr == 0:
            P.op("gpsimd", lambda e: e.dma_start(out=dbg_kc, in_=kc0),
                 [("sAC", m_qrot)])
            P.mark("gpsimd", "sST", 16)
            P.op("gpsimd", lambda e: e.dma_start(out=dbg_kr, in_=krot))
            P.mark("gpsimd", "sST", 16)
            P.op("gpsimd", lambda e: e.dma_start(out=dbg_v, in_=vpair))
            P.mark("gpsimd", "sST", 16)
            P.op("gpsimd", lambda e: e.dma_start(out=dbg_qcs, in_=qcs[0]))
            P.mark("gpsimd", "sST", 16)
            P.op("gpsimd", lambda e: e.dma_start(out=dbg_qrot, in_=qrot))
            P.mark("gpsimd", "sST", 16)
            P.op("gpsimd", lambda e: e.dma_start(out=dbg_bvu, in_=bvu_bf))
            P.mark("gpsimd", "sST", 16)
            P.op("gpsimd", lambda e: e.dma_start(out=dbg_oner, in_=oner_bf))
            P.mark("gpsimd", "sST", 16)
            P.op("gpsimd", lambda e: e.dma_start(out=dbg_wvu, in_=wvu_bf))
            P.mark("gpsimd", "sST", 16)
            P.op("gpsimd", lambda e: e.dma_start(out=dbg_wqu, in_=wqu_bf))
            m_dbg = P.mark("gpsimd", "sST", 16)

        for hh in range(2):
            for qh in range(2):
                for kt in range(32):
                    bank = 1 + (kt % 2)
                    P.op("tensor", lambda e, hh=hh, qh=qh, kt=kt, bank=bank: e.matmul(
                        ps[bank], kcs[hh][:, kt * 128:(kt + 1) * 128],
                        qcs[hh][:, qh * 512:(qh + 1) * 512], start=True, stop=False),
                        ([("sAC", m_qrot)] if (hh == 0 and qh == 0 and kt == 0) else []) + P.guard(bank))
                    P.op("tensor", lambda e, hh=hh, qh=qh, kt=kt, bank=bank: e.matmul(
                        ps[bank], krot[hh * DR:(hh + 1) * DR, kt * 128:(kt + 1) * 128],
                        qrot[hh * DR:(hh + 1) * DR, qh * 512:(qh + 1) * 512],
                        start=False, stop=True))
                    m_s = P.mark("tensor", "sPE")
                    P.op("scalar", lambda s, kt=kt, bank=bank: s.activation(
                        pT[kt % 2], ps[bank], AF.Exp, scale=ALPHA),
                        [("sPE", m_s)] + P.guard(bank + 10))
                    m_p = P.mark("scalar", "sAC")
                    P.set_guard(bank, [("sAC", m_p)])
                    P.op("tensor", lambda e, hh=hh, kt=kt: e.matmul(
                        ps[5], vpair[:, kt * 256 + hh * 128: kt * 256 + hh * 128 + 128],
                        pT[kt % 2], start=(kt == 0), stop=(kt == 31)),
                        [("sAC", m_p)] + (P.guard(5) if kt == 0 else []))
                    P.op("tensor", lambda e, kt=kt: e.matmul(
                        ps[6][0:1, :], onec_bf, pT[kt % 2],
                        start=(kt == 0), stop=(kt == 31)),
                        P.guard(6) if kt == 0 else [])
                    m_ctxmm = P.mark("tensor", "sPE")
                    P.set_guard(bank + 10, [("sPE", m_ctxmm)])
                    if kt % 4 == 3:
                        pump_step()
                m_l = P.now("sPE")
                P.op("vector", lambda v: v.reciprocal(srow[0:1, 1536:2048], ps[6][0:1, :]),
                     [("sPE", m_l)])
                m_h3 = P.mark("vector", "sDV")
                P.op("vector", lambda v: v.tensor_copy(out=rowbf[0:1, 0:512],
                                                       in_=srow[0:1, 1536:2048]),
                     [("sDV", m_h3)])
                m_linv = P.mark("vector", "sDV")
                P.set_guard(6, [("sDV", m_linv)])
                P.op("tensor", lambda e: e.matmul(ps[7], oner_bf, rowbf[0:1, 0:512],
                                                  start=True, stop=True),
                     [("sDV", m_linv)] + P.guard(7))
                m_bc = P.mark("tensor", "sPE")
                P.op("scalar", lambda s: s.activation(linv_sb, ps[7], AF.Copy),
                     [("sPE", m_bc)])
                m_lsb = P.mark("scalar", "sAC")
                P.set_guard(7, [("sAC", m_lsb)])
                P.op("vector", lambda v, hh=hh, qh=qh: v.tensor_tensor(
                    ctx_pair[:, hh * TOWN + qh * 512: hh * TOWN + (qh + 1) * 512],
                    ps[5], linv_sb, ALU.mult),
                    [("sAC", m_lsb)] + P.guard(22))
                m_cv = P.mark("vector", "sDV")
                P.set_guard(5, [("sDV", m_cv)])
                P.set_guard(21, [("sDV", m_cv)])
        P.op("gpsimd", lambda e, h0=h0: e.dma_start(
            out=ctx_d[:, h0 * TOWN:(h0 + 2) * TOWN], in_=ctx_pair),
            [("sDV", m_cv)])
        m_ctx_st = P.mark("gpsimd", "sST", 16)
        P.set_guard(22, [("sST", m_ctx_st)])

    while pump_step():
        pass

    m_attn_done = P.now("sDV")
    nc.sbuf_base = base_persist
    ctx_full = sb("ctx_full", [128, NH * TOWN], BF16)
    wo_st = sb("wo_st", [128, 16 * 128], F32)
    wo_bf = sb("wo_bf", [128, 16 * 128], BF16)
    ev_o = sb("ev_o", [128, 512], F32)
    xr_o = sb("xr_o", [128, 1024], F32)
    P.op("sync", lambda e: e.dma_start(out=ctx_full, in_=ctx_d),
         [("sDV", m_attn_done), ("sAC", P.now("sAC")), ("sPE", P.now("sPE")),
          ("sST", P.now("sST"))])
    m_ctxf = P.mark("sync", "sLD", 16)
    for odt in range(16):
        P.op("sync", lambda e, odt=odt: e.dma_start(
            out=wo_st.rearrange("p (c n) -> p c n", c=16),
            in_=woutT[:, odt * 128:(odt + 1) * 128].rearrange("(c p) n -> p c n", p=128)),
            [("sDV", P.now("sDV")), ("sPE", P.now("sPE"))])
        m_l = P.mark("sync", "sLD", 16)
        P.op("vector", lambda v: v.tensor_copy(out=wo_bf, in_=wo_st), [("sLD", m_l)])
        m_w = P.mark("vector", "sDV")
        for qh in range(2):
            bank = 1 + qh
            for c in range(16):
                P.op("tensor", lambda e, odt=odt, qh=qh, c=c, bank=bank: e.matmul(
                    ps[bank], wo_bf[:, c * 128:(c + 1) * 128],
                    ctx_full[:, c * TOWN + qh * 512: c * TOWN + (qh + 1) * 512],
                    start=(c == 0), stop=(c == 15)),
                    ([("sDV", m_w), ("sLD", m_ctxf)] + P.guard(bank)) if c == 0 else [])
            m_o = P.mark("tensor", "sPE")
            P.op("scalar", lambda s, odt=odt, bank=bank: s.activation(
                ev_o, ps[bank], AF.Identity, bias=bout_sb[:, odt:odt + 1]),
                [("sPE", m_o)] + P.guard(20) + P.guard(21))
            m_oe = P.mark("scalar", "sAC")
            P.set_guard(bank, [("sAC", m_oe)])
            P.op("sync", lambda e, odt=odt, qh=qh: e.dma_start(
                out=xr_o[:, 0:512],
                in_=x_fm[odt * 128:(odt + 1) * 128, qh * 512:(qh + 1) * 512]),
                [("sDV", P.now("sDV"))])
            m_xr = P.mark("sync", "sLD", 16)
            P.op("vector", lambda v: v.tensor_tensor(
                xr_o[:, 512:1024], ev_o, xr_o[:, 0:512], ALU.add),
                [("sAC", m_oe), ("sLD", m_xr), ("sST", P.now("sST"))])
            m_acc = P.mark("vector", "sDV")
            P.set_guard(20, [("sDV", m_acc)])
            P.op("gpsimd", lambda e, odt=odt, qh=qh: e.dma_start(
                out=acc_d[odt * 128:(odt + 1) * 128, qh * 512:(qh + 1) * 512],
                in_=xr_o[:, 512:1024]), [("sDV", m_acc)])
            P.mark("gpsimd", "sST", 16)
    m_acc_all = P.now("sST")

    nc.sbuf_base = base_consts
    xq_fm = sb("xq_fm", [128, 16 * TOWN], BF16)
    scr = [sb(f"scr{i}", [128, 2048], F32) for i in range(2)]
    scrb = [sb(f"scrb{i}", [128, 2048], BF16) for i in range(4)]
    smq = [sb(f"smq{i}", [128, 256], F32) for i in range(4)]
    smb = sb("smb", [128, 256], BF16)
    cols = sb("cols", [128, 4], F32)
    base_fbranch = nc.sbuf_base
    xt = sb("xt", [128, H], F32)
    sqf = sb("sqf", [128, H], F32)
    xqt = sb("xqt", [128, H], BF16)
    nc.sbuf_base = base_fbranch
    wgqT = sb("wgqT", [128, 16 * 512], BF16)
    wuqT = sb("wuqT", [128, 16 * 512], BF16)
    sg_f = sb("sg_f", [128, 512], F32)
    _ism_pack = sb("_ism_pack", [128, 32], F32)
    it_f = [sb(f"it_f{i}", [128, 512], F32) for i in range(2)]
    itq = [sb(f"itq{i}", [128, 512], BF16) for i in range(2)]

    m_scr_act = 0
    for tt in range(8):
        P.op("sync", lambda e, tt=tt: e.dma_start(
            out=xt, in_=x_tm[tt * 128:(tt + 1) * 128, :]),
            [("sDV", P.now("sDV")), ("sAC", P.now("sAC")), ("sST", P.now("sST"))])
        m_x = P.mark("sync", "sLD", 16)
        P.op("vector", lambda v: v.tensor_reduce(cols[:, 0:1], xt, AX.X, ALU.add),
             [("sLD", m_x)])
        m_red = P.mark("vector", "sDV")
        P.op("vector", lambda v: v.tensor_scalar_mul(cols[:, 0:1], cols[:, 0:1], 1.0 / H),
             [("sDV", m_red)])
        m_m = P.mark("vector", "sDV")
        P.op("scalar", lambda s: s.activation(sqf, xt, AF.Square), [("sLD", m_x)])
        m_sq = P.mark("scalar", "sAC")
        P.op("vector", lambda v: v.tensor_reduce(cols[:, 1:2], sqf, AX.X, ALU.add),
             [("sAC", m_sq)])
        m_red2 = P.mark("vector", "sDV")
        P.op("vector", lambda v: v.tensor_scalar_mul(cols[:, 1:2], cols[:, 1:2], 1.0 / H),
             [("sDV", m_red2)])
        m_g1 = P.mark("vector", "sDV")
        P.op("vector", lambda v: v.scalar_tensor_tensor(
            cols[:, 2:3], cols[:, 0:1], 1.0, cols[:, 0:1], ALU.mult, ALU.mult),
            [("sDV", m_g1)])
        m_g2 = P.mark("vector", "sDV")
        P.op("vector", lambda v: v.tensor_tensor(cols[:, 1:2], cols[:, 1:2],
                                                 cols[:, 2:3], ALU.subtract),
            [("sDV", m_g2)])
        m_var = P.mark("vector", "sDV")
        P.op("scalar", lambda s: s.activation(cols[:, 1:2], cols[:, 1:2], AF.Sqrt,
                                              bias=eps_col), [("sDV", m_var)])
        m_sd = P.mark("scalar", "sAC")
        P.op("vector", lambda v: v.reciprocal(cols[:, 1:2], cols[:, 1:2]),
             [("sAC", m_sd)])
        m_g3 = P.mark("vector", "sDV")
        P.op("vector", lambda v: v.scalar_tensor_tensor(
            cols[:, 3:4], cols[:, 0:1], -1.0, cols[:, 1:2], ALU.mult, ALU.mult),
            [("sDV", m_g3)])
        m_xn0 = P.mark("vector", "sDV")
        P.op("scalar", lambda s: s.activation(sqf, xt, AF.Identity,
                                              scale=cols[:, 1:2], bias=cols[:, 3:4]),
             [("sDV", m_xn0)])
        m_xn = P.mark("scalar", "sAC")
        sqf3 = sqf.rearrange("p (b c) -> p b c", c=32)
        m_sc = emit_scale_batch(
            P, [(sqf3, slice(0, 64), [("sAC", m_xn)])],
            smq[0][:, 0:64], smq[1][:, 0:64], smq[2][:, 0:64], smq[3][:, 0:64],
            smb[:, 0:64], half=False)
        m_as_q, m_q = emit_quant_act_multi(P, [dict(
            src=sqf, outq=xqt, rs=smq[2][:, 0:64], scb=smb[:, 0:64],
            zf=scr[0], azf=scr[1], g1=scrb[0], g2=scrb[1], g3=scrb[2],
            g4=scrb[3], wt=[("sAC", m_xn)],
            wt_out=[("sST", P.now("sST"))])], m_sc, m_scr_act)
        m_scr_act = m_as_q
        P.op("gpsimd", lambda e, tt=tt: e.dma_start(
            out=xq_d[tt * 128:(tt + 1) * 128, :], in_=xqt), [("sDV", m_q)])
        P.mark("gpsimd", "sST", 16)
    m_xq_st = P.now("sST")
    P.op("sync", lambda e: e.dma_start(
        out=xq_fm.rearrange("p (c n) -> p c n", c=16),
        in_=xq_d.rearrange("n (c p) -> n c p", p=128), transpose=True),
        [("sST", m_xq_st)])
    m_xqfm = P.mark("sync", "sLD", 16)

    m_itq_store = [0, 0]

    for seg in range(16):
        for mat, (wqd_t, wT_dst) in enumerate(((wgq_d, wgqT), (wuq_d, wuqT))):
            ci0 = (seg * 2 + mat) * 4
            m_stq = max(pump_store_mark[ci0:ci0 + 4])
            P.op("sync", lambda e, wqd_t=wqd_t, wT_dst=wT_dst, seg=seg: e.dma_start(
                out=wT_dst.rearrange("p (c n) -> p c n", c=16),
                in_=wqd_t[seg * 512:(seg + 1) * 512, :].rearrange(
                    "n (c p) -> n c p", p=128),
                transpose=True), [("sST", m_stq), ("sPE", P.now("sPE"))])
            P.mark("sync", "sLD", 16)
        m_wT = P.now("sLD")
        for tp in range(4):
            pair = []
            for half in (0, 1):
                tt = tp * 2 + half
                bg = 1 + 2 * half
                bu = 2 + 2 * half
                for c in range(16):
                    P.op("tensor", lambda e, tt=tt, c=c, bg=bg: e.matmul(
                        ps[bg], xq_fm[:, c * TOWN + tt * 128: c * TOWN + tt * 128 + 128],
                        wgqT[:, c * 512:(c + 1) * 512], start=(c == 0), stop=(c == 15)),
                        ([("sLD", m_wT), ("sLD", m_xqfm)] + P.guard(bg)) if c == 0 else [])
                for c in range(16):
                    P.op("tensor", lambda e, tt=tt, c=c, bu=bu: e.matmul(
                        ps[bu], xq_fm[:, c * TOWN + tt * 128: c * TOWN + tt * 128 + 128],
                        wuqT[:, c * 512:(c + 1) * 512], start=(c == 0), stop=(c == 15)),
                        P.guard(bu) if c == 0 else [])
                m_gu = P.mark("tensor", "sPE")
                P.op("scalar", lambda s, bg=bg: s.activation(sg_f, ps[bg], AF.Silu),
                     [("sPE", m_gu)] + P.guard(30))
                m_sg = P.mark("scalar", "sAC")
                P.set_guard(bg, [("sAC", m_sg)])
                P.op("vector", lambda v, bu=bu, half=half: v.tensor_tensor(
                    it_f[half], sg_f, ps[bu], ALU.mult),
                    [("sAC", m_sg)] + P.guard(31 + half))
                m_it = P.mark("vector", "sDV")
                P.set_guard(bu, [("sDV", m_it)])
                P.set_guard(30, [("sDV", m_it)])
                pair.append(m_it)
            jobs2 = []
            for half in (0, 1):
                i3 = it_f[half][:].rearrange("p (b c) -> p b c", c=32)
                jobs2.append((i3, slice(half * 16, (half + 1) * 16),
                              [("sDV", pair[half])]))
            m_sc2 = emit_scale_batch(P, jobs2, smq[0][:, 0:32], smq[1][:, 0:32],
                                     smq[2][:, 0:32], smq[3][:, 0:32],
                                     smb[:, 0:32], half=False)
            cas = []
            for half in (0, 1):
                sl = slice(half * 512, (half + 1) * 512)
                cas.append(dict(
                    src=it_f[half][:], outq=itq[half][:],
                    rs=smq[2][:, half * 16:(half + 1) * 16],
                    scb=smb[:, half * 16:(half + 1) * 16],
                    zf=scr[0][:, sl], azf=scr[1][:, sl],
                    g1=scrb[0][:, sl], g2=scrb[1][:, sl],
                    g3=scrb[2][:, sl], g4=scrb[3][:, sl],
                    wt=[("sDV", pair[half])],
                    wt_out=[("sST", m_itq_store[half])]))
            m_as2, m_fin2 = emit_quant_act_multi(P, cas, m_sc2, m_scr_act)
            m_scr_act = m_as2
            P.set_guard(31, [("sDV", m_fin2)])
            P.set_guard(32, [("sDV", m_fin2)])
            for half in (0, 1):
                tt = tp * 2 + half
                P.op("gpsimd", lambda e, seg=seg, tt=tt, half=half: e.dma_start(
                    out=itq_d[tt * 128:(tt + 1) * 128, seg * 512:(seg + 1) * 512],
                    in_=itq[half]), [("sDV", m_fin2)])
                m_itq_store[half] = P.mark("gpsimd", "sST", 16)

    m_wdq = max(pump_store_mark[128:192])

    nc.sbuf_base = base_consts
    itq_fm = sb("itq_fm", [128, 64 * TOWN], BF16)
    wdT = [sb(f"wdT{i}", [128, 512], BF16) for i in range(2)]
    ev_f = sb("ev_f", [128, 512], F32)
    xr_f = sb("xr_f", [128, 512], F32)
    P.op("sync", lambda e: e.dma_start(
        out=itq_fm.rearrange("p (c n) -> p c n", c=64),
        in_=itq_d.rearrange("n (c p) -> n c p", p=128), transpose=True),
        [("sST", P.now("sST"))])
    m_itfm = P.mark("sync", "sLD", 16)
    for odg in range(4):
        for oc in range(64):
            bufi = oc % 2
            P.op("sync", lambda e, odg=odg, oc=oc, bufi=bufi: e.dma_start(
                out=wdT[bufi],
                in_=wdq_d[odg * 512:(odg + 1) * 512, oc * 128:(oc + 1) * 128],
                transpose=True),
                [("sST", m_wdq)] + P.guard(40 + bufi))
            m_w = P.mark("sync", "sLD", 16)
            for odt4 in range(4):
                for th in range(2):
                    bank = odt4 * 2 + th
                    P.op("tensor", lambda e, oc=oc, odt4=odt4, th=th, bufi=bufi, bank=bank: e.matmul(
                        ps[bank], wdT[bufi][:, odt4 * 128:(odt4 + 1) * 128],
                        itq_fm[:, oc * TOWN + th * 512: oc * TOWN + (th + 1) * 512],
                        start=(oc == 0), stop=(oc == 63)),
                        ([("sLD", m_w), ("sLD", m_itfm)] + (P.guard(bank) if oc == 0 else []))
                        if (odt4 == 0 and th == 0) else
                        (P.guard(bank) if oc == 0 else []))
            m_mm = P.mark("tensor", "sPE")
            P.set_guard(40 + bufi, [("sPE", m_mm)])
        m_dn = P.now("sPE")
        for odt4 in range(4):
            for th in range(2):
                bank = odt4 * 2 + th
                od = odg * 4 + odt4
                P.op("scalar", lambda s, bank=bank: s.activation(ev_f, ps[bank], AF.Copy),
                     [("sPE", m_dn)] + P.guard(50))
                m_ev = P.mark("scalar", "sAC")
                P.set_guard(bank, [("sAC", m_ev)])
                P.op("sync", lambda e, od=od, th=th: e.dma_start(
                    out=xr_f, in_=acc_d[od * 128:(od + 1) * 128, th * 512:(th + 1) * 512]),
                    [("sST", m_acc_all)] + P.guard(51))
                m_xr = P.mark("sync", "sLD", 16)
                P.op("vector", lambda v: v.tensor_tensor(ev_f, ev_f, xr_f, ALU.add),
                     [("sAC", m_ev), ("sLD", m_xr)])
                m_y = P.mark("vector", "sDV")
                P.set_guard(50, [("sDV", m_y)])
                P.set_guard(51, [("sDV", m_y)])
                P.op("gpsimd", lambda e, od=od, th=th: e.dma_start(
                    out=y_out[od * 128:(od + 1) * 128, th * 512:(th + 1) * 512],
                    in_=ev_f), [("sDV", m_y)])
                m_yst = P.mark("gpsimd", "sST", 16)
                P.set_guard(50, [("sDV", m_y), ("sST", m_yst)])
    m_final = P.now("sST")
    P.op("gpsimd", lambda e: e.engine_nop(), [("sST", m_final)])

    import contextlib
    with contextlib.ExitStack() as es:
        sems = {n: es.enter_context(nc.semaphore(n))
                for n in ("sLD", "sST", "sPE", "sDV", "sAC")}
        block = es.enter_context(nc.Block())

        def replay(eng_obj, stream):
            for fn, waits, inc_sem, inc_n in stream:
                for s, v in waits:
                    eng_obj.wait_ge(sems[s], v)
                ins = fn(eng_obj)
                if inc_sem is not None:
                    ins.then_inc(sems[inc_sem], inc_n)

        @block.sync
        def _(e):
            replay(e, P.streams["sync"])

        @block.vector
        def _(e):
            replay(e, P.streams["vector"])

        @block.scalar
        def _(e):
            replay(e, P.streams["scalar"])

        @block.tensor
        def _(e):
            replay(e, P.streams["tensor"])

        @block.gpsimd
        def _(e):
            replay(e, P.streams["gpsimd"])

    return nc


_NC_CACHE = None


def _get_nc():
    global _NC_CACHE
    if _NC_CACHE is None:
        _NC_CACHE = build_nc()
    return _NC_CACHE


def _rope_mats():
    inv = 10000.0 ** (-(np.arange(0, DR, 2, dtype=np.float64)) / DR)
    rm = np.zeros((128, (NH // 2) * DR), dtype=np.float64)
    for h in range(NH):
        pr, hh = h // 2, h % 2
        ang = h * inv
        c, s = np.cos(ang), np.sin(ang)
        for i in range(DR // 2):
            rm[hh * DR + 2 * i, pr * DR + 2 * i] = c[i]
            rm[hh * DR + 2 * i + 1, pr * DR + 2 * i] = -s[i]
            rm[hh * DR + 2 * i, pr * DR + 2 * i + 1] = s[i]
            rm[hh * DR + 2 * i + 1, pr * DR + 2 * i + 1] = c[i]
    return rm.astype(np.float32)


def kernel(L0_attn_x, L0_attn_kv_down_w, L0_attn_kv_down_b, L0_attn_key_up_w,
           L0_attn_key_up_b, L0_attn_value_up_w, L0_attn_value_up_b,
           L0_attn_key_rope_w, L0_attn_key_rope_b, L0_attn_query_down_w,
           L0_attn_query_down_b, L0_attn_query_up_w, L0_attn_query_up_b,
           L0_attn_query_rope_w, L0_attn_query_rope_b, L0_attn_out_w,
           L0_attn_out_b, L0_ffn_w_gate, L0_ffn_w_up, L0_ffn_w_down):
    f32 = np.float32
    x = np.asarray(L0_attn_x, f32)

    def T(a):
        return np.ascontiguousarray(np.asarray(a, f32).T)

    wkvT = T(L0_attn_kv_down_w)
    wkuT = T(L0_attn_key_up_w)
    wvuT = T(L0_attn_value_up_w)
    wkrT = T(L0_attn_key_rope_w)
    wqdT = T(L0_attn_query_down_w)
    wquT = T(L0_attn_query_up_w)
    wqrT = T(L0_attn_query_rope_w)
    woutT = T(L0_attn_out_w)

    def colpack(b, n):
        return np.ascontiguousarray(np.asarray(b, f32).reshape(n, 128).T)

    b_kv = colpack(L0_attn_kv_down_b, KV // 128)
    b_qd = colpack(L0_attn_query_down_b, QC // 128)
    b_qu = colpack(L0_attn_query_up_b, NH * DH // 128)
    b_out = colpack(L0_attn_out_b, H // 128)
    bqr = np.asarray(L0_attn_query_rope_b, f32).reshape(NH, DR)
    b_qr = np.zeros((128, NH // 2), f32)
    for pr in range(NH // 2):
        b_qr[0:DR, pr] = bqr[2 * pr]
        b_qr[DR:128, pr] = bqr[2 * pr + 1]
    b_vu_row = np.asarray(L0_attn_value_up_b, f32).reshape(1, NH * DH)
    rmat = _rope_mats()
    ones_col = np.ones((128, 1), f32)
    wg = np.ascontiguousarray(np.asarray(L0_ffn_w_gate, f32))
    wu = np.ascontiguousarray(np.asarray(L0_ffn_w_up, f32))
    wd = np.ascontiguousarray(np.asarray(L0_ffn_w_down, f32))

    shared = dict(wkvT=wkvT, wkuT=wkuT, wvuT=wvuT, wkrT=wkrT, wqdT=wqdT,
                  wquT=wquT, wqrT=wqrT, woutT=woutT, b_kv=b_kv, b_qd=b_qd,
                  b_qu=b_qu, b_qr=b_qr, b_vu_row=b_vu_row, b_out=b_out,
                  rmat=rmat, ones_col=ones_col,
                  ones_row=np.ones((1, 128), f32), wg=wg, wu=wu, wd=wd)

    in_maps = []
    for cid in range(NCORES):
        b = cid // 4
        toff = (cid % 4) * TOWN
        xb = x[b]
        x_roll = np.roll(xb, -toff, axis=0)
        m = dict(shared)
        m["x_fm"] = np.ascontiguousarray(x_roll.T)
        m["x_tm"] = np.ascontiguousarray(x_roll[:TOWN])
        in_maps.append(m)

    nc = _get_nc()
    res = run_bass_kernel_spmd(nc, in_maps, core_ids=list(range(NCORES)))

    out = np.empty((B, S, H), dtype=f32)
    for cid in range(NCORES):
        b = cid // 4
        toff = (cid % 4) * TOWN
        y = np.asarray(res.results[cid]["y_out"], f32)
        out[b, toff:toff + TOWN, :] = y.T
    return out


# version 09
# speedup vs baseline: 1.2513x; 1.0441x over previous
import math
import numpy as np
import ml_dtypes
import concourse.bass as bass
import concourse.mybir as mybir
from concourse.bass_utils import run_bass_kernel_spmd

F32 = mybir.dt.float32
BF16 = mybir.dt.bfloat16
AF = mybir.ActivationFunctionType
ALU = mybir.AluOpType
AX = mybir.AxisListType

NCORES = 8
B, S, H = 2, 4096, 2048
TOWN = 1024
KV, QC = 512, 1024
NH, DH, DR = 16, 128, 64
II = 8192
EPS = 1e-5
ALPHA = 1.0 / math.sqrt(DH + DR)
LN2_INV = float(1.0 / np.log(2.0))

ENGS = ("sync", "vector", "scalar", "tensor", "gpsimd")
_DEBUG = False
PARANOID_SCALE = True
PARANOID_ACT = 0


class Plan:

    def __init__(self):
        self.streams = {e: [] for e in ENGS}
        self.cnt = {}
        self.bank_guard = {}

    AUTO = {}

    def op(self, eng, fn, waits=()):
        waits = list(waits)
        auto = self.AUTO.get(eng)
        if auto is not None:
            waits.append((auto, self.cnt.get(auto, 0)))
        for sem, val in waits:
            assert val <= self.cnt.get(sem, 0), (
                f"wait {sem}>={val} exceeds planned count {self.cnt.get(sem, 0)} "
                f"(deadlock)")
        if auto is not None:
            self.cnt[auto] = self.cnt.get(auto, 0) + 1
            self.streams[eng].append([fn, waits, auto, 1])
        else:
            self.streams[eng].append([fn, waits, None, 0])

    def mark(self, eng, sem, n=1):
        entry = self.streams[eng][-1]
        auto = self.AUTO.get(eng)
        if auto is not None:
            assert sem == auto and n == 1, f"{eng} marks only {auto}"
            return self.cnt[sem]
        if entry[2] == sem:
            return self.cnt[sem]
        assert entry[2] is None, "op already has an inc"
        entry[2] = sem
        entry[3] = n
        self.cnt[sem] = self.cnt.get(sem, 0) + n
        return self.cnt[sem]

    def now(self, sem):
        return self.cnt.get(sem, 0)

    def guard(self, bank):
        return self.bank_guard.pop(bank, [])

    def set_guard(self, bank, waits):
        self.bank_guard[bank] = list(waits)



def emit_scale_batch(P, jobs, amax, t0, t1, t2, t2h, half):
    def gop(fn, waits=()):
        P.op("vector", fn, list(waits) + [("sDV", P.now("sDV"))])
        P.mark("vector", "sDV")
    m_prev_ln = getattr(P, "scale_ln_mark", 0)
    for src3, sl, wt in jobs:
        gop(lambda v, src3=src3, sl=sl: v.tensor_reduce(
            amax[:, sl], src3, AX.X, ALU.max, apply_absolute_value=True),
            list(wt) + [("sAC", m_prev_ln)])
    gop(lambda v: v.tensor_scalar_max(amax, amax, 1e-30))
    mv = P.now("sDV")
    P.op("scalar", lambda s: s.activation(t2, amax, AF.Ln), [("sDV", mv)])
    ma = P.mark("scalar", "sAC")
    P.scale_ln_mark = ma
    gop(lambda v: v.tensor_scalar(t2, t2, LN2_INV, 104.5, ALU.mult, ALU.add),
        [("sAC", ma)])
    first = True
    for b in (128, 64, 32, 16, 8, 4, 2, 1):
        fb = float(b)
        gop(lambda v, fb=fb: v.tensor_single_scalar(t1, t2, fb, ALU.is_ge))
        gop(lambda v, fb=fb: v.scalar_tensor_tensor(t2, t1, -fb, t2, ALU.mult, ALU.add))
        if b == 128:
            f64 = float(2.0**64 - 1.0)
            gop(lambda v, f64=f64: v.tensor_scalar(t1, t1, f64, 1.0, ALU.mult, ALU.add))
            gop(lambda v: v.tensor_scalar_mul(t0, t1, float(2.0**-104)))
            gop(lambda v: v.tensor_tensor(t0, t0, t1, ALU.mult))
            first = False
            continue
        fw = float(2.0**b - 1.0)
        gop(lambda v, fw=fw: v.tensor_scalar(t1, t1, fw, 1.0, ALU.mult, ALU.add))
        if first:
            gop(lambda v: v.tensor_scalar_mul(t0, t1, float(2.0**-104)))
            first = False
        else:
            gop(lambda v: v.tensor_tensor(t0, t0, t1, ALU.mult))
    gop(lambda v: v.tensor_tensor(t1, t0, amax, ALU.is_gt))
    gop(lambda v: v.tensor_scalar(t1, t1, -0.125, 0.25, ALU.mult, ALU.add))
    gop(lambda v: v.tensor_tensor(t0, t0, t1, ALU.mult))
    gop(lambda v: v.reciprocal(t1, t0))
    if half:
        gop(lambda v: v.tensor_scalar_mul(t2h, t0, 0.5))
    else:
        gop(lambda v: v.tensor_copy(out=t2h, in_=t0))
    return P.now("sDV")


def emit_quant_w(P, src, outq, rs, sch, azf, b1, b2, b3, m_sc, wt, wt_out=(),
                 m_scr_act=0, wt_b2=(), dve_sign=False):
    Pp, F = src.shape
    nb = F // 32
    src3 = src.rearrange("p (b c) -> p b c", c=32)
    rs_b = rs[:, :, None].broadcast_to([Pp, nb, 32])
    sch_b = sch[:, :, None].broadcast_to([Pp, nb, 32])
    az3 = azf.rearrange("p (b c) -> p b c", c=32)
    P.op("vector", lambda v: v.tensor_tensor(az3, src3, rs_b, ALU.mult),
         list(wt) + [("sDV", m_sc)] + ([] if dve_sign else [("sAC", m_scr_act)]))
    m_z = P.mark("vector", "sDV")
    V = lambda fn, w=(): P.op("vector", fn, w)
    if dve_sign:
        V(lambda v: v.tensor_single_scalar(b3, azf, 0.0, ALU.is_gt))
        V(lambda v: v.tensor_scalar(b3, b3, 2.0, -1.0, ALU.mult, ALU.add))
        V(lambda v: v.scalar_tensor_tensor(azf, azf, -1.0, azf, ALU.mult, ALU.max))
        m_as = m_scr_act
    else:
        P.op("scalar", lambda s: s.activation(b3, azf, AF.Sign), [("sDV", m_z)])
        P.op("scalar", lambda s: s.activation(azf, azf, AF.Abs))
        m_as = P.mark("scalar", "sAC")
    V(lambda v: v.tensor_single_scalar(b1, azf, 0.25, ALU.is_gt),
      [] if dve_sign else [("sAC", m_as)])
    V(lambda v: v.scalar_tensor_tensor(b1, azf, 0.75, b1, ALU.is_ge, ALU.add))
    V(lambda v: v.scalar_tensor_tensor(b1, azf, 1.25, b1, ALU.is_gt, ALU.add))
    V(lambda v: v.scalar_tensor_tensor(b1, azf, 1.75, b1, ALU.is_ge, ALU.add))
    V(lambda v: v.tensor_single_scalar(b2, azf, 2.5, ALU.is_gt), list(wt_b2))
    V(lambda v: v.scalar_tensor_tensor(b2, azf, 3.5, b2, ALU.is_ge, ALU.add))
    V(lambda v: v.tensor_single_scalar(azf, azf, 5.0, ALU.is_gt))
    V(lambda v: v.scalar_tensor_tensor(b2, azf, 2.0, b2, ALU.mult, ALU.add))
    V(lambda v: v.scalar_tensor_tensor(b1, b2, 2.0, b1, ALU.mult, ALU.add))
    V(lambda v: v.tensor_tensor(b1, b1, b3, ALU.mult))
    b13 = b1.rearrange("p (b c) -> p b c", c=32)
    oq3 = outq.rearrange("p (b c) -> p b c", c=32)
    V(lambda v: v.tensor_tensor(oq3, b13, sch_b, ALU.mult), list(wt_out))
    return m_z, m_as, P.mark("vector", "sDV")


def emit_quant_act_multi(P, cascades, m_sc, m_scr_act=0):
    seq = []

    for c in cascades:
        Pp, F = c["src"].shape
        nb = F // 32
        src3 = c["src"].rearrange("p (b c) -> p b c", c=32)
        rs_b = c["rs"][:, :, None].broadcast_to([Pp, nb, 32])
        sc_b = c["scb"][:, :, None].broadcast_to([Pp, nb, 32])
        zf, azf = c["zf"], c["azf"]
        g1, g2, g3, g4 = c["g1"], c["g2"], c["g3"], c["g4"]
        z3 = zf.rearrange("p (b c) -> p b c", c=32)
        oq3 = c["outq"].rearrange("p (b c) -> p b c", c=32)
        ops = []
        ops.append(("vz", lambda v, z3=z3, src3=src3, rs_b=rs_b:
                    v.tensor_tensor(z3, src3, rs_b, ALU.mult), c["wt"]))
        ops.append(("s", lambda s, azf=azf, zf=zf: s.activation(azf, zf, AF.Abs), ()))
        ops.append(("s2", lambda s, g1=g1, zf=zf: s.activation(g1, zf, AF.Sign), ()))
        ops.append(("v1", lambda v, g2=g2, azf=azf:
                    v.tensor_single_scalar(g2, azf, 2.0, ALU.is_ge), ()))
        ops.append(("v", lambda v, g3=g3, azf=azf:
                    v.tensor_single_scalar(g3, azf, 4.0, ALU.is_ge), ()))
        ops.append(("v", lambda v, g4=g4, g2=g2:
                    v.tensor_scalar(g4, g2, -4.0, 8.0, ALU.mult, ALU.add), ()))
        ops.append(("v", lambda v, g4=g4, g3=g3:
                    v.scalar_tensor_tensor(g4, g3, -2.0, g4, ALU.mult, ALU.add), ()))
        ops.append(("v", lambda v, g2=g2:
                    v.tensor_scalar(g2, g2, 0.125, 0.125, ALU.mult, ALU.add), ()))
        ops.append(("v", lambda v, g2=g2, g3=g3:
                    v.scalar_tensor_tensor(g2, g3, 0.25, g2, ALU.mult, ALU.add), ()))
        ops.append(("v", lambda v, zf=zf, azf=azf, g4=g4:
                    v.tensor_tensor(zf, azf, g4, ALU.mult), ()))
        ops.append(("v", lambda v, g3=g3, zf=zf:
                    v.tensor_single_scalar(g3, zf, 15.5, ALU.is_ge), ()))
        ops.append(("v", lambda v, azf=azf, g3=g3, zf=zf:
                    v.scalar_tensor_tensor(azf, g3, -16.0, zf, ALU.mult, ALU.add), ()))
        for b in (8, 4, 2, 1):
            fb = float(b)
            ops.append(("v", lambda v, g3=g3, azf=azf, fb=fb:
                        v.tensor_single_scalar(g3, azf, fb - 0.5, ALU.is_ge), ()))
            ops.append(("v", lambda v, azf=azf, g3=g3, fb=fb:
                        v.scalar_tensor_tensor(azf, g3, -fb, azf, ALU.mult, ALU.add), ()))
        ops.append(("v", lambda v, g3=g3, azf=azf:
                    v.scalar_tensor_tensor(g3, azf, -0.5, g3, ALU.is_equal, ALU.mult), ()))
        ops.append(("v", lambda v, zf=zf, azf=azf:
                    v.tensor_tensor(zf, zf, azf, ALU.subtract), ()))
        ops.append(("v", lambda v, zf=zf, g3=g3:
                    v.tensor_tensor(zf, zf, g3, ALU.subtract), ()))
        ops.append(("v", lambda v, zf=zf, g2=g2:
                    v.tensor_tensor(zf, zf, g2, ALU.mult), ()))
        ops.append(("v", lambda v, zf=zf:
                    v.tensor_scalar_min(zf, zf, 7.5), ()))
        ops.append(("v", lambda v, zf=zf, g1=g1:
                    v.tensor_tensor(zf, zf, g1, ALU.mult), ()))
        ops.append(("vo", lambda v, oq3=oq3, z3=z3, sc_b=sc_b:
                    v.tensor_tensor(oq3, z3, sc_b, ALU.mult), c.get("wt_out", ())))
        seq.append(ops)

    merged = []
    n = max(len(s) for s in seq)
    for i in range(n):
        for s in seq:
            if i < len(s):
                merged.append(s[i])
    pend_z = 0
    m_as = 0
    for tag, fn, wt in merged:
        if tag == "vz":
            P.op("vector", fn, list(wt) + [("sDV", m_sc), ("sAC", m_scr_act)])
            P.mark("vector", "sDV")
            pend_z = P.now("sDV")
        elif tag == "s":
            P.op("scalar", fn, [("sDV", pend_z)])
        elif tag == "s2":
            P.op("scalar", fn, ())
            m_as = P.mark("scalar", "sAC")
        elif tag == "v1":
            P.op("vector", fn, [("sAC", m_as)])
        else:
            P.op("vector", fn, list(wt))
    return m_as, P.mark("vector", "sDV")



def build_nc():
    nc = bass.Bass("TRN2", target_bir_lowering=False,
                   detect_race_conditions=False)

    def dram(name, shape, dtype=F32, kind="ExternalInput"):
        return nc.dram_tensor(name, list(shape), dtype, kind=kind).ap()

    x_fm = dram("x_fm", [H, S])
    x_tm = dram("x_tm", [TOWN, H])
    wkvT = dram("wkvT", [H, KV])
    wkuT = dram("wkuT", [KV, NH * DH])
    wvuT = dram("wvuT", [KV, NH * DH])
    wkrT = dram("wkrT", [KV, NH * DR])
    wqdT = dram("wqdT", [H, QC])
    wquT = dram("wquT", [QC, NH * DH])
    wqrT = dram("wqrT", [QC, NH * DR])
    woutT = dram("woutT", [NH * DH, H])
    b_kv = dram("b_kv", [128, KV // 128])
    b_qd = dram("b_qd", [128, QC // 128])
    b_qu = dram("b_qu", [128, NH * DH // 128])
    b_qr = dram("b_qr", [128, NH // 2])
    b_vu_row = dram("b_vu_row", [1, NH * DH])
    b_out = dram("b_out", [128, H // 128])
    rmat = dram("rmat", [128, (NH // 2) * DR])
    ones_col = dram("ones_col", [128, 1])
    ones_row = dram("ones_row", [1, 128])
    wg = dram("wg", [II, H])
    wu = dram("wu", [II, H])
    wd = dram("wd", [H, II])

    y_out = dram("y_out", [H, TOWN], F32, kind="ExternalOutput")

    scratch_kind = "ExternalOutput" if _DEBUG else "Internal"
    acc_d = dram("acc_d", [H, TOWN], F32, kind=scratch_kind)
    ctx_d = dram("ctx_d", [128, NH * TOWN], BF16, kind="Internal")
    xq_d = dram("xq_d", [TOWN, H], BF16, kind=scratch_kind)
    wgq_d = dram("wgq_d", [II, H], BF16, kind=scratch_kind)
    wuq_d = dram("wuq_d", [II, H], BF16, kind=scratch_kind)
    wdq_d = dram("wdq_d", [H, II], BF16, kind=scratch_kind)
    itq_d = dram("itq_d", [TOWN, II], BF16, kind=scratch_kind)
    if _DEBUG:
        dbg_kvc = dram("dbg_kvc", [128, 4 * S], BF16, kind="ExternalOutput")
        dbg_qc = dram("dbg_qc", [128, 8 * TOWN], BF16, kind="ExternalOutput")
        dbg_ctx = dram("dbg_ctx", [128, NH * TOWN], BF16, kind="ExternalOutput")
        dbg_kc = dram("dbg_kc", [128, S], BF16, kind="ExternalOutput")
        dbg_kr = dram("dbg_kr", [128, S], BF16, kind="ExternalOutput")
        dbg_v = dram("dbg_v", [128, 32 * 256], BF16, kind="ExternalOutput")
        dbg_qcs = dram("dbg_qcs", [128, TOWN], BF16, kind="ExternalOutput")
        dbg_qrot = dram("dbg_qrot", [128, TOWN], BF16, kind="ExternalOutput")
        dbg_bvu = dram("dbg_bvu", [1, NH * DH], BF16, kind="ExternalOutput")
        dbg_oner = dram("dbg_oner", [1, 128], BF16, kind="ExternalOutput")
        dbg_wvu = dram("dbg_wvu", [128, 4 * 256], BF16, kind="ExternalOutput")
        dbg_wqu = dram("dbg_wqu", [128, 8 * 256], BF16, kind="ExternalOutput")
        dbg_wgqT = dram("dbg_wgqT", [128, 16 * 512], BF16, kind="ExternalOutput")
        dbg_sg = dram("dbg_sg", [128, 512], F32, kind="ExternalOutput")
        dbg_it = dram("dbg_it", [128, 512], F32, kind="ExternalOutput")
        dbg_xqfm = dram("dbg_xqfm", [128, 16 * TOWN], BF16, kind="ExternalOutput")
        dbg_itq0 = dram("dbg_itq0", [128, 512], BF16, kind="ExternalOutput")
        dbg_ism = dram("dbg_ism", [128, 32], F32, kind="ExternalOutput")

    P = Plan()

    def sb(name, shape, dtype):
        return nc.alloc_sbuf_tensor(name, list(shape), dtype).ap()

    onec_bf = sb("onec_bf", [128, 1], BF16)
    oner_bf = sb("oner_bf", [1, 128], BF16)
    bkv_sb = sb("bkv_sb", [128, KV // 128], F32)
    bqd_sb = sb("bqd_sb", [128, QC // 128], F32)
    bqu_sb = sb("bqu_sb", [128, NH * DH // 128], F32)
    bqr_sb = sb("bqr_sb", [128, NH // 2], F32)
    bvu_bf = sb("bvu_bf", [1, NH * DH], BF16)
    bout_sb = sb("bout_sb", [128, H // 128], F32)
    rmat_sb = sb("rmat_sb", [128, (NH // 2) * DR], F32)
    colkv_bf = sb("colkv_bf", [1, KV], BF16)
    colqd_bf = sb("colqd_bf", [1, QC], BF16)
    srow = sb("srow", [1, 4 * 512], F32)
    eps_col = sb("eps_col", [128, 1], F32)
    onec_f = sb("onec_f", [128, 1], F32)
    oner_f = sb("oner_f", [1, 128], F32)
    rowbf = sb("rowbf", [1, 2 * 512], BF16)
    base_consts = nc.sbuf_base

    kv_c = sb("kv_c", [128, 4 * S], BF16)
    q_c = sb("q_c", [128, 8 * TOWN], BF16)
    base_persist = nc.sbuf_base

    ps = [nc.alloc_psum_tensor(f"ps{i}", [128, 512], F32).ap() for i in range(8)]

    wstage = sb("wstage", [128, 4096], F32)
    wkv_bf = sb("wkv_bf", [128, 16 * 512], BF16)
    wqd_bf = sb("wqd_bf", [128, 16 * 1024], BF16)
    xstage = sb("xstage", [128, 16 * 512], F32)
    xbf = sb("xbf", [128, 16 * 512], BF16)
    sqbf = sb("sqbf", [128, 16 * 512], BF16)
    bvu_f32 = wstage[0:1, 0:NH * DH]

    for dst, src in ((bkv_sb, b_kv), (bqd_sb, b_qd), (bqu_sb, b_qu),
                     (bqr_sb, b_qr), (bout_sb, b_out), (rmat_sb, rmat),
                     (bvu_f32, b_vu_row)):
        P.op("sync", lambda e, dst=dst, src=src: e.dma_start(out=dst, in_=src))
        P.mark("sync", "sLD", 16)
    P.op("sync", lambda e: e.dma_start(out=onec_f, in_=ones_col))
    P.mark("sync", "sLD", 16)
    P.op("sync", lambda e: e.dma_start(out=oner_f, in_=ones_row))
    m_consts = P.mark("sync", "sLD", 16)
    P.op("vector", lambda v: v.tensor_copy(out=onec_bf, in_=onec_f),
         [("sLD", m_consts)])
    P.op("vector", lambda v: v.tensor_copy(out=oner_bf, in_=oner_f))
    P.op("vector", lambda v: v.tensor_copy(out=bvu_bf, in_=bvu_f32))
    P.op("vector", lambda v: v.memset(eps_col, EPS))
    m_c0 = P.mark("vector", "sDV")

    for hlf in range(2):
        P.op("sync", lambda e, hlf=hlf: e.dma_start(
            out=wstage[:, 0:8 * 512].rearrange("p (c n) -> p c n", c=8),
            in_=wkvT[hlf * 1024:(hlf + 1) * 1024, :].rearrange("(c p) n -> p c n", p=128)),
            [("sDV", m_c0 if hlf == 0 else P.now("sDV"))])
        m_wkv = P.mark("sync", "sLD", 16)
        P.op("vector", lambda v, hlf=hlf: v.tensor_copy(
            out=wkv_bf[:, hlf * 8 * 512:(hlf + 1) * 8 * 512], in_=wstage[:, 0:8 * 512]),
            [("sLD", m_wkv)])
        P.mark("vector", "sDV")
    m_wkvc = P.now("sDV")
    for c in range(16):
        P.op("tensor", lambda e, c=c: e.matmul(
            ps[0][0:1, 0:KV], onec_bf, wkv_bf[:, c * KV:(c + 1) * KV],
            start=(c == 0), stop=(c == 15)),
            [("sDV", m_wkvc)] if c == 0 else [])
    m_pe = P.mark("tensor", "sPE")
    P.op("vector", lambda v: v.tensor_copy(out=colkv_bf, in_=ps[0][0:1, 0:KV]),
         [("sPE", m_pe)])
    m_colkv = P.mark("vector", "sDV")
    for qtr in range(4):
        P.op("sync", lambda e, qtr=qtr: e.dma_start(
            out=wstage[:, 0:4 * 1024].rearrange("p (c n) -> p c n", c=4),
            in_=wqdT[qtr * 512:(qtr + 1) * 512, :].rearrange("(c p) n -> p c n", p=128)),
            [("sDV", m_colkv if qtr == 0 else P.now("sDV"))])
        mq = P.mark("sync", "sLD", 16)
        P.op("vector", lambda v, qtr=qtr: v.tensor_copy(
            out=wqd_bf[:, qtr * 4 * QC:(qtr + 1) * 4 * QC], in_=wstage[:, 0:4 * 1024]),
            [("sLD", mq)])
        P.mark("vector", "sDV")
    m_wqdc = P.now("sDV")
    for half in range(2):
        for c in range(16):
            P.op("tensor", lambda e, c=c, half=half: e.matmul(
                ps[half][0:1, :], onec_bf,
                wqd_bf[:, c * QC + half * 512: c * QC + (half + 1) * 512],
                start=(c == 0), stop=(c == 15)),
                [("sDV", m_wqdc)] if (c == 0 and half == 0) else [])
    m_pe = P.mark("tensor", "sPE")
    P.op("vector", lambda v: v.tensor_copy(out=colqd_bf[0:1, 0:512], in_=ps[0][0:1, :]),
         [("sPE", m_pe)])
    P.op("vector", lambda v: v.tensor_copy(out=colqd_bf[0:1, 512:1024], in_=ps[1][0:1, :]))
    m_colqd = P.mark("vector", "sDV")

    r_m = srow[0:1, 0:512]
    r_ss = srow[0:1, 512:1024]
    r_t = srow[0:1, 1024:1536]
    rstd_bf = rowbf[0:1, 0:512]
    mrs_bf = rowbf[0:1, 512:1024]
    for tt in range(8):
        own = tt < 2
        P.op("sync", lambda e, tt=tt: e.dma_start(
            out=xstage.rearrange("p (c n) -> p c n", c=16),
            in_=x_fm[:, tt * 512:(tt + 1) * 512].rearrange("(c p) n -> p c n", p=128)),
            [("sPE", P.now("sPE")), ("sAC", P.now("sAC")), ("sDV", P.now("sDV"))])
        m_x = P.mark("sync", "sLD", 16)
        P.op("vector", lambda v: v.tensor_copy(out=xbf, in_=xstage), [("sLD", m_x)])
        m_xbf = P.mark("vector", "sDV")
        P.op("scalar", lambda s: s.activation(sqbf, xstage, AF.Square), [("sLD", m_x)])
        m_sq = P.mark("scalar", "sAC")
        for c in range(16):
            P.op("tensor", lambda e, c=c: e.matmul(
                ps[1][0:1, :], onec_bf, xbf[:, c * 512:(c + 1) * 512],
                start=(c == 0), stop=(c == 15)),
                ([("sDV", m_xbf)] + P.guard(1)) if c == 0 else [])
        for c in range(16):
            P.op("tensor", lambda e, c=c: e.matmul(
                ps[2][0:1, :], onec_bf, sqbf[:, c * 512:(c + 1) * 512],
                start=(c == 0), stop=(c == 15)),
                ([("sAC", m_sq)] + P.guard(2)) if c == 0 else [])
        m_stats = P.mark("tensor", "sPE")
        P.op("vector", lambda v: v.tensor_scalar_mul(r_m, ps[1][0:1, :], 1.0 / H),
             [("sPE", m_stats)])
        P.op("vector", lambda v: v.tensor_scalar_mul(r_ss, ps[2][0:1, :], 1.0 / H))
        m_statrd = P.mark("vector", "sDV")
        P.set_guard(1, [("sDV", m_statrd)])
        P.set_guard(2, [("sDV", m_statrd)])
        P.op("vector", lambda v: v.scalar_tensor_tensor(r_t, r_m, 1.0, r_m, ALU.mult, ALU.mult))
        m_h0 = P.mark("vector", "sDV")
        P.op("vector", lambda v: v.tensor_tensor(r_ss, r_ss, r_t, ALU.subtract),
            [("sDV", m_h0)])
        m_var = P.mark("vector", "sDV")
        P.op("scalar", lambda s: s.activation(r_ss, r_ss, AF.Sqrt, bias=eps_col[0:1, :]),
             [("sDV", m_var)])
        m_sd = P.mark("scalar", "sAC")
        P.op("vector", lambda v: v.reciprocal(r_ss, r_ss), [("sAC", m_sd)])
        m_h1 = P.mark("vector", "sDV")
        P.op("vector", lambda v: v.tensor_copy(out=rstd_bf, in_=r_ss), [("sDV", m_h1)])
        P.op("vector", lambda v: v.scalar_tensor_tensor(r_m, r_m, -1.0, r_ss, ALU.mult, ALU.mult))
        m_h2 = P.mark("vector", "sDV")
        P.op("vector", lambda v: v.tensor_copy(out=mrs_bf, in_=r_m), [("sDV", m_h2)])
        m_rows = P.mark("vector", "sDV")
        P.op("tensor", lambda e: e.matmul(ps[3], oner_bf, rstd_bf, start=True, stop=True),
             [("sDV", m_rows)] + P.guard(3))
        m_bc = P.mark("tensor", "sPE")
        dst = xbf
        dst3 = dst.rearrange("p (c n) -> p c n", c=16)
        P.op("vector", lambda v, dst3=dst3: v.tensor_tensor(
            dst3, xstage.rearrange("p (c n) -> p c n", c=16),
            ps[3][:, None, :].broadcast_to([128, 16, 512]), ALU.mult),
            [("sPE", m_bc)])
        m_xnr = P.mark("vector", "sDV")
        P.set_guard(3, [("sDV", m_xnr)])
        for ot in range(4):
            bank = 4 + (ot % 2)
            for c in range(16):
                P.op("tensor", lambda e, ot=ot, c=c, dst=dst, bank=bank: e.matmul(
                    ps[bank], wkv_bf[:, c * KV + ot * 128: c * KV + ot * 128 + 128],
                    dst[:, c * 512:(c + 1) * 512], start=(c == 0), stop=False),
                    ([("sDV", m_xnr)] + P.guard(bank)) if c == 0 else [])
            P.op("tensor", lambda e, ot=ot, bank=bank: e.matmul(
                ps[bank], colkv_bf[0:1, ot * 128:(ot + 1) * 128], mrs_bf,
                start=False, stop=True))
            m_kv = P.mark("tensor", "sPE")
            P.op("scalar", lambda s, ot=ot, tt=tt, bank=bank: s.activation(
                kv_c[:, ot * S + tt * 512: ot * S + (tt + 1) * 512], ps[bank],
                AF.Identity, bias=bkv_sb[:, ot:ot + 1]), [("sPE", m_kv)])
            m_ev = P.mark("scalar", "sAC")
            P.set_guard(bank, [("sAC", m_ev)])
        if own:
            for ot in range(8):
                bank = 6 + (ot % 2)
                for c in range(16):
                    P.op("tensor", lambda e, ot=ot, c=c, dst=dst, bank=bank: e.matmul(
                        ps[bank], wqd_bf[:, c * QC + ot * 128: c * QC + ot * 128 + 128],
                        dst[:, c * 512:(c + 1) * 512], start=(c == 0), stop=False),
                        P.guard(bank) if c == 0 else [])
                P.op("tensor", lambda e, ot=ot, bank=bank: e.matmul(
                    ps[bank], colqd_bf[0:1, ot * 128:(ot + 1) * 128], mrs_bf,
                    start=False, stop=True))
                m_qc = P.mark("tensor", "sPE")
                P.op("scalar", lambda s, ot=ot, tt=tt, bank=bank: s.activation(
                    q_c[:, ot * TOWN + tt * 512: ot * TOWN + (tt + 1) * 512], ps[bank],
                    AF.Identity, bias=bqd_sb[:, ot:ot + 1]), [("sPE", m_qc)])
                m_ev = P.mark("scalar", "sAC")
                P.set_guard(bank, [("sAC", m_ev)])

    if _DEBUG:
        P.op("gpsimd", lambda e: e.dma_start(out=dbg_kvc, in_=kv_c),
             [("sAC", P.now("sAC"))])
        P.mark("gpsimd", "sST", 16)
        P.op("gpsimd", lambda e: e.dma_start(out=dbg_qc, in_=q_c))
        P.mark("gpsimd", "sST", 16)

    nc.sbuf_base = base_persist
    ctx_pair = sb("ctx_pair", [128, 2 * TOWN], BF16)
    mkv_bf = sb("mkv_bf", [128, 4], BF16)
    mkv_f = sb("mkv_f", [128, 4], F32)
    wstp = sb("wstp", [128, 4 * 256], F32)
    wku_bf = sb("wku_bf", [128, 4 * 256], BF16)
    wvu_bf = sb("wvu_bf", [128, 4 * 256], BF16)
    wkr_bf = sb("wkr_bf", [128, 4 * 128], BF16)
    wqu_bf = sb("wqu_bf", [128, 8 * 256], BF16)
    wqr_bf = sb("wqr_bf", [128, 8 * 128], BF16)
    kc0 = sb("kc0", [128, S], BF16)
    kc1 = sb("kc1", [128, S], BF16)
    krf32 = sb("krf32", [128, S], F32)
    krot = sb("krot", [128, S], BF16)
    vpair = sb("vpair", [128, 32 * 256], BF16)
    qcs = [sb(f"qcs{i}", [128, TOWN], BF16) for i in range(2)]
    qrf32 = sb("qrf32", [128, TOWN], F32)
    qrot = sb("qrot", [128, TOWN], BF16)
    pT = [sb(f"pT{i}", [128, 512], BF16) for i in range(2)]
    linv_sb = sb("linv_sb", [128, 512], F32)
    biasK = sb("biasK", [128, 4], F32)
    qwst = sb("qwst", [128, 2048], F32)
    qaz = sb("qaz", [128, 2048], F32)
    qb = [sb(f"qb{i}", [128, 2048], BF16) for i in range(3)]
    qsm = [sb(f"qsm{i}", [128, 256], F32) for i in range(4)]
    qsmb = sb("qsmb", [128, 256], BF16)

    P.op("vector", lambda v: v.tensor_reduce(
        mkv_f, kv_c.rearrange("p (c n) -> p c n", c=4), AX.X, ALU.add),
        [("sAC", P.now("sAC"))])
    m_red = P.mark("vector", "sDV")
    P.op("vector", lambda v: v.tensor_scalar(mkv_bf, mkv_f, 1.0 / S, None, ALU.mult),
         [("sDV", m_red)])
    m_mkv = P.mark("vector", "sDV")

    qchunks = []
    for seg_ in range(16):
        for wsrc_, wqd_ in ((wg, wgq_d), (wu, wuq_d)):
            for sub_ in range(4):
                r0 = seg_ * 512 + sub_ * 128
                qchunks.append((wsrc_[r0:r0 + 128, :], wqd_[r0:r0 + 128, :]))
    for ch_ in range(16):
        for sub_ in range(4):
            qchunks.append((wd[ch_ * 128:(ch_ + 1) * 128, sub_ * 2048:(sub_ + 1) * 2048],
                            wdq_d[ch_ * 128:(ch_ + 1) * 128, sub_ * 2048:(sub_ + 1) * 2048]))
    pump_store_mark = [0] * len(qchunks)
    pstate = {"step": 0, "m_qwst_free": [], "m_sc": 0, "m_scr": 0,
              "m_b2st": 0, "first": True}

    def pump_step():
        st = pstate["step"]
        g, ph = st // 5, st % 5
        if g * 4 >= len(qchunks):
            return False
        if ph == 0:
            m_half = [pstate["m_qwst_free"], pstate["m_qwst_free"]]
            m_ln = getattr(P, "scale_ln_mark", 0)
            for hf in range(8):
                ci = g * 4 + hf // 2
                hsl = slice((hf % 2) * 1024, (hf % 2) * 1024 + 1024)
                src = qchunks[ci][0]
                w_ld = list(m_half[hf % 2])
                if pstate["first"]:
                    w_ld += [("sDV", P.now("sDV")), ("sAC", P.now("sAC")),
                             ("sPE", P.now("sPE")), ("sST", P.now("sST"))]
                    pstate["first"] = False
                P.op("sync", lambda e, src=src, hsl=hsl: e.dma_start(
                    out=qwst[:, hsl], in_=src[:, hsl]), w_ld)
                m_l = P.mark("sync", "sLD", 16)
                P.op("vector", lambda v, hf=hf, hsl=hsl: v.tensor_reduce(
                    qsm[0][:, hf * 32:(hf + 1) * 32],
                    qwst[:, hsl].rearrange("p (b c) -> p b c", c=32),
                    AX.X, ALU.max, apply_absolute_value=True),
                    [("sLD", m_l), ("sAC", m_ln)])
                m_r = P.mark("vector", "sDV")
                m_half[hf % 2] = [("sDV", m_r)]
            pstate["m_qwst_free"] = [("sDV", P.now("sDV"))]
            pstate["m_sc"] = emit_scale_batch(
                P, [], qsm[0][:, 0:256], qsm[1][:, 0:256], qsm[2][:, 0:256],
                qsm[3][:, 0:256], qsmb[:, 0:256], half=True)
        else:
            sub = ph - 1
            ci = g * 4 + sub
            src, dst = qchunks[ci]
            P.op("sync", lambda e, src=src: e.dma_start(out=qwst, in_=src),
                 list(pstate["m_qwst_free"]))
            m_l = P.mark("sync", "sLD", 16)
            m_z, m_as, m_fin = emit_quant_w(
                P, qwst[:], qb[1][:],
                qsm[2][:, sub * 64:(sub + 1) * 64],
                qsmb[:, sub * 64:(sub + 1) * 64],
                qaz, qb[0], qb[1], qb[2], pstate["m_sc"],
                [("sLD", m_l)], m_scr_act=pstate["m_scr"],
                wt_b2=[("sST", pstate["m_b2st"])], dve_sign=True)
            pstate["m_scr"] = m_as
            pstate["m_qwst_free"] = [("sDV", m_z)]
            P.op("gpsimd", lambda e, dst=dst: e.dma_start(out=dst, in_=qb[1]),
                 [("sDV", m_fin)])
            m_st = P.mark("gpsimd", "sST", 16)
            pstate["m_b2st"] = m_st
            pump_store_mark[ci] = m_st
        pstate["step"] = st + 1
        return True

    kcs = (kc0, kc1)
    for pr in range(NH // 2):
        h0 = 2 * pr
        loads = (
            (wkuT[:, h0 * DH:(h0 + 2) * DH], wku_bf, 4, 256),
            (wvuT[:, h0 * DH:(h0 + 2) * DH], wvu_bf, 4, 256),
            (wkrT[:, h0 * DR:(h0 + 2) * DR], wkr_bf, 4, 128),
            (wquT[:, h0 * DH:(h0 + 2) * DH], wqu_bf, 8, 256),
            (wqrT[:, h0 * DR:(h0 + 2) * DR], wqr_bf, 8, 128),
        )
        for src, dstb, nch, w in loads:
            per = max(1, 1024 // w)
            for r0 in range(0, nch, per):
                n2 = min(per, nch - r0)
                P.op("sync", lambda e, src=src, r0=r0, n2=n2, w=w: e.dma_start(
                    out=wstp[:, 0:n2 * w].rearrange("p (c n) -> p c n", c=n2),
                    in_=src[r0 * 128:(r0 + n2) * 128, :].rearrange(
                        "(c p) n -> p c n", p=128)),
                    [("sDV", P.now("sDV")), ("sPE", P.now("sPE")),
                     ("sST", P.now("sST"))])
                m_l = P.mark("sync", "sLD", 16)
                P.op("vector", lambda v, dstb=dstb, r0=r0, n2=n2, w=w: v.tensor_copy(
                    out=dstb[:, r0 * w:(r0 + n2) * w], in_=wstp[:, 0:n2 * w]),
                    [("sLD", m_l)])
                P.mark("vector", "sDV")
        m_wld = P.now("sDV")

        for hh in range(2):
            for c in range(4):
                P.op("tensor", lambda e, hh=hh, c=c: e.matmul(
                    ps[0][:, hh:hh + 1],
                    wku_bf[:, c * 256 + hh * 128: c * 256 + hh * 128 + 128],
                    mkv_bf[:, c:c + 1], start=(c == 0), stop=(c == 3)),
                    ([("sDV", m_wld), ("sDV", m_mkv)] + P.guard(0)) if (hh == 0 and c == 0) else [])
        for c in range(4):
            P.op("tensor", lambda e, c=c: e.matmul(
                ps[0][:, 2:3], wkr_bf[:, c * 128:(c + 1) * 128],
                mkv_bf[:, c:c + 1], start=(c == 0), stop=(c == 3)))
        m_kbar = P.mark("tensor", "sPE")
        P.op("vector", lambda v: v.tensor_scalar_mul(biasK[:, 0:3], ps[0][:, 0:3], -1.0),
             [("sPE", m_kbar), ("sAC", P.now("sAC"))])
        m_biasK = P.mark("vector", "sDV")
        P.set_guard(0, [("sDV", m_biasK)])

        for hh in range(2):
            for k8 in range(8):
                bank = 1 + (k8 % 2)
                for c in range(4):
                    P.op("tensor", lambda e, hh=hh, k8=k8, c=c, bank=bank: e.matmul(
                        ps[bank],
                        wku_bf[:, c * 256 + hh * 128: c * 256 + hh * 128 + 128],
                        kv_c[:, c * S + k8 * 512: c * S + (k8 + 1) * 512],
                        start=(c == 0), stop=(c == 3)),
                        ([("sDV", m_biasK)] + P.guard(bank)) if c == 0 else [])
                m_k = P.mark("tensor", "sPE")
                P.op("scalar", lambda s, hh=hh, k8=k8, bank=bank: s.activation(
                    kcs[hh][:, k8 * 512:(k8 + 1) * 512], ps[bank], AF.Identity,
                    bias=biasK[:, hh:hh + 1]), [("sPE", m_k)])
                m_e = P.mark("scalar", "sAC")
                P.set_guard(bank, [("sAC", m_e)])
        for k8 in range(8):
            bank = 1 + (k8 % 2)
            for c in range(4):
                P.op("tensor", lambda e, k8=k8, c=c, bank=bank: e.matmul(
                    ps[bank], wkr_bf[:, c * 128:(c + 1) * 128],
                    kv_c[:, c * S + k8 * 512: c * S + (k8 + 1) * 512],
                    start=(c == 0), stop=(c == 3)), P.guard(bank) if c == 0 else [])
            m_k = P.mark("tensor", "sPE")
            P.op("scalar", lambda s, k8=k8, bank=bank: s.activation(
                krf32[:, k8 * 512:(k8 + 1) * 512], ps[bank], AF.Identity,
                bias=biasK[:, 2:3]), [("sPE", m_k)])
            m_e = P.mark("scalar", "sAC")
            P.set_guard(bank, [("sAC", m_e)])
        m_krf = P.now("sAC")
        for hh in range(2):
            for k8 in range(8):
                bank = 3 + (k8 % 2)
                P.op("tensor", lambda e, hh=hh, k8=k8, bank=bank, pr=pr: e.matmul(
                    ps[bank][0:DR, :],
                    rmat_sb[hh * DR:(hh + 1) * DR, pr * DR:(pr + 1) * DR],
                    krf32[hh * DR:(hh + 1) * DR, k8 * 512:(k8 + 1) * 512],
                    start=True, stop=True),
                    ([("sAC", m_krf)] + P.guard(bank)) if k8 in (0, 1) else P.guard(bank))
                m_r = P.mark("tensor", "sPE")
                P.op("scalar", lambda s, hh=hh, k8=k8, bank=bank: s.activation(
                    krot[hh * DR:(hh + 1) * DR, k8 * 512:(k8 + 1) * 512],
                    ps[bank][0:DR, :], AF.Copy), [("sPE", m_r)])
                m_e = P.mark("scalar", "sAC")
                P.set_guard(bank, [("sAC", m_e)])
        for kt in range(32):
            bank = 5 + (kt % 2)
            for c in range(4):
                P.op("tensor", lambda e, kt=kt, c=c, bank=bank: e.matmul(
                    ps[bank][:, 0:256], kv_c[:, c * S + kt * 128: c * S + kt * 128 + 128],
                    wvu_bf[:, c * 256:(c + 1) * 256], start=(c == 0), stop=False),
                    P.guard(bank) if c == 0 else [])
            P.op("tensor", lambda e, kt=kt, bank=bank, h0=h0: e.matmul(
                ps[bank][:, 0:256], oner_bf, bvu_bf[0:1, h0 * DH: h0 * DH + 256],
                start=False, stop=True))
            m_v = P.mark("tensor", "sPE")
            P.op("scalar", lambda s, kt=kt, bank=bank: s.activation(
                vpair[:, kt * 256:(kt + 1) * 256], ps[bank][:, 0:256], AF.Copy),
                [("sPE", m_v)])
            m_e = P.mark("scalar", "sAC")
            P.set_guard(bank, [("sAC", m_e)])
        for hh in range(2):
            for qh in range(2):
                bank = 1 + (qh % 2)
                for c in range(8):
                    P.op("tensor", lambda e, hh=hh, qh=qh, c=c, bank=bank: e.matmul(
                        ps[bank],
                        wqu_bf[:, c * 256 + hh * 128: c * 256 + hh * 128 + 128],
                        q_c[:, c * TOWN + qh * 512: c * TOWN + (qh + 1) * 512],
                        start=(c == 0), stop=(c == 7)), P.guard(bank) if c == 0 else [])
                m_q = P.mark("tensor", "sPE")
                P.op("scalar", lambda s, hh=hh, qh=qh, bank=bank, h0=h0: s.activation(
                    qcs[hh][:, qh * 512:(qh + 1) * 512], ps[bank], AF.Identity,
                    bias=bqu_sb[:, h0 + hh: h0 + hh + 1]), [("sPE", m_q)])
                m_e = P.mark("scalar", "sAC")
                P.set_guard(bank, [("sAC", m_e)])
        for qh in range(2):
            bank = 1 + (qh % 2)
            for c in range(8):
                P.op("tensor", lambda e, qh=qh, c=c, bank=bank: e.matmul(
                    ps[bank], wqr_bf[:, c * 128:(c + 1) * 128],
                    q_c[:, c * TOWN + qh * 512: c * TOWN + (qh + 1) * 512],
                    start=(c == 0), stop=(c == 7)), P.guard(bank) if c == 0 else [])
            m_q = P.mark("tensor", "sPE")
            P.op("scalar", lambda s, qh=qh, bank=bank, pr=pr: s.activation(
                qrf32[:, qh * 512:(qh + 1) * 512], ps[bank], AF.Identity,
                bias=bqr_sb[:, pr:pr + 1]), [("sPE", m_q)])
            m_e = P.mark("scalar", "sAC")
            P.set_guard(bank, [("sAC", m_e)])
        m_qrf = P.now("sAC")
        for hh in range(2):
            for qh in range(2):
                bank = 3 + (qh % 2)
                P.op("tensor", lambda e, hh=hh, qh=qh, bank=bank, pr=pr: e.matmul(
                    ps[bank][0:DR, :],
                    rmat_sb[hh * DR:(hh + 1) * DR, pr * DR:(pr + 1) * DR],
                    qrf32[hh * DR:(hh + 1) * DR, qh * 512:(qh + 1) * 512],
                    start=True, stop=True), [("sAC", m_qrf)] + P.guard(bank))
                m_r = P.mark("tensor", "sPE")
                P.op("scalar", lambda s, hh=hh, qh=qh, bank=bank: s.activation(
                    qrot[hh * DR:(hh + 1) * DR, qh * 512:(qh + 1) * 512],
                    ps[bank][0:DR, :], AF.Copy), [("sPE", m_r)])
                m_e = P.mark("scalar", "sAC")
                P.set_guard(bank, [("sAC", m_e)])
        m_qrot = P.now("sAC")

        if _DEBUG and pr == 0:
            P.op("gpsimd", lambda e: e.dma_start(out=dbg_kc, in_=kc0),
                 [("sAC", m_qrot)])
            P.mark("gpsimd", "sST", 16)
            P.op("gpsimd", lambda e: e.dma_start(out=dbg_kr, in_=krot))
            P.mark("gpsimd", "sST", 16)
            P.op("gpsimd", lambda e: e.dma_start(out=dbg_v, in_=vpair))
            P.mark("gpsimd", "sST", 16)
            P.op("gpsimd", lambda e: e.dma_start(out=dbg_qcs, in_=qcs[0]))
            P.mark("gpsimd", "sST", 16)
            P.op("gpsimd", lambda e: e.dma_start(out=dbg_qrot, in_=qrot))
            P.mark("gpsimd", "sST", 16)
            P.op("gpsimd", lambda e: e.dma_start(out=dbg_bvu, in_=bvu_bf))
            P.mark("gpsimd", "sST", 16)
            P.op("gpsimd", lambda e: e.dma_start(out=dbg_oner, in_=oner_bf))
            P.mark("gpsimd", "sST", 16)
            P.op("gpsimd", lambda e: e.dma_start(out=dbg_wvu, in_=wvu_bf))
            P.mark("gpsimd", "sST", 16)
            P.op("gpsimd", lambda e: e.dma_start(out=dbg_wqu, in_=wqu_bf))
            m_dbg = P.mark("gpsimd", "sST", 16)

        for hh in range(2):
            for qh in range(2):
                for kt in range(32):
                    bank = 1 + (kt % 2)
                    P.op("tensor", lambda e, hh=hh, qh=qh, kt=kt, bank=bank: e.matmul(
                        ps[bank], kcs[hh][:, kt * 128:(kt + 1) * 128],
                        qcs[hh][:, qh * 512:(qh + 1) * 512], start=True, stop=False),
                        ([("sAC", m_qrot)] if (hh == 0 and qh == 0 and kt == 0) else []) + P.guard(bank))
                    P.op("tensor", lambda e, hh=hh, qh=qh, kt=kt, bank=bank: e.matmul(
                        ps[bank], krot[hh * DR:(hh + 1) * DR, kt * 128:(kt + 1) * 128],
                        qrot[hh * DR:(hh + 1) * DR, qh * 512:(qh + 1) * 512],
                        start=False, stop=True))
                    m_s = P.mark("tensor", "sPE")
                    P.op("scalar", lambda s, kt=kt, bank=bank: s.activation(
                        pT[kt % 2], ps[bank], AF.Exp, scale=ALPHA),
                        [("sPE", m_s)] + P.guard(bank + 10))
                    m_p = P.mark("scalar", "sAC")
                    P.set_guard(bank, [("sAC", m_p)])
                    P.op("tensor", lambda e, hh=hh, kt=kt: e.matmul(
                        ps[5], vpair[:, kt * 256 + hh * 128: kt * 256 + hh * 128 + 128],
                        pT[kt % 2], start=(kt == 0), stop=(kt == 31)),
                        [("sAC", m_p)] + (P.guard(5) if kt == 0 else []))
                    P.op("tensor", lambda e, kt=kt: e.matmul(
                        ps[6][0:1, :], onec_bf, pT[kt % 2],
                        start=(kt == 0), stop=(kt == 31)),
                        P.guard(6) if kt == 0 else [])
                    m_ctxmm = P.mark("tensor", "sPE")
                    P.set_guard(bank + 10, [("sPE", m_ctxmm)])
                    if kt % 4 == 3:
                        pump_step()
                m_l = P.now("sPE")
                P.op("vector", lambda v: v.reciprocal(srow[0:1, 1536:2048], ps[6][0:1, :]),
                     [("sPE", m_l)])
                m_h3 = P.mark("vector", "sDV")
                P.op("vector", lambda v: v.tensor_copy(out=rowbf[0:1, 0:512],
                                                       in_=srow[0:1, 1536:2048]),
                     [("sDV", m_h3)])
                m_linv = P.mark("vector", "sDV")
                P.set_guard(6, [("sDV", m_linv)])
                P.op("tensor", lambda e: e.matmul(ps[7], oner_bf, rowbf[0:1, 0:512],
                                                  start=True, stop=True),
                     [("sDV", m_linv)] + P.guard(7))
                m_bc = P.mark("tensor", "sPE")
                P.op("scalar", lambda s: s.activation(linv_sb, ps[7], AF.Copy),
                     [("sPE", m_bc)])
                m_lsb = P.mark("scalar", "sAC")
                P.set_guard(7, [("sAC", m_lsb)])
                P.op("vector", lambda v, hh=hh, qh=qh: v.tensor_tensor(
                    ctx_pair[:, hh * TOWN + qh * 512: hh * TOWN + (qh + 1) * 512],
                    ps[5], linv_sb, ALU.mult),
                    [("sAC", m_lsb)] + P.guard(22))
                m_cv = P.mark("vector", "sDV")
                P.set_guard(5, [("sDV", m_cv)])
                P.set_guard(21, [("sDV", m_cv)])
        P.op("gpsimd", lambda e, h0=h0: e.dma_start(
            out=ctx_d[:, h0 * TOWN:(h0 + 2) * TOWN], in_=ctx_pair),
            [("sDV", m_cv)])
        m_ctx_st = P.mark("gpsimd", "sST", 16)
        P.set_guard(22, [("sST", m_ctx_st)])

    while pump_step():
        pass

    m_attn_done = P.now("sDV")
    nc.sbuf_base = base_persist
    ctx_full = sb("ctx_full", [128, NH * TOWN], BF16)
    wo_st = sb("wo_st", [128, 16 * 128], F32)
    wo_bf = sb("wo_bf", [128, 16 * 128], BF16)
    ev_o = sb("ev_o", [128, 512], F32)
    xr_o = sb("xr_o", [128, 1024], F32)
    P.op("sync", lambda e: e.dma_start(out=ctx_full, in_=ctx_d),
         [("sDV", m_attn_done), ("sAC", P.now("sAC")), ("sPE", P.now("sPE")),
          ("sST", P.now("sST"))])
    m_ctxf = P.mark("sync", "sLD", 16)
    for odt in range(16):
        P.op("sync", lambda e, odt=odt: e.dma_start(
            out=wo_st.rearrange("p (c n) -> p c n", c=16),
            in_=woutT[:, odt * 128:(odt + 1) * 128].rearrange("(c p) n -> p c n", p=128)),
            [("sDV", P.now("sDV")), ("sPE", P.now("sPE"))])
        m_l = P.mark("sync", "sLD", 16)
        P.op("vector", lambda v: v.tensor_copy(out=wo_bf, in_=wo_st), [("sLD", m_l)])
        m_w = P.mark("vector", "sDV")
        for qh in range(2):
            bank = 1 + qh
            for c in range(16):
                P.op("tensor", lambda e, odt=odt, qh=qh, c=c, bank=bank: e.matmul(
                    ps[bank], wo_bf[:, c * 128:(c + 1) * 128],
                    ctx_full[:, c * TOWN + qh * 512: c * TOWN + (qh + 1) * 512],
                    start=(c == 0), stop=(c == 15)),
                    ([("sDV", m_w), ("sLD", m_ctxf)] + P.guard(bank)) if c == 0 else [])
            m_o = P.mark("tensor", "sPE")
            P.op("scalar", lambda s, odt=odt, bank=bank: s.activation(
                ev_o, ps[bank], AF.Identity, bias=bout_sb[:, odt:odt + 1]),
                [("sPE", m_o)] + P.guard(20) + P.guard(21))
            m_oe = P.mark("scalar", "sAC")
            P.set_guard(bank, [("sAC", m_oe)])
            P.op("sync", lambda e, odt=odt, qh=qh: e.dma_start(
                out=xr_o[:, 0:512],
                in_=x_fm[odt * 128:(odt + 1) * 128, qh * 512:(qh + 1) * 512]),
                [("sDV", P.now("sDV"))])
            m_xr = P.mark("sync", "sLD", 16)
            P.op("vector", lambda v: v.tensor_tensor(
                xr_o[:, 512:1024], ev_o, xr_o[:, 0:512], ALU.add),
                [("sAC", m_oe), ("sLD", m_xr), ("sST", P.now("sST"))])
            m_acc = P.mark("vector", "sDV")
            P.set_guard(20, [("sDV", m_acc)])
            P.op("gpsimd", lambda e, odt=odt, qh=qh: e.dma_start(
                out=acc_d[odt * 128:(odt + 1) * 128, qh * 512:(qh + 1) * 512],
                in_=xr_o[:, 512:1024]), [("sDV", m_acc)])
            P.mark("gpsimd", "sST", 16)
    m_acc_all = P.now("sST")

    nc.sbuf_base = base_consts
    xq_fm = sb("xq_fm", [128, 16 * TOWN], BF16)
    scr = [sb(f"scr{i}", [128, 2048], F32) for i in range(2)]
    scrb = [sb(f"scrb{i}", [128, 2048], BF16) for i in range(4)]
    smq = [sb(f"smq{i}", [128, 256], F32) for i in range(4)]
    smb = sb("smb", [128, 256], BF16)
    cols = sb("cols", [128, 4], F32)
    base_fbranch = nc.sbuf_base
    xt = sb("xt", [128, H], F32)
    sqf = sb("sqf", [128, H], F32)
    xqt = sb("xqt", [128, H], BF16)
    nc.sbuf_base = base_fbranch
    wgqT = sb("wgqT", [128, 16 * 512], BF16)
    wuqT = sb("wuqT", [128, 16 * 512], BF16)
    sg_f = sb("sg_f", [128, 512], F32)
    _ism_pack = sb("_ism_pack", [128, 32], F32)
    it_f = [sb(f"it_f{i}", [128, 512], F32) for i in range(2)]
    itq = [sb(f"itq{i}", [128, 512], BF16) for i in range(2)]

    m_scr_act = 0
    for tt in range(8):
        P.op("sync", lambda e, tt=tt: e.dma_start(
            out=xt, in_=x_tm[tt * 128:(tt + 1) * 128, :]),
            [("sDV", P.now("sDV")), ("sAC", P.now("sAC")), ("sST", P.now("sST"))])
        m_x = P.mark("sync", "sLD", 16)
        P.op("vector", lambda v: v.tensor_reduce(cols[:, 0:1], xt, AX.X, ALU.add),
             [("sLD", m_x)])
        m_red = P.mark("vector", "sDV")
        P.op("vector", lambda v: v.tensor_scalar_mul(cols[:, 0:1], cols[:, 0:1], 1.0 / H),
             [("sDV", m_red)])
        m_m = P.mark("vector", "sDV")
        P.op("scalar", lambda s: s.activation(sqf, xt, AF.Square), [("sLD", m_x)])
        m_sq = P.mark("scalar", "sAC")
        P.op("vector", lambda v: v.tensor_reduce(cols[:, 1:2], sqf, AX.X, ALU.add),
             [("sAC", m_sq)])
        m_red2 = P.mark("vector", "sDV")
        P.op("vector", lambda v: v.tensor_scalar_mul(cols[:, 1:2], cols[:, 1:2], 1.0 / H),
             [("sDV", m_red2)])
        m_g1 = P.mark("vector", "sDV")
        P.op("vector", lambda v: v.scalar_tensor_tensor(
            cols[:, 2:3], cols[:, 0:1], 1.0, cols[:, 0:1], ALU.mult, ALU.mult),
            [("sDV", m_g1)])
        m_g2 = P.mark("vector", "sDV")
        P.op("vector", lambda v: v.tensor_tensor(cols[:, 1:2], cols[:, 1:2],
                                                 cols[:, 2:3], ALU.subtract),
            [("sDV", m_g2)])
        m_var = P.mark("vector", "sDV")
        P.op("scalar", lambda s: s.activation(cols[:, 1:2], cols[:, 1:2], AF.Sqrt,
                                              bias=eps_col), [("sDV", m_var)])
        m_sd = P.mark("scalar", "sAC")
        P.op("vector", lambda v: v.reciprocal(cols[:, 1:2], cols[:, 1:2]),
             [("sAC", m_sd)])
        m_g3 = P.mark("vector", "sDV")
        P.op("vector", lambda v: v.scalar_tensor_tensor(
            cols[:, 3:4], cols[:, 0:1], -1.0, cols[:, 1:2], ALU.mult, ALU.mult),
            [("sDV", m_g3)])
        m_xn0 = P.mark("vector", "sDV")
        P.op("scalar", lambda s: s.activation(sqf, xt, AF.Identity,
                                              scale=cols[:, 1:2], bias=cols[:, 3:4]),
             [("sDV", m_xn0)])
        m_xn = P.mark("scalar", "sAC")
        sqf3 = sqf.rearrange("p (b c) -> p b c", c=32)
        m_sc = emit_scale_batch(
            P, [(sqf3, slice(0, 64), [("sAC", m_xn)])],
            smq[0][:, 0:64], smq[1][:, 0:64], smq[2][:, 0:64], smq[3][:, 0:64],
            smb[:, 0:64], half=False)
        m_as_q, m_q = emit_quant_act_multi(P, [dict(
            src=sqf, outq=xqt, rs=smq[2][:, 0:64], scb=smb[:, 0:64],
            zf=scr[0], azf=scr[1], g1=scrb[0], g2=scrb[1], g3=scrb[2],
            g4=scrb[3], wt=[("sAC", m_xn)],
            wt_out=[("sST", P.now("sST"))])], m_sc, m_scr_act)
        m_scr_act = m_as_q
        P.op("gpsimd", lambda e, tt=tt: e.dma_start(
            out=xq_d[tt * 128:(tt + 1) * 128, :], in_=xqt), [("sDV", m_q)])
        P.mark("gpsimd", "sST", 16)
    m_xq_st = P.now("sST")
    P.op("sync", lambda e: e.dma_start(
        out=xq_fm.rearrange("p (c n) -> p c n", c=16),
        in_=xq_d.rearrange("n (c p) -> n c p", p=128), transpose=True),
        [("sST", m_xq_st)])
    m_xqfm = P.mark("sync", "sLD", 16)

    m_itq_store = [0, 0]

    for seg in range(16):
        for mat, (wqd_t, wT_dst) in enumerate(((wgq_d, wgqT), (wuq_d, wuqT))):
            ci0 = (seg * 2 + mat) * 4
            m_stq = max(pump_store_mark[ci0:ci0 + 4])
            P.op("sync", lambda e, wqd_t=wqd_t, wT_dst=wT_dst, seg=seg: e.dma_start(
                out=wT_dst.rearrange("p (c n) -> p c n", c=16),
                in_=wqd_t[seg * 512:(seg + 1) * 512, :].rearrange(
                    "n (c p) -> n c p", p=128),
                transpose=True), [("sST", m_stq), ("sPE", P.now("sPE"))])
            P.mark("sync", "sLD", 16)
        m_wT = P.now("sLD")
        for tp in range(4):
            pair = []
            for half in (0, 1):
                tt = tp * 2 + half
                bg = 1 + 2 * half
                bu = 2 + 2 * half
                for c in range(16):
                    P.op("tensor", lambda e, tt=tt, c=c, bg=bg: e.matmul(
                        ps[bg], xq_fm[:, c * TOWN + tt * 128: c * TOWN + tt * 128 + 128],
                        wgqT[:, c * 512:(c + 1) * 512], start=(c == 0), stop=(c == 15)),
                        ([("sLD", m_wT), ("sLD", m_xqfm)] + P.guard(bg)) if c == 0 else [])
                for c in range(16):
                    P.op("tensor", lambda e, tt=tt, c=c, bu=bu: e.matmul(
                        ps[bu], xq_fm[:, c * TOWN + tt * 128: c * TOWN + tt * 128 + 128],
                        wuqT[:, c * 512:(c + 1) * 512], start=(c == 0), stop=(c == 15)),
                        P.guard(bu) if c == 0 else [])
                m_gu = P.mark("tensor", "sPE")
                P.op("scalar", lambda s, bg=bg: s.activation(sg_f, ps[bg], AF.Silu),
                     [("sPE", m_gu)] + P.guard(30))
                m_sg = P.mark("scalar", "sAC")
                P.set_guard(bg, [("sAC", m_sg)])
                P.op("vector", lambda v, bu=bu, half=half: v.tensor_tensor(
                    it_f[half], sg_f, ps[bu], ALU.mult),
                    [("sAC", m_sg)] + P.guard(31 + half))
                m_it = P.mark("vector", "sDV")
                P.set_guard(bu, [("sDV", m_it)])
                P.set_guard(30, [("sDV", m_it)])
                pair.append(m_it)
            jobs2 = []
            for half in (0, 1):
                i3 = it_f[half][:].rearrange("p (b c) -> p b c", c=32)
                jobs2.append((i3, slice(half * 16, (half + 1) * 16),
                              [("sDV", pair[half])]))
            m_sc2 = emit_scale_batch(P, jobs2, smq[0][:, 0:32], smq[1][:, 0:32],
                                     smq[2][:, 0:32], smq[3][:, 0:32],
                                     smb[:, 0:32], half=False)
            cas = []
            for half in (0, 1):
                sl = slice(half * 512, (half + 1) * 512)
                cas.append(dict(
                    src=it_f[half][:], outq=itq[half][:],
                    rs=smq[2][:, half * 16:(half + 1) * 16],
                    scb=smb[:, half * 16:(half + 1) * 16],
                    zf=scr[0][:, sl], azf=scr[1][:, sl],
                    g1=scrb[0][:, sl], g2=scrb[1][:, sl],
                    g3=scrb[2][:, sl], g4=scrb[3][:, sl],
                    wt=[("sDV", pair[half])],
                    wt_out=[("sST", m_itq_store[half])]))
            m_as2, m_fin2 = emit_quant_act_multi(P, cas, m_sc2, m_scr_act)
            m_scr_act = m_as2
            P.set_guard(31, [("sDV", m_fin2)])
            P.set_guard(32, [("sDV", m_fin2)])
            for half in (0, 1):
                tt = tp * 2 + half
                P.op("gpsimd", lambda e, seg=seg, tt=tt, half=half: e.dma_start(
                    out=itq_d[tt * 128:(tt + 1) * 128, seg * 512:(seg + 1) * 512],
                    in_=itq[half]), [("sDV", m_fin2)])
                m_itq_store[half] = P.mark("gpsimd", "sST", 16)

    m_wdq = max(pump_store_mark[128:192])

    nc.sbuf_base = base_consts
    itq_fm = sb("itq_fm", [128, 64 * TOWN], BF16)
    wdT = [sb(f"wdT{i}", [128, 512], BF16) for i in range(2)]
    ev_f = sb("ev_f", [128, 512], F32)
    xr_f = sb("xr_f", [128, 512], F32)
    P.op("sync", lambda e: e.dma_start(
        out=itq_fm.rearrange("p (c n) -> p c n", c=64),
        in_=itq_d.rearrange("n (c p) -> n c p", p=128), transpose=True),
        [("sST", P.now("sST"))])
    m_itfm = P.mark("sync", "sLD", 16)
    for odg in range(4):
        for oc in range(64):
            bufi = oc % 2
            P.op("sync", lambda e, odg=odg, oc=oc, bufi=bufi: e.dma_start(
                out=wdT[bufi],
                in_=wdq_d[odg * 512:(odg + 1) * 512, oc * 128:(oc + 1) * 128],
                transpose=True),
                [("sST", m_wdq)] + P.guard(40 + bufi))
            m_w = P.mark("sync", "sLD", 16)
            for odt4 in range(4):
                for th in range(2):
                    bank = odt4 * 2 + th
                    P.op("tensor", lambda e, oc=oc, odt4=odt4, th=th, bufi=bufi, bank=bank: e.matmul(
                        ps[bank], wdT[bufi][:, odt4 * 128:(odt4 + 1) * 128],
                        itq_fm[:, oc * TOWN + th * 512: oc * TOWN + (th + 1) * 512],
                        start=(oc == 0), stop=(oc == 63)),
                        ([("sLD", m_w), ("sLD", m_itfm)] + (P.guard(bank) if oc == 0 else []))
                        if (odt4 == 0 and th == 0) else
                        (P.guard(bank) if oc == 0 else []))
            m_mm = P.mark("tensor", "sPE")
            P.set_guard(40 + bufi, [("sPE", m_mm)])
        m_dn = P.now("sPE")
        for odt4 in range(4):
            for th in range(2):
                bank = odt4 * 2 + th
                od = odg * 4 + odt4
                P.op("scalar", lambda s, bank=bank: s.activation(ev_f, ps[bank], AF.Copy),
                     [("sPE", m_dn)] + P.guard(50))
                m_ev = P.mark("scalar", "sAC")
                P.set_guard(bank, [("sAC", m_ev)])
                P.op("sync", lambda e, od=od, th=th: e.dma_start(
                    out=xr_f, in_=acc_d[od * 128:(od + 1) * 128, th * 512:(th + 1) * 512]),
                    [("sST", m_acc_all)] + P.guard(51))
                m_xr = P.mark("sync", "sLD", 16)
                P.op("vector", lambda v: v.tensor_tensor(ev_f, ev_f, xr_f, ALU.add),
                     [("sAC", m_ev), ("sLD", m_xr)])
                m_y = P.mark("vector", "sDV")
                P.set_guard(50, [("sDV", m_y)])
                P.set_guard(51, [("sDV", m_y)])
                P.op("gpsimd", lambda e, od=od, th=th: e.dma_start(
                    out=y_out[od * 128:(od + 1) * 128, th * 512:(th + 1) * 512],
                    in_=ev_f), [("sDV", m_y)])
                m_yst = P.mark("gpsimd", "sST", 16)
                P.set_guard(50, [("sDV", m_y), ("sST", m_yst)])
    m_final = P.now("sST")
    P.op("gpsimd", lambda e: e.engine_nop(), [("sST", m_final)])

    import contextlib
    with contextlib.ExitStack() as es:
        sems = {n: es.enter_context(nc.semaphore(n))
                for n in ("sLD", "sST", "sPE", "sDV", "sAC")}
        block = es.enter_context(nc.Block())

        def replay(eng_obj, stream):
            for fn, waits, inc_sem, inc_n in stream:
                for s, v in waits:
                    eng_obj.wait_ge(sems[s], v)
                ins = fn(eng_obj)
                if inc_sem is not None:
                    ins.then_inc(sems[inc_sem], inc_n)

        @block.sync
        def _(e):
            replay(e, P.streams["sync"])

        @block.vector
        def _(e):
            replay(e, P.streams["vector"])

        @block.scalar
        def _(e):
            replay(e, P.streams["scalar"])

        @block.tensor
        def _(e):
            replay(e, P.streams["tensor"])

        @block.gpsimd
        def _(e):
            replay(e, P.streams["gpsimd"])

    return nc


_NC_CACHE = None


def _get_nc():
    global _NC_CACHE
    if _NC_CACHE is None:
        _NC_CACHE = build_nc()
    return _NC_CACHE


def _rope_mats():
    inv = 10000.0 ** (-(np.arange(0, DR, 2, dtype=np.float64)) / DR)
    rm = np.zeros((128, (NH // 2) * DR), dtype=np.float64)
    for h in range(NH):
        pr, hh = h // 2, h % 2
        ang = h * inv
        c, s = np.cos(ang), np.sin(ang)
        for i in range(DR // 2):
            rm[hh * DR + 2 * i, pr * DR + 2 * i] = c[i]
            rm[hh * DR + 2 * i + 1, pr * DR + 2 * i] = -s[i]
            rm[hh * DR + 2 * i, pr * DR + 2 * i + 1] = s[i]
            rm[hh * DR + 2 * i + 1, pr * DR + 2 * i + 1] = c[i]
    return rm.astype(np.float32)


def kernel(L0_attn_x, L0_attn_kv_down_w, L0_attn_kv_down_b, L0_attn_key_up_w,
           L0_attn_key_up_b, L0_attn_value_up_w, L0_attn_value_up_b,
           L0_attn_key_rope_w, L0_attn_key_rope_b, L0_attn_query_down_w,
           L0_attn_query_down_b, L0_attn_query_up_w, L0_attn_query_up_b,
           L0_attn_query_rope_w, L0_attn_query_rope_b, L0_attn_out_w,
           L0_attn_out_b, L0_ffn_w_gate, L0_ffn_w_up, L0_ffn_w_down):
    f32 = np.float32
    x = np.asarray(L0_attn_x, f32)

    def T(a):
        return np.ascontiguousarray(np.asarray(a, f32).T)

    wkvT = T(L0_attn_kv_down_w)
    wkuT = T(L0_attn_key_up_w)
    wvuT = T(L0_attn_value_up_w)
    wkrT = T(L0_attn_key_rope_w)
    wqdT = T(L0_attn_query_down_w)
    wquT = T(L0_attn_query_up_w)
    wqrT = T(L0_attn_query_rope_w)
    woutT = T(L0_attn_out_w)

    def colpack(b, n):
        return np.ascontiguousarray(np.asarray(b, f32).reshape(n, 128).T)

    b_kv = colpack(L0_attn_kv_down_b, KV // 128)
    b_qd = colpack(L0_attn_query_down_b, QC // 128)
    b_qu = colpack(L0_attn_query_up_b, NH * DH // 128)
    b_out = colpack(L0_attn_out_b, H // 128)
    bqr = np.asarray(L0_attn_query_rope_b, f32).reshape(NH, DR)
    b_qr = np.zeros((128, NH // 2), f32)
    for pr in range(NH // 2):
        b_qr[0:DR, pr] = bqr[2 * pr]
        b_qr[DR:128, pr] = bqr[2 * pr + 1]
    b_vu_row = np.asarray(L0_attn_value_up_b, f32).reshape(1, NH * DH)
    rmat = _rope_mats()
    ones_col = np.ones((128, 1), f32)
    wg = np.ascontiguousarray(np.asarray(L0_ffn_w_gate, f32))
    wu = np.ascontiguousarray(np.asarray(L0_ffn_w_up, f32))
    wd = np.ascontiguousarray(np.asarray(L0_ffn_w_down, f32))

    shared = dict(wkvT=wkvT, wkuT=wkuT, wvuT=wvuT, wkrT=wkrT, wqdT=wqdT,
                  wquT=wquT, wqrT=wqrT, woutT=woutT, b_kv=b_kv, b_qd=b_qd,
                  b_qu=b_qu, b_qr=b_qr, b_vu_row=b_vu_row, b_out=b_out,
                  rmat=rmat, ones_col=ones_col,
                  ones_row=np.ones((1, 128), f32), wg=wg, wu=wu, wd=wd)

    in_maps = []
    for cid in range(NCORES):
        b = cid // 4
        toff = (cid % 4) * TOWN
        xb = x[b]
        x_roll = np.roll(xb, -toff, axis=0)
        m = dict(shared)
        m["x_fm"] = np.ascontiguousarray(x_roll.T)
        m["x_tm"] = np.ascontiguousarray(x_roll[:TOWN])
        in_maps.append(m)

    nc = _get_nc()
    res = run_bass_kernel_spmd(nc, in_maps, core_ids=list(range(NCORES)))

    out = np.empty((B, S, H), dtype=f32)
    for cid in range(NCORES):
        b = cid // 4
        toff = (cid % 4) * TOWN
        y = np.asarray(res.results[cid]["y_out"], f32)
        out[b, toff:toff + TOWN, :] = y.T
    return out
